# version 01
# baseline (speedup 1.0000x reference)
import numpy as np

import concourse.bass as bass
import concourse.mybir as mybir
from concourse.bass_utils import run_bass_kernel_spmd

F32 = mybir.dt.float32
BF16 = mybir.dt.bfloat16
AF = mybir.ActivationFunctionType
ALU = mybir.AluOpType
AX = mybir.AxisListType

D = 2048
DC = 16
S = 1024
SK = 2048
FF = 8192
FC = 64
B = 4
NCORES = 8

ENGS = ("sync", "tensor", "vector", "scalar", "gpsimd")
PROG_SEM = {"sync": "SY", "tensor": "T", "vector": "V", "scalar": "SC", "gpsimd": "G"}
NDMASUB = 4
DMA_SEM = {q: [f"{q[:2].upper()}{i}" for i in range(NDMASUB)]
           for q in ("sync", "scalar", "gpsimd")}


class Prog:
    def __init__(self, nc):
        self.nc = nc
        self.q = {e: [] for e in ENGS}
        self.sem_handles = {}
        self.cnt = {}
        self._sem_ctxs = []
        self.dma_ctr = {q: 0 for q in DMA_SEM}
        self._pend = {e: None for e in ENGS}
        for s in list(PROG_SEM.values()) + ["CC"]:
            self.new_sem(s)
        for q, subs in DMA_SEM.items():
            for s in subs:
                self.new_sem(s)

    def new_sem(self, name):
        ctx = self.nc.semaphore(name)
        handle = ctx.__enter__()
        self._sem_ctxs.append(ctx)
        self.sem_handles[name] = handle
        self.cnt[name] = 0

    def emit(self, eng, fn):
        self._flush(eng)
        self.q[eng].append(fn)

    def pend(self, eng, fn):
        self._flush(eng)
        self._pend[eng] = fn

    def close(self, eng):
        fn = self._pend[eng]
        assert fn is not None, f"close({eng}) with no pending instruction"
        self._pend[eng] = None
        self.cnt[PROG_SEM[eng]] += 1
        tgt = self.cnt[PROG_SEM[eng]]
        h = self.sem_handles[PROG_SEM[eng]]
        self.q[eng].append(lambda E, f=fn, h=h: f(E).then_inc(h, 1))
        return tgt

    def _flush(self, eng):
        fn = self._pend.get(eng)
        if fn is not None:
            self._pend[eng] = None
            self.q[eng].append(fn)

    def inc(self, eng, sem, fn, n=1):
        self._flush(eng)
        self.cnt[sem] += n
        tgt = self.cnt[sem]
        h = self.sem_handles[sem]
        self.q[eng].append(lambda E, f=fn, h=h, n=n: f(E).then_inc(h, n))
        return tgt

    def step(self, eng, fn):
        return self.inc(eng, PROG_SEM[eng], fn)

    def dma(self, eng, fn):
        sub = DMA_SEM[eng][self.dma_ctr[eng] % NDMASUB]
        self.dma_ctr[eng] += 1
        if self.cnt[sub] > 0:
            self.wait(eng, sub, self.cnt[sub])
        v = self.inc(eng, sub, fn, 16)
        return (sub, v)

    def wait_tok(self, eng, tok):
        if tok:
            self.wait(eng, tok[0], tok[1])

    def wait_dma_all(self, eng, queue):
        for sub in DMA_SEM[queue]:
            self.wait(eng, sub, self.cnt[sub])

    def wait(self, eng, sem, val=None):
        v = self.cnt[sem] if val is None else val
        assert v <= self.cnt[sem], f"wait target {v} > emitted {self.cnt[sem]} on {sem}"
        if v <= 0:
            return
        self._flush(eng)
        h = self.sem_handles[sem]
        self.q[eng].append(lambda E, h=h, v=v: E.wait_ge(h, v))

    def mark(self, sem):
        return self.cnt[sem]

    def barrier(self, engs=ENGS):
        marks = {}
        for eng in engs:
            self.wait(eng, PROG_SEM[eng], self.cnt[PROG_SEM[eng]])
            marks[eng] = self.inc(eng, PROG_SEM[eng], lambda E: E.nop())
        for eng in engs:
            for oeng in engs:
                if oeng != eng:
                    self.wait(eng, PROG_SEM[oeng], marks[oeng])
            for q in DMA_SEM:
                self.wait_dma_all(eng, q)


class Arena:

    def __init__(self, tensor, total):
        self.t = tensor
        self.total = total
        self.off = 0

    def alloc(self, parts, n):
        assert self.off + n <= self.total, f"arena overflow: {self.off}+{n} > {self.total}"
        ap = self.t[0:parts, self.off:self.off + n]
        self.off += n
        return ap

    def alloc3(self, parts, a, b_):
        ap = self.alloc(parts, a * b_)
        return ap.rearrange("p (a b) -> p a b", a=a)

    def reset(self):
        self.off = 0


class Ring:

    def __init__(self, p):
        self.p = p
        self.ctr = 0
        self.guard = [0, 0]

    def next(self):
        par = self.ctr % 2
        self.ctr += 1
        if self.guard[par]:
            self.p.wait("gpsimd", "T", self.guard[par])
        return par

    def consumed(self, par, mark):
        self.guard[par] = mark


def build(tap=None):
    nc = bass.Bass("TRN2", target_bir_lowering=False)

    def din(name, shape, dtype=F32):
        return nc.dram_tensor(name, list(shape), dtype, kind="ExternalInput").ap()

    x_in = din("x", [S, D])
    Wq = din("L0_attn_Wq", [D, D])
    Wk = din("L0_attn_Wk", [D, D])
    Wv = din("L0_attn_Wv", [D, D])
    Wo = din("L0_attn_Wo", [D, D])
    bo_in = din("L0_attn_bo", [D])
    up_w = din("L0_ffn_up_weight", [FF, D])
    dn_w = din("L0_ffn_down_weight", [D, FF])
    up_s = din("L0_ffn_up_scale", [FF])
    dn_s = din("L0_ffn_down_scale", [D])
    out_s = din("L0_ffn_output_scale", [1])
    basis_in, phase_in, amp_in = {}, {}, {}
    for nm in ("q", "k", "v", "o"):
        basis_in[nm] = din(f"L1_attn_basis_{nm}", [64, D])
        phase_in[nm] = din(f"L1_attn_phase_{nm}", [D, 64])
        amp_in[nm] = din(f"L1_attn_amp_{nm}", [D, 64])
    Wr = din("L1_ffn_Wr", [D, 8])
    br_in = din("L1_ffn_br", [8])
    W1 = din("L1_ffn_W1", [D, FF])
    b1_in = din("L1_ffn_b1", [FF])
    W2 = din("L1_ffn_W2", [FF, D])
    b2_in = din("L1_ffn_b2", [D])
    A1 = din("L1_ffn_A1", [8, 16, D])
    B1 = din("L1_ffn_B1", [8, FF, 16])
    A2 = din("L1_ffn_A2", [8, 16, FF])
    B2 = din("L1_ffn_B2", [8, D, 16])
    ident_in = din("ident128", [128, 128])
    pmask_in = din("pmask", [128, SK], BF16)
    selbc_in = din("selbc", [32, 2048], BF16)

    y_out = nc.dram_tensor("y", [S, D], F32, kind="ExternalOutput").ap()
    dbg_out = nc.dram_tensor("dbg", [128, 4096], F32, kind="ExternalOutput").ap()

    cc_kv_in = nc.dram_tensor("cc_kv_in", [128, 2064], F32).ap()
    cc_kv_out = nc.dram_tensor("cc_kv_out", [128, 2064], F32).ap()
    cc_res_in = nc.dram_tensor("cc_res_in", [128, SK], BF16).ap()
    cc_res_out = nc.dram_tensor("cc_res_out", [128, SK], BF16).ap()
    upTc = nc.dram_tensor("upTc", [FC, DC, 128, 128], BF16).ap()
    dnTc = nc.dram_tensor("dnTc", [DC, FC, 128, 128], BF16).ap()
    w1c = nc.dram_tensor("w1c", [FC, DC, 128, 128], BF16).ap()
    w2c = nc.dram_tensor("w2c", [DC, FC, 128, 128], BF16).ap()
    b1tc = nc.dram_tensor("b1tc", [8, 16, 8192], BF16).ap()
    a2tc = nc.dram_tensor("a2tc", [FC, 128, 128], BF16).ap()
    b2tc = nc.dram_tensor("b2tc", [DC, 8, 16, 128], BF16).ap()
    crows_d = nc.dram_tensor("crows_d", [9, S], F32).ap()

    x_res = nc.alloc_sbuf_tensor("x_res", [128, DC, S], F32)
    xln = nc.alloc_sbuf_tensor("xln", [128, DC, S], BF16)
    bufA = nc.alloc_sbuf_tensor("bufA", [128, 16384], BF16)
    bufB = nc.alloc_sbuf_tensor("bufB", [128, 16384], BF16)
    wring = nc.alloc_sbuf_tensor("wring", [128, 32, 128], BF16)
    ident = nc.alloc_sbuf_tensor("ident", [128, 128], F32)
    ident_bf = nc.alloc_sbuf_tensor("ident_bf", [128, 128], BF16)
    ones_col = nc.alloc_sbuf_tensor("ones_col", [128, 1], F32)
    ones_colb = nc.alloc_sbuf_tensor("ones_colb", [128, 1], BF16)
    ones_row = nc.alloc_sbuf_tensor("ones_row", [1, 128], F32)
    pio2 = nc.alloc_sbuf_tensor("pio2", [128, 1], F32)
    ones_rowb = nc.alloc_sbuf_tensor("ones_rowb", [1, 512], BF16)
    arena_bf_t = nc.alloc_sbuf_tensor("arena_bf", [128, 13056], BF16)
    arena_f32_t = nc.alloc_sbuf_tensor("arena_f32", [128, 2624], F32)
    sbuf_left = nc.sbuf_bytes_remaining

    ABF = Arena(arena_bf_t, 13056)
    AFA = Arena(arena_f32_t, 2624)

    PSB = nc.alloc_psum_tensor("PSB", [128, 2048], F32)
    PS0 = nc.alloc_psum_tensor("PS0", [128, 512], F32)
    PS1 = nc.alloc_psum_tensor("PS1", [128, 512], F32)
    PS2 = nc.alloc_psum_tensor("PS2", [128, 512], F32)
    PT = nc.alloc_psum_tensor("PT", [128, 1024], BF16)

    p = Prog(nc)
    meta = dict(sbuf_left=sbuf_left)

    def lp(fn):
        def wrapped(E, fn=fn):
            with nc.allow_low_precision(reason="bf16 operand for subsequent matmul"):
                return fn(E)
        return wrapped

    def load_colvec(vec_dram, nch, dst, tmp):
        p.wait("sync", "T", p.cnt["T"])
        ld = p.dma("sync", lambda E: E.dma_start(
            out=tmp[0:nch, 0:128], in_=vec_dram.rearrange("(a b) -> a b", b=128)))
        p.wait_tok("tensor", ld)
        p.wait("tensor", "V", p.cnt["V"])
        tm = p.step("tensor", lambda E: E.transpose(PS2[:, 0:nch], tmp[0:nch, 0:128],
                                                    ident[0:nch, 0:nch]))
        p.wait("vector", "T", tm)
        return p.step("vector", lambda E: E.tensor_copy(dst, PS2[:, 0:nch]))

    def phase0():
        AFA.reset()
        stg = AFA.alloc(128, 1024)
        stg2 = AFA.alloc(128, 1024)
        for t_ in (ones_col, ones_colb, ones_row, ones_rowb):
            p.emit("vector", lambda E, t_=t_: E.memset(t_[:, :], 1.0))
        p.emit("vector", lambda E: E.memset(pio2[:, :], float(np.pi / 2)))
        p.step("vector", lambda E: E.memset(stg2, 0.0))
        p.dma("sync", lambda E: E.dma_start(out=ident[:, :], in_=ident_in[:, :]))
        p.dma("gpsimd", lambda E: E.dma_start(out=ident_bf[:, :], in_=ident_in[:, :]))
        p.wait("sync", "V")
        for c0 in range(0, 2064, 1024):
            w = min(1024, 2064 - c0)
            p.dma("sync", lambda E, c0=c0, w=w:
                  E.dma_start(out=cc_kv_in[:, c0:c0 + w], in_=stg2[:, 0:w]))
        p.wait_dma_all("sync", "sync")
        guards = {0: 0, 1: 0}
        sguard = {0: 0, 1: 0}
        for tt in range(8):
            for dh in range(2):
                par = (tt * 2 + dh) % 2
                st = stg if par == 0 else stg2
                if sguard[par]:
                    p.wait("sync", "T", sguard[par])
                ld = p.dma("sync", lambda E, tt=tt, dh=dh, st=st:
                           E.dma_start(out=st,
                                       in_=x_in[tt * 128:(tt + 1) * 128,
                                                dh * 1024:(dh + 1) * 1024]))
                p.wait_tok("tensor", ld)
                for j in range(8):
                    dc = dh * 8 + j
                    pst = [PS0, PS1][j % 2]
                    if guards[j % 2]:
                        p.wait("tensor", "V", guards[j % 2])
                    td = p.step("tensor", lambda E, pst=pst, st=st, j=j:
                                E.transpose(pst[:, 0:128], st[:, j * 128:(j + 1) * 128],
                                            ident[:, :]))
                    p.wait("vector", "T", td)
                    ev = p.step("vector", lambda E, pst=pst, dc=dc, tt=tt:
                                E.tensor_copy(x_res[:, dc, tt * 128:(tt + 1) * 128],
                                              pst[:, 0:128]))
                    guards[j % 2] = ev
                sguard[par] = p.mark("T")

    def layernorm():
        p.barrier()
        AFA.reset()
        strow = AFA.alloc3(1, 2, 1024)
        stg = AFA.alloc(128, 512)
        prev_mm = 0
        for dc in range(DC):
            for th in range(2):
                sl = slice(th * 512, (th + 1) * 512)
                if prev_mm:
                    p.wait("scalar", "T", prev_mm)
                sq = p.step("scalar", lambda E, dc=dc, sl=sl:
                            E.activation(stg, x_res[:, dc, :][:, sl], AF.Square))
                p.emit("tensor", lambda E, dc=dc, sl=sl:
                       E.matmul(PSB[0:1, 0:1024][:, sl], ones_col[:, :],
                                x_res[:, dc, :][:, sl], start=(dc == 0), stop=(dc == DC - 1)))
                p.wait("tensor", "SC", sq)
                prev_mm = p.step("tensor", lambda E, dc=dc, th=th:
                                 E.matmul(PSB[32:33, 1024 + th * 512:1536 + th * 512],
                                          ones_col[:, :], stg,
                                          start=(dc == 0), stop=(dc == DC - 1)))
        p.wait("vector", "T")
        p.step("vector", lambda E: E.tensor_scalar_mul(strow[:, 0, :], PSB[0:1, 0:1024], 1.0 / D))
        p.wait("vector", "V")
        p.step("vector", lambda E: E.tensor_mul(strow[:, 1, :], strow[:, 0, :], strow[:, 0, :]))
        p.wait("vector", "V")
        p.step("vector", lambda E:
               E.scalar_tensor_tensor(strow[:, 1, :], PSB[32:33, 1024:2048], 1.0 / D,
                                      strow[:, 1, :], ALU.mult, ALU.subtract))
        p.wait("vector", "V")
        vs = p.step("vector", lambda E: E.tensor_scalar_add(strow[:, 1, :], strow[:, 1, :], 1e-5))
        p.wait("scalar", "V", vs)
        sq = p.step("scalar", lambda E: E.activation(strow[:, 1, :], strow[:, 1, :], AF.Sqrt))
        p.wait("vector", "SC", sq)
        vb = p.step("vector", lambda E: E.reciprocal(strow[:, 1, :], strow[:, 1, :]))
        p.wait("tensor", "V", vb)
        for th in range(2):
            sl = slice(th * 512, (th + 1) * 512)
            rps = PS0 if th == 0 else PS1
            p.emit("tensor", lambda E, rps=rps, sl=sl:
                   E.matmul(rps[:, :], ones_row[0:1, :], strow[:, 1, :][:, sl],
                            start=True, stop=True))
            p.step("tensor", lambda E, th=th, sl=sl:
                   E.matmul(PSB[:, th * 512:(th + 1) * 512], ones_row[0:1, :],
                            strow[:, 0, :][:, sl], start=True, stop=True))
        p.wait("vector", "T")
        for dc in range(DC):
            for th in range(2):
                sl = slice(th * 512, (th + 1) * 512)
                mps = PSB[:, th * 512:(th + 1) * 512]
                fn = lambda E, dc=dc, sl=sl, mps=mps: \
                    E.tensor_sub(xln[:, dc, :][:, sl], x_res[:, dc, :][:, sl], mps)
                if dc == DC - 1 and th == 1:
                    p.step("vector", fn)
                else:
                    p.emit("vector", fn)
        p.wait("vector", "V")
        for dc in range(DC):
            for th in range(2):
                sl = slice(th * 512, (th + 1) * 512)
                rps = PS0 if th == 0 else PS1
                p.pend("vector", lambda E, dc=dc, sl=sl, rps=rps:
                       E.tensor_mul(xln[:, dc, :][:, sl], xln[:, dc, :][:, sl], rps[:, :]))
        p.close("vector")
        p.barrier()
        AFA.reset()

    def load_col_chunks(ring, Wap, ic):
        par = ring.next()
        mk = p.dma("gpsimd", lambda E, Wap=Wap, ic=ic, par=par:
                   E.dma_start(out=wring[:, par * 16:(par + 1) * 16, :],
                               in_=Wap[:, ic * 128:(ic + 1) * 128]
                               .rearrange("(a p) c -> p a c", p=128)))
        return par, mk

    def l0_attn():
        layernorm()
        ABF.reset()
        AFA.reset()
        kvp = AFA.alloc(64, 2048)
        kssb = AFA.alloc(128, 16)
        kvbf = ABF.alloc(128, 2048)
        ksD = ABF.alloc3(128, 16, 32)
        rnbf = ABF.alloc(32, 1024)
        selbc = ABF.alloc3(32, 32, 64)
        bocol = AFA.alloc(128, 16)
        kring = ABF.alloc3(128, 2, 1024)
        vring = ABF.alloc3(128, 2, 1024)
        ktm = ABF.alloc(128, 512)
        vtm = ABF.alloc(128, 512)
        rnb = ABF.alloc(64, 512)
        p.dma("sync", lambda E: E.dma_start(
            out=selbc.rearrange("p a b -> p (a b)"), in_=selbc_in[:, :]))
        tmpcv = AFA.alloc(64, 128)
        load_colvec(bo_in, 16, bocol, tmpcv)
        p.step("vector", lambda E: E.memset(ksD.rearrange("p a b -> p (a b)"), 0.0))
        ring = Ring(p)

        ks_first = {0: True, 1: True}
        n_ks = 0
        ev_guard = {"k": {}, "v": {}}
        kv_evac_g = 0
        tp_guard = [0]
        slotg = {0: 0, 1: 0}
        for ic in range(16):
            for which, Wap, dst, relu in (("k", Wk, kring, True), ("v", Wv, vring, False)):
                par, ld = load_col_chunks(ring, Wap, ic)
                psl = PSB[:, 0:1024] if which == "k" else PSB[:, 1024:2048]
                p.wait_tok("tensor", ld)
                if ic >= 1:
                    eng, mk = ev_guard[which][ic - 1]
                    p.wait("tensor", eng, mk)
                for dc in range(DC):
                    for th in range(2):
                        p.pend("tensor", lambda E, psl=psl, dc=dc, th=th, par=par:
                               E.matmul(psl[:, th * 512:(th + 1) * 512],
                                        wring[:, par * 16 + dc, :],
                                        xln[:, dc, th * 512:(th + 1) * 512],
                                        start=(dc == 0), stop=(dc == DC - 1)))
                mm = p.close("tensor")
                ring.consumed(par, mm)
                slot = ic % 2
                if relu:
                    p.wait("scalar", "T", mm)
                    p.wait("scalar", "T", slotg[slot])
                    ev = p.step("scalar", lambda E, psl=psl, dst=dst, slot=slot:
                                E.activation(dst[:, slot, :], psl[:, :], AF.Relu))
                    ev_guard[which][ic] = ("SC", ev)
                else:
                    p.wait("vector", "T", mm)
                    p.wait("vector", "T", slotg[slot])
                    ev = p.step("vector", lambda E, psl=psl, dst=dst, slot=slot:
                                E.tensor_copy(dst[:, slot, :], psl[:, :]))
                    ev_guard[which][ic] = ("V", ev)
            slot = ic % 2
            p.wait("tensor", "SC", ev_guard["k"][ic][1])
            p.wait("tensor", "V", ev_guard["v"][ic][1])
            for hp in range(2):
                h = 2 * ic + hp
                if tp_guard[0]:
                    p.wait("tensor", "V", tp_guard[0])
                idsl = ident_bf[64 * hp:64 * hp + 64, 64 * hp:64 * hp + 64]
                for tt in range(8):
                    p.pend("tensor", lambda E, hp=hp, tt=tt, slot=slot, idsl=idsl:
                           E.transpose(PT[:, tt * 64:(tt + 1) * 64],
                                       kring[64 * hp:64 * hp + 64, slot,
                                             tt * 128:(tt + 1) * 128],
                                       idsl))
                    p.pend("tensor", lambda E, hp=hp, tt=tt, slot=slot, idsl=idsl:
                           E.transpose(PT[:, 512 + tt * 64:512 + (tt + 1) * 64],
                                       vring[64 * hp:64 * hp + 64, slot,
                                             tt * 128:(tt + 1) * 128],
                                       idsl))
                td = p.close("tensor")
                p.wait("vector", "T", td)
                if kv_evac_g:
                    p.wait("vector", "T", kv_evac_g)
                evt = 0
                for half, dsttm in ((0, ktm), (1, vtm)):
                    evt = p.step("vector", lambda E, dsttm=dsttm, half=half:
                                 E.tensor_copy(dsttm, PT[:, half * 512:(half + 1) * 512]))
                tp_guard[0] = evt
                p.wait("tensor", "V", evt)
                for tt in range(8):
                    st = (tt == 0 and hp == 0)
                    sp = (tt == 7 and hp == 1)
                    p.pend("tensor", lambda E, tt=tt, hp=hp, st=st, sp=sp:
                           E.matmul(PS2[0:64, hp * 64:(hp + 1) * 64],
                                    ktm[:, tt * 64:(tt + 1) * 64],
                                    vtm[:, tt * 64:(tt + 1) * 64],
                                    start=st, stop=sp))
                for tt in range(8):
                    n_ks += 1
                    stk = ks_first[hp]
                    ks_first[hp] = False
                    spk = (ic == 15 and tt == 7)
                    ksdst = PS0[0:64, ic:ic + 1] if hp == 0 else PS1[64:128, ic:ic + 1]
                    p.pend("tensor", lambda E, tt=tt, ksdst=ksdst, st=stk, sp=spk:
                           E.matmul(ksdst,
                                    ktm[:, tt * 64:(tt + 1) * 64],
                                    ones_colb[0:128, :],
                                    start=st, stop=sp))
                kv_evac_g = p.close("tensor")
            slotg[slot] = kv_evac_g
            p.wait("vector", "T", kv_evac_g)
            p.step("vector", lambda E, ic=ic:
                   E.tensor_copy(kvp[:, ic * 128:(ic + 1) * 128], PS2[0:64, 0:128]))
        p.step("vector", lambda E: E.tensor_copy(kssb[0:64, :], PS0[0:64, 0:16]))
        evks = p.step("vector", lambda E: E.tensor_copy(kssb[64:128, :], PS1[64:128, 0:16]))
        p.wait("sync", "V", evks)
        p.dma("sync", lambda E: E.dma_start(out=cc_kv_in[0:64, 0:2048], in_=kvp))
        d2 = p.dma("sync", lambda E: E.dma_start(out=cc_kv_in[:, 2048:2064], in_=kssb))
        p.wait_dma_all("gpsimd", "sync")
        cc = p.inc("gpsimd", "CC", lambda E:
                   E.collective_compute("AllReduce", ALU.add,
                                        replica_groups=[[0, 1], [2, 3], [4, 5], [6, 7]],
                                        ins=[cc_kv_in[:, :]], outs=[cc_kv_out[:, :]]))
        qfm = bufA[:, :].rearrange("p (a b) -> p a b", a=16)
        qev = {}
        for ic in range(16):
            par, ld = load_col_chunks(ring, Wq, ic)
            psl = PSB[:, 0:1024] if ic % 2 == 0 else PSB[:, 1024:2048]
            p.wait_tok("tensor", ld)
            if ic >= 2:
                p.wait("tensor", "SC", qev[ic - 2])
            for dc in range(DC):
                for th in range(2):
                    p.pend("tensor", lambda E, psl=psl, dc=dc, th=th, par=par:
                           E.matmul(psl[:, th * 512:(th + 1) * 512],
                                    wring[:, par * 16 + dc, :],
                                    xln[:, dc, th * 512:(th + 1) * 512],
                                    start=(dc == 0), stop=(dc == DC - 1)))
            mq = p.close("tensor")
            ring.consumed(par, mq)
            p.wait("scalar", "T", mq)
            qev[ic] = p.step("scalar", lambda E, psl=psl, ic=ic:
                             E.activation(qfm[:, ic, :], psl[:, :], AF.Relu))
        p.wait("gpsimd", "CC", cc)
        dkv = p.dma("gpsimd", lambda E: E.dma_start(out=kvbf[0:64, :],
                                                    in_=cc_kv_out[0:64, 0:2048]))
        dkv2 = p.dma("gpsimd", lambda E: E.dma_start(out=kvbf[64:128, :],
                                                     in_=cc_kv_out[0:64, 0:2048]))
        p.wait("sync", "CC", cc)
        dks = p.dma("sync", lambda E: E.dma_start(out=kssb, in_=cc_kv_out[:, 2048:2064]))
        p.wait_tok("vector", dks)
        p.wait_tok("vector", dkv)
        p.wait_tok("vector", dkv2)
        for dc in range(DC):
            p.pend("vector", lambda E, dc=dc:
                   E.tensor_copy(ksD[0:64, dc, 2 * dc:2 * dc + 1], kssb[0:64, dc:dc + 1]))
            p.pend("vector", lambda E, dc=dc:
                   E.tensor_copy(ksD[64:128, dc, 2 * dc + 1:2 * dc + 2],
                                 kssb[64:128, dc:dc + 1]))
        evD = p.close("vector")
        p.wait("tensor", "V", evD)
        p.wait("tensor", "SC")
        for th in range(2):
            for dc in range(DC):
                p.pend("tensor", lambda E, th=th, dc=dc:
                       E.matmul(PSB[0:32, th * 512:(th + 1) * 512], ksD[:, dc, :],
                                qfm[:, dc, th * 512:(th + 1) * 512],
                                start=(dc == 0), stop=(dc == DC - 1)))
        nm = p.close("tensor")
        p.wait("vector", "T", nm)
        p.step("vector", lambda E: E.tensor_scalar_max(PSB[0:32, 0:1024],
                                                       PSB[0:32, 0:1024], 1e-6))
        p.wait("vector", "V")
        evn = p.step("vector", lp(lambda E: E.reciprocal(rnbf, PSB[0:32, 0:1024])))
        p.wait("tensor", "V", evn)
        ao = bufB[:, :].rearrange("p (a b) -> p a b", a=16)
        qkv_guard = {0: 0, 1: 0}
        rnb_guard = 0
        for h in range(32):
            hp = h % 2
            ic = h // 2
            ps = PS0 if h % 2 == 0 else PS1
            for th in range(2):
                if rnb_guard:
                    p.wait("tensor", "V", rnb_guard)
                mb = p.step("tensor", lambda E, h=h, th=th:
                            E.matmul(PS2[0:64, :], selbc[:, h, :],
                                     rnbf[:, th * 512:(th + 1) * 512],
                                     start=True, stop=True))
                p.wait("vector", "T", mb)
                ev = p.step("vector", lambda E, th=th:
                            E.tensor_copy(rnb, PS2[0:64, :]))
                p.wait("tensor", "V", ev)
                p.wait("vector", "V", ev)
                if qkv_guard[h % 2]:
                    p.wait("tensor", "V", qkv_guard[h % 2])
                mq2 = p.step("tensor", lambda E, h=h, hp=hp, ic=ic, th=th, ps=ps:
                             E.matmul(ps[0:64, :] if th == 0 else ps[64:128, :],
                                      kvbf[64 * hp:64 * hp + 64, h * 64:(h + 1) * 64],
                                      qfm[64 * hp:64 * hp + 64, ic,
                                          th * 512:(th + 1) * 512],
                                      start=True, stop=True))
                p.wait("vector", "T", mq2)
                src_ = ps[0:64, :] if th == 0 else ps[64:128, :]
                ev2 = p.step("vector", lambda E, src_=src_, hp=hp, ic=ic, th=th:
                             E.tensor_mul(ao[64 * hp:64 * hp + 64, ic,
                                             th * 512:(th + 1) * 512],
                                          src_, rnb))
                qkv_guard[h % 2] = ev2
                rnb_guard = ev2
        aog = p.mark("V")
        p.wait("tensor", "V", aog)
        og = {}
        for dc in range(DC):
            par, ldo = load_col_chunks(ring, Wo, dc)
            psl = PSB[:, 0:1024] if dc % 2 == 0 else PSB[:, 1024:2048]
            p.wait_tok("tensor", ldo)
            if dc >= 2:
                p.wait("tensor", "V", og[dc - 2])
            for ic in range(16):
                for th in range(2):
                    p.pend("tensor", lambda E, psl=psl, ic=ic, th=th, par=par:
                           E.matmul(psl[:, th * 512:(th + 1) * 512],
                                    wring[:, par * 16 + ic, :],
                                    ao[:, ic, th * 512:(th + 1) * 512],
                                    start=(ic == 0), stop=(ic == DC - 1)))
            mo = p.close("tensor")
            ring.consumed(par, mo)
            p.wait("vector", "T", mo)
            for th in range(2):
                og[dc] = p.step("vector", lambda E, psl=psl, dc=dc, th=th:
                                E.scalar_tensor_tensor(
                                    x_res[:, dc, th * 512:(th + 1) * 512],
                                    psl[:, th * 512:(th + 1) * 512],
                                    bocol[:, dc:dc + 1],
                                    x_res[:, dc, th * 512:(th + 1) * 512],
                                    ALU.add, ALU.add))
        p.barrier()
        ABF.reset()
        AFA.reset()

    def l0_ffn():
        layernorm()
        ABF.reset()
        AFA.reset()
        uscol = AFA.alloc(128, 64)
        dscol = AFA.alloc(128, 16)
        os_sb = AFA.alloc(1, 1)
        tmpcv = AFA.alloc(64, 128)
        load_colvec(up_s, 64, uscol, tmpcv)
        load_colvec(dn_s, 16, dscol, tmpcv)
        dos = p.dma("sync", lambda E: E.dma_start(out=os_sb, in_=out_s[0:1]))
        p.wait_tok("tensor", dos)
        p.wait("tensor", "V", p.cnt["V"])
        mos = p.step("tensor", lambda E:
                     E.matmul(PS2[:, 0:1], ones_row[0:1, :], os_sb,
                              start=True, stop=True))
        p.wait("vector", "T", mos)
        p.wait_dma_all("vector", "sync")
        p.step("vector", lambda E:
               E.tensor_scalar_mul(dscol, dscol, PS2[:, 0:1]))

        hA = bufA[:, :].rearrange("p (a b) -> p a b", b=512)
        hB = bufB[:, :].rearrange("p (a b) -> p a b", b=512)

        def h_slot(fc):
            return (hA if fc < 32 else hB)[:, fc % 32, :]

        natr = ABF.alloc3(128, 2, 2048)

        def prep_chunks(par, src_dram, n128, cache_out):
            nslot = natr[:, par, 0:n128 * 128]
            ldn = p.dma("gpsimd", lambda E, nslot=nslot, src_dram=src_dram:
                        E.dma_start(out=nslot, in_=src_dram))
            p.wait_tok("vector", ldn)
            p.step("vector", lambda E, nslot=nslot:
                   E.tensor_scalar(nslot, nslot, 0.0, 2.0, ALU.is_gt, ALU.mult))
            p.wait("vector", "V")
            sg = p.step("vector", lambda E, nslot=nslot:
                        E.tensor_scalar_add(nslot, nslot, -1.0))
            p.wait("tensor", "V", sg)
            for half in range((n128 + 7) // 8):
                jn = min(8, n128 - half * 8)
                p.wait("tensor", "V", ptg[0])
                for j in range(jn):
                    p.pend("tensor", lambda E, nslot=nslot, half=half, j=j:
                           E.transpose(PT[:, j * 128:(j + 1) * 128],
                                       nslot[:, (8 * half + j) * 128:(8 * half + j + 1) * 128],
                                       ident_bf[:, :]))
                tdu = p.close("tensor")
                p.wait("vector", "T", tdu)
                ptg[0] = p.step(
                    "vector", lambda E, par=par, half=half, jn=jn:
                    E.tensor_copy(wring[:, par * 16 + half * 8:par * 16 + half * 8 + jn, :]
                                  .rearrange("p a b -> p (a b)"), PT[:, 0:jn * 128]))
            wvm = p.mark("V")
            p.wait("sync", "V", wvm)
            cd = p.dma("sync", lambda E, cache_out=cache_out, par=par:
                       E.dma_start(out=cache_out,
                                   in_=wring[:, par * 16:par * 16 + 16, :]))
            return p.mark("T"), cd, wvm

        ring = Ring(p)
        ptg = [0]
        cache_rd = {}
        evh = {}
        dvg = 0
        for th in range(2):
            build = (th == 0)
            natg = [0, 0]
            for fc in range(FC):
                if build:
                    parn = fc % 2
                    if natg[parn]:
                        p.wait("gpsimd", "T", natg[parn])
                    par = ring.next()
                    if cache_rd.get(par):
                        p.wait_tok("gpsimd", cache_rd[par])
                    tm, cd, wvm = prep_chunks(
                        par,
                        up_w[fc * 128:(fc + 1) * 128, :],
                        16,
                        upTc[fc, :, :, :].transpose([1, 0, 2]))
                    natg[parn] = p.mark("T")
                    cache_rd[par] = cd
                    p.wait("tensor", "V", wvm)
                else:
                    par = ring.next()
                    if cache_rd.get(par):
                        p.wait_tok("gpsimd", cache_rd[par])
                        cache_rd[par] = 0
                    ld = p.dma("gpsimd", lambda E, fc=fc, par=par:
                               E.dma_start(out=wring[:, par * 16:(par + 1) * 16, :],
                                           in_=upTc[fc, :, :, :].transpose([1, 0, 2])))
                    p.wait_tok("tensor", ld)
                psl = [PS0, PS1][fc % 2]
                if fc >= 2:
                    p.wait("tensor", "SC", evh[fc - 2])
                for dc in range(DC):
                    p.pend("tensor", lambda E, psl=psl, dc=dc, th=th, par=par:
                           E.matmul(psl[:, :], wring[:, par * 16 + dc, :],
                                    xln[:, dc, th * 512:(th + 1) * 512],
                                    start=(dc == 0), stop=(dc == DC - 1)))
                mh = p.close("tensor")
                ring.consumed(par, mh)
                p.wait("scalar", "T", mh)
                evh[fc] = p.step("scalar", lambda E, psl=psl, fc=fc:
                                 E.activation(h_slot(fc), psl[:, :], AF.Relu,
                                              scale=uscol[:, fc:fc + 1]))
            p.wait("tensor", "SC")
            for dc in range(DC):
                for g in range(4):
                    if build:
                        parn = (dc * 4 + g) % 2
                        if natg[parn]:
                            p.wait("gpsimd", "T", natg[parn])
                        par = ring.next()
                        if cache_rd.get(par):
                            p.wait_tok("gpsimd", cache_rd[par])
                        tm, cd, wvm = prep_chunks(
                            par,
                            dn_w[dc * 128:(dc + 1) * 128, g * 2048:(g + 1) * 2048],
                            16,
                            dnTc[dc, g * 16:(g + 1) * 16, :, :].transpose([1, 0, 2]))
                        natg[parn] = p.mark("T")
                        cache_rd[par] = cd
                        p.wait("tensor", "V", wvm)
                    else:
                        par = ring.next()
                        if cache_rd.get(par):
                            p.wait_tok("gpsimd", cache_rd[par])
                            cache_rd[par] = 0
                        ld = p.dma("gpsimd", lambda E, dc=dc, g=g, par=par:
                                   E.dma_start(out=wring[:, par * 16:(par + 1) * 16, :],
                                               in_=dnTc[dc, g * 16:(g + 1) * 16, :, :]
                                               .transpose([1, 0, 2])))
                        p.wait_tok("tensor", ld)
                    if g == 0 and dvg:
                        p.wait("tensor", "V", dvg)
                    for j in range(16):
                        fc = g * 16 + j
                        p.pend("tensor", lambda E, fc=fc, j=j, par=par:
                               E.matmul(PS2[:, :], wring[:, par * 16 + j, :], h_slot(fc),
                                        start=(fc == 0), stop=(fc == FC - 1)))
                    mdg = p.close("tensor")
                    ring.consumed(par, mdg)
                md = p.mark("T")
                p.wait("vector", "T", md)
                dvg = p.step("vector", lambda E, dc=dc, th=th:
                             E.scalar_tensor_tensor(x_res[:, dc, th * 512:(th + 1) * 512],
                                                    PS2[:, :], dscol[:, dc:dc + 1],
                                                    x_res[:, dc, th * 512:(th + 1) * 512],
                                                    ALU.mult, ALU.add))
        p.barrier()
        ABF.reset()
        AFA.reset()

    def l1_attn():
        layernorm()
        ABF.reset()
        AFA.reset()
        res_kT = ABF.alloc(64, 2048)
        res_vT = ABF.alloc(64, 2048)
        res_q = ABF.alloc(64, 1024)
        basTo = ABF.alloc3(128, 16, 64)
        sums = AFA.alloc(128, 8)
        recip = AFA.alloc(128, 8)
        negmax = AFA.alloc(128, 1)
        tmpa = AFA.alloc(128, 512)
        tmpp = AFA.alloc(128, 512)
        mark0 = ABF.off
        wnat = ABF.alloc3(128, 8, 64)
        basn = ABF.alloc3(64, 16, 128)
        basTt = ABF.alloc3(128, 16, 64)
        pmh = ABF.alloc(128, 1024)
        exr = ABF.alloc(128, 2048)
        wT = bufB[0:64, 8192:16384].rearrange("p (j a c) -> p j a c", j=4, a=16)
        AT = bufB[:, 0:8192].rearrange("p (a c) -> p a c", a=16)
        qfm = bufA[:, :].rearrange("p (a b) -> p a b", a=16)
        ao = qfm
        for j, nm in enumerate("qkvo"):
            scl = (1.0 / float(np.sqrt(128.0))) if nm == "q" else 1.0
            for half in range(2):
                rows = slice(half * 1024, (half + 1) * 1024)
                p.wait("sync", "V", p.cnt["V"])
                lda = p.dma("sync", lambda E, nm=nm, rows=rows: E.dma_start(
                    out=tmpa.rearrange("p (a c) -> p a c", a=8),
                    in_=amp_in[nm][rows, :].rearrange("(a p) c -> p a c", p=128)))
                ldp = p.dma("sync", lambda E, nm=nm, rows=rows: E.dma_start(
                    out=tmpp.rearrange("p (a c) -> p a c", a=8),
                    in_=phase_in[nm][rows, :].rearrange("(a p) c -> p a c", p=128)))
                p.wait_tok("scalar", ldp)
                p.wait("scalar", "V", p.cnt["V"])
                sn = p.step("scalar", lambda E: E.activation(
                    tmpp, tmpp, AF.Sin, bias=pio2[:, :]))
                p.wait_tok("vector", lda)
                p.wait("scalar", "SC", sn)
                p.wait("vector", "SC", sn)
                p.wait("vector", "T", p.cnt["T"])
                wv = p.step("vector", lp(lambda E, scl=scl: E.scalar_tensor_tensor(
                    wnat.rearrange("p a c -> p (a c)"), tmpp, scl, tmpa,
                    ALU.mult, ALU.mult)))
                p.wait("tensor", "V", wv)
                for d8 in range(8):
                    p.pend("tensor", lambda E, d8=d8:
                           E.transpose(PT[0:64, d8 * 128:(d8 + 1) * 128],
                                       wnat[:, d8, :], ident_bf[:, :]))
                tw = p.close("tensor")
                p.wait("vector", "T", tw)
                p.step("vector", lambda E, j=j, half=half:
                       E.tensor_copy(wT[0:64, j, half * 8:(half + 1) * 8, :]
                                     .rearrange("p a c -> p (a c)"), PT[0:64, :]))
        kv_res_mark = {}
        for j, nm in enumerate("qkvo"):
            p.wait("gpsimd", "T", p.cnt["T"])
            ldb = p.dma("gpsimd", lambda E, nm=nm: E.dma_start(
                out=basn.rearrange("p a c -> p (a c)"), in_=basis_in[nm][:, :]))
            p.wait_tok("tensor", ldb)
            p.wait("tensor", "V", p.cnt["V"])
            for dc in range(DC):
                p.pend("tensor", lambda E, dc=dc:
                       E.transpose(PT[:, dc * 64:(dc + 1) * 64],
                                   basn[:, dc, :], ident_bf[0:64, 0:64]))
            tb = p.close("tensor")
            p.wait("vector", "T", tb)
            dstT = basTo if nm == "o" else basTt
            bv = p.step("vector", lambda E, dstT=dstT:
                        E.tensor_copy(dstT.rearrange("p a c -> p (a c)"), PT[:, :]))
            if nm == "o":
                continue
            p.wait("tensor", "V", bv)
            for th in range(2):
                psl = PSB[:, th * 512:(th + 1) * 512]
                for dc in range(DC):
                    if nm == "q":
                        out = psl[0:64, :]
                    elif nm == "k":
                        out = psl[0:64, :]
                    else:
                        out = psl[64:128, :]
                    p.pend("tensor", lambda E, out=out, dc=dc, th=th:
                           E.matmul(out, basTt[:, dc, :],
                                    xln[:, dc, th * 512:(th + 1) * 512],
                                    start=(dc == 0), stop=(dc == DC - 1)))
                kv_res_mark[(nm, th)] = p.close("tensor")
            if nm == "q":
                m = kv_res_mark[("q", 1)]
                p.wait("vector", "T", m)
                p.step("vector", lambda E:
                       E.tensor_copy(res_q[:, 0:512], PSB[0:64, 0:512]))
                qv = p.step("vector", lambda E:
                            E.tensor_copy(res_q[:, 512:1024], PSB[0:64, 512:1024]))
                p.wait("tensor", "V", qv)
        ldm0 = p.dma("sync", lambda E: E.dma_start(out=pmh, in_=pmask_in[:, 0:1024]))
        m = kv_res_mark[("v", 1)]
        p.wait("vector", "T", m)
        p.step("vector", lambda E: E.tensor_copy(exr[:, 0:1024], PSB[:, 0:1024]))
        cv = p.step("vector", lambda E: E.tensor_copy(exr[:, 1024:2048], PSB[:, 0:1024]))
        p.wait_tok("vector", ldm0)
        p.wait("vector", "V", cv)
        m0 = p.step("vector", lambda E: E.tensor_mul(exr[:, 0:1024], exr[:, 0:1024], pmh))
        p.wait("sync", "V", m0)
        ldm1 = p.dma("sync", lambda E: E.dma_start(out=pmh, in_=pmask_in[:, 1024:2048]))
        p.wait_tok("vector", ldm1)
        p.wait("vector", "V", p.cnt["V"])
        mv = p.step("vector", lambda E:
                    E.tensor_mul(exr[:, 1024:2048], exr[:, 1024:2048], pmh))
        p.wait("sync", "V", mv)
        dex = p.dma("sync", lambda E: E.dma_start(out=cc_res_in[:, :], in_=exr))
        p.wait_dma_all("gpsimd", "sync")
        ccr = p.inc("gpsimd", "CC", lambda E:
                    E.collective_compute("AllReduce", ALU.add,
                                         replica_groups=[[0, 1], [2, 3], [4, 5], [6, 7]],
                                         ins=[cc_res_in[:, :]], outs=[cc_res_out[:, :]]))
        qg = {0: 0, 1: 0}
        for ic in range(16):
            ps = PS0 if ic % 2 == 0 else PS1
            for th in range(2):
                if qg[ic % 2]:
                    p.wait("tensor", "V", qg[ic % 2])
                mqe = p.step("tensor", lambda E, ic=ic, th=th, ps=ps:
                             E.matmul(ps[:, :],
                                      wT[0:64, 0, ic, :],
                                      res_q[:, th * 512:(th + 1) * 512],
                                      start=True, stop=True))
                p.wait("vector", "T", mqe)
                qg[ic % 2] = p.step("vector", lambda E, ic=ic, th=th, ps=ps:
                                    E.tensor_copy(qfm[:, ic, th * 512:(th + 1) * 512],
                                                  ps[:, :]))
        p.wait("sync", "CC", ccr)
        dk = p.dma("sync", lambda E: E.dma_start(out=res_kT, in_=cc_res_out[0:64, :]))
        dv = p.dma("sync", lambda E: E.dma_start(out=res_vT, in_=cc_res_out[64:128, :]))
        p.wait_tok("tensor", dk)
        p.wait_tok("tensor", dv)
        p.wait("tensor", "V", p.cnt["V"])
        ABF.off = mark0
        Kh = ABF.alloc(128, 2048)
        Vtok = ABF.alloc3(128, 16, 128)
        Aexp = ABF.alloc(128, 2048)
        khg = [0]
        scg = [0]
        avg = {0: 0, 1: 0}
        atg = [0]
        aeg = [0]
        for h in range(16):
            for k4 in range(4):
                ps = PS0 if k4 % 2 == 0 else PS1
                if avg[k4 % 2]:
                    p.wait("tensor", "V", avg[k4 % 2])
                mk = p.step("tensor", lambda E, h=h, k4=k4, ps=ps:
                            E.matmul(ps[:, :], wT[0:64, 1, h, :],
                                     res_kT[:, k4 * 512:(k4 + 1) * 512],
                                     start=True, stop=True))
                p.wait("vector", "T", mk)
                avg[k4 % 2] = p.step("vector", lambda E, h=h, k4=k4, ps=ps:
                                     E.tensor_copy(Kh[:, k4 * 512:(k4 + 1) * 512],
                                                   ps[:, :]))
            khg[0] = avg[1]
            for grp in range(4):
                ps = PS2
                if grp > 0 or h > 0:
                    p.wait("tensor", "V", p.cnt["V"])
                for jj in range(4):
                    kc = grp * 4 + jj
                    p.pend("tensor", lambda E, h=h, kc=kc, jj=jj:
                           E.matmul(PS2[:, jj * 128:(jj + 1) * 128],
                                    res_vT[:, kc * 128:(kc + 1) * 128],
                                    wT[0:64, 2, h, :],
                                    start=True, stop=True))
                mv2 = p.close("tensor")
                p.wait("vector", "T", mv2)
                p.step("vector", lambda E, grp=grp:
                       E.tensor_copy(Vtok[:, grp * 4:(grp + 1) * 4, :]
                                     .rearrange("p a c -> p (a c)"), PS2[:, :]))
            vtg = p.mark("V")
            p.wait("tensor", "V", vtg)
            for qh in range(2):
                for qq in range(4):
                    qt = qh * 4 + qq
                    if scg[0]:
                        p.wait("tensor", "SC", scg[0])
                    for k4 in range(4):
                        p.pend("tensor", lambda E, h=h, qt=qt, k4=k4:
                               E.matmul(PSB[:, k4 * 512:(k4 + 1) * 512],
                                        qfm[:, h, qt * 128:(qt + 1) * 128],
                                        Kh[:, k4 * 512:(k4 + 1) * 512],
                                        start=True, stop=True))
                    sm = p.close("tensor")
                    p.wait("vector", "T", sm)
                    p.step("vector", lambda E: E.reduce_max(
                        negmax, PSB[:, :], axis=AX.X, negate=True))
                    nv = p.mark("V")
                    p.wait("scalar", "V", nv)
                    p.wait_dma_all("scalar", "scalar")
                    ex = p.step("scalar", lambda E, qt=qt: E.activation(
                        Aexp, PSB[:, :], AF.Exp, bias=negmax,
                        accum_out=sums[:, qt:qt + 1]))
                    scg[0] = ex
                    p.wait("vector", "SC", ex)
                    p.step("vector", lambda E, qt=qt:
                           E.reciprocal(recip[:, qt:qt + 1], sums[:, qt:qt + 1]))
                    p.wait("vector", "V", p.cnt["V"])
                    an = p.step("vector", lp(lambda E, qt=qt: E.tensor_scalar_mul(
                        Aexp, Aexp, recip[:, qt:qt + 1])))
                    p.wait("scalar", "V", an)
                    if atg[0]:
                        p.wait("scalar", "T", atg[0])
                    tok = None
                    for kc in range(16):
                        tok = p.dma("scalar", lambda E, kc=kc, qq=qq:
                                    E.dma_start_transpose(
                                        AT[:, kc, qq * 128:(qq + 1) * 128],
                                        Aexp[:, kc * 128:(kc + 1) * 128]))
                    aeg[0] = tok
                p.wait_dma_all("tensor", "scalar")
                ps = PS0 if qh == 0 else PS1
                if avg[qh]:
                    p.wait("tensor", "V", avg[qh])
                for kc in range(16):
                    p.pend("tensor", lambda E, kc=kc, qh=qh, ps=ps:
                           E.matmul(ps[:, :], Vtok[:, kc, :], AT[:, kc, 0:512],
                                    start=(kc == 0), stop=(kc == 15)))
                av = p.close("tensor")
                atg[0] = av
                p.wait("vector", "T", av)
                avg[qh] = p.step("vector", lambda E, h=h, qh=qh, ps=ps:
                                 E.tensor_copy(ao[:, h, qh * 512:(qh + 1) * 512],
                                               ps[:, :]))
        reso = res_q
        p.wait("tensor", "V", p.cnt["V"])
        for th in range(2):
            for ic in range(16):
                p.pend("tensor", lambda E, ic=ic, th=th:
                       E.matmul(PSB[0:64, th * 512:(th + 1) * 512], basTo[:, ic, :],
                                ao[:, ic, th * 512:(th + 1) * 512],
                                start=(ic == 0), stop=(ic == DC - 1)))
            ro = p.close("tensor")
            p.wait("vector", "T", ro)
            rv = p.step("vector", lambda E, th=th:
                        E.tensor_copy(reso[:, th * 512:(th + 1) * 512],
                                      PSB[0:64, th * 512:(th + 1) * 512]))
            p.wait("tensor", "V", rv)
        fg = {0: 0, 1: 0}
        for dc in range(DC):
            ps = PS0 if dc % 2 == 0 else PS1
            for th in range(2):
                if fg[dc % 2]:
                    p.wait("tensor", "V", fg[dc % 2])
                fm = p.step("tensor", lambda E, dc=dc, th=th, ps=ps:
                            E.matmul(ps[:, :], wT[0:64, 3, dc, :],
                                     reso[:, th * 512:(th + 1) * 512],
                                     start=True, stop=True))
                p.wait("vector", "T", fm)
                fg[dc % 2] = p.step("vector", lambda E, dc=dc, th=th, ps=ps:
                                    E.tensor_add(x_res[:, dc, th * 512:(th + 1) * 512],
                                                 x_res[:, dc, th * 512:(th + 1) * 512],
                                                 ps[:, :]))
        p.barrier()
        ABF.reset()
        AFA.reset()

    def l1_ffn():
        layernorm()
        ABF.reset()
        AFA.reset()
        TQ = 256
        b1col = AFA.alloc(128, 64)
        lgrow = AFA.alloc3(8, 2, 512)
        lgT = AFA.alloc3(128, 8, 8)
        top8 = AFA.alloc(128, 8)
        nm8 = AFA.alloc(128, 1)
        prb = AFA.alloc(128, 8)
        crtmp = AFA.alloc(16, 128)
        mixw = AFA.alloc(128, 256)
        tmpcv2 = AFA.alloc(64, 128)
        brcol = AFA.alloc(8, 1)
        t1sb = bufB[0:16, 0:2048].rearrange("p (a c) -> p a c", a=8)
        t2c = t1sb
        cbe = bufB[:, 2048:4096].rearrange("p (a c) -> p a c", a=8)
        base_r = bufB[:, 4096:6144].rearrange("p (a c) -> p a c", a=8)
        t2acc = bufB[0:16, 6144:8192].rearrange("p (a c) -> p a c", a=8)
        b1r = bufB[0:16, 8192:16384].rearrange("p (a b c) -> p a b c", a=8, b=8)
        b2r = bufA[0:16, 0:2048].rearrange("p (r a c) -> p r a c", r=2, a=8)
        crt = bufA[0:1, 2048:2560].rearrange("p (r c) -> p r c", r=2)
        csb = bufA[0:1, 2560:2816]
        aet = bufA[:, 2816:3328]
        cae = bufA[:, 3328:3840]
        b2row = bufA[0:1, 3840:5888]
        a1T = bufA[:, 5888:7936].rearrange("p (a c) -> p a c", a=16)
        mixq = bufA[:, 8192:16384].rearrange("p (a c) -> p a c", a=32)
        a2r = ABF.alloc3(128, 2, 1024)
        wrch = ABF.alloc3(128, 16, 8)
        load_colvec(b1_in, 64, b1col, tmpcv2)
        dbr = p.dma("sync", lambda E: E.dma_start(
            out=brcol, in_=br_in.rearrange("(a b) -> a b", b=1)))
        db2 = p.dma("gpsimd", lambda E: E.dma_start(
            out=b2row, in_=b2_in.rearrange("(a b) -> a b", a=1)))
        dwr = p.dma("gpsimd", lambda E: E.dma_start(
            out=wrch[:, :, :], in_=Wr.rearrange("(a p) c -> p a c", p=128)))
        a1nat = bufA[:, 8192:10240]
        da1 = p.dma("gpsimd", lambda E: E.dma_start(
            out=a1nat, in_=A1.rearrange("e r d -> (e r) d")))
        p.wait_tok("tensor", da1)
        for rnd in range(2):
            p.wait("tensor", "V", p.cnt["V"])
            for jj in range(8):
                p.pend("tensor", lambda E, rnd=rnd, jj=jj:
                       E.transpose(PT[:, jj * 128:(jj + 1) * 128],
                                   a1nat.rearrange("p (a c) -> p a c", a=16)
                                   [:, rnd * 8 + jj, :], ident_bf[:, :]))
            ta = p.close("tensor")
            p.wait("vector", "T", ta)
            p.step("vector", lambda E, rnd=rnd:
                   E.tensor_copy(a1T[:, rnd * 8:(rnd + 1) * 8, :]
                                 .rearrange("p a c -> p (a c)"), PT[:, :]))
        bnat = bufA[:, 8192:9216].rearrange("p (a c) -> p a c", a=64)
        stg16 = bufA[0:16, 9216:10240]
        for e in range(8):
            p.wait("gpsimd", "T", p.cnt["T"])
            p.wait("gpsimd", "V", p.cnt["V"])
            p.wait_dma_all("gpsimd", "sync")
            dbn = p.dma("gpsimd", lambda E, e=e: E.dma_start(
                out=bnat.rearrange("p a c -> p (a c)"),
                in_=B1[e, :, :].rearrange("(a p) c -> p a c", p=128)))
            p.wait_tok("tensor", dbn)
            for blk in range(8):
                p.wait("tensor", "V", p.cnt["V"])
                for jj in range(8):
                    p.pend("tensor", lambda E, blk=blk, jj=jj:
                           E.transpose(PT[0:16, jj * 128:(jj + 1) * 128],
                                       bnat[:, blk * 8 + jj, :], ident_bf[:, :]))
                tb1 = p.close("tensor")
                p.wait("vector", "T", tb1)
                p.wait_dma_all("vector", "sync")
                sv = p.step("vector", lambda E: E.tensor_copy(stg16, PT[0:16, :]))
                p.wait("sync", "V", sv)
                p.dma("sync", lambda E, e=e, blk=blk: E.dma_start(
                    out=b1tc[blk, :, e * 1024:(e + 1) * 1024],
                    in_=stg16))
        a2stg = bufA[:, 8192:10240]
        for half in range(4):
            p.wait("gpsimd", "T", p.cnt["T"])
            p.wait("gpsimd", "V", p.cnt["V"])
            p.wait_dma_all("gpsimd", "sync")
            dan = p.dma("gpsimd", lambda E, half=half: E.dma_start(
                out=a2stg, in_=A2.rearrange("e r f -> (e r) f")
                [:, half * 2048:(half + 1) * 2048]))
            p.wait_tok("tensor", dan)
            for blk in range(2):
                p.wait("tensor", "V", p.cnt["V"])
                for jj in range(8):
                    p.pend("tensor", lambda E, blk=blk, jj=jj:
                           E.transpose(PT[:, jj * 128:(jj + 1) * 128],
                                       a2stg.rearrange("p (a c) -> p a c", a=16)
                                       [:, blk * 8 + jj, :], ident_bf[:, :]))
                ta2 = p.close("tensor")
                p.wait("vector", "T", ta2)
                p.wait_dma_all("vector", "sync")
                sv = p.step("vector", lambda E: E.tensor_copy(aet, PT[:, 0:512]))
                sv2 = p.step("vector", lambda E: E.tensor_copy(cae, PT[:, 512:1024]))
                p.wait("sync", "V", sv2)
                for q4 in range(8):
                    fcx = half * 16 + blk * 8 + q4
                    srcb = aet if q4 < 4 else cae
                    p.dma("sync", lambda E, fcx=fcx, srcb=srcb, q4=q4: E.dma_start(
                        out=a2tc[fcx, :, :],
                        in_=srcb.rearrange("p (a c) -> p a c", a=4)[:, q4 % 4, :]))
        b2nat = bufA[:, 8192:8448].rearrange("p (a c) -> p a c", a=16)
        for e in range(8):
            p.wait("gpsimd", "T", p.cnt["T"])
            p.wait("gpsimd", "V", p.cnt["V"])
            p.wait_dma_all("gpsimd", "sync")
            dbn = p.dma("gpsimd", lambda E, e=e: E.dma_start(
                out=b2nat.rearrange("p a c -> p (a c)"),
                in_=B2[e, :, :].rearrange("(a p) c -> p a c", p=128)))
            p.wait_tok("tensor", dbn)
            for blk in range(2):
                p.wait("tensor", "V", p.cnt["V"])
                for jj in range(8):
                    p.pend("tensor", lambda E, blk=blk, jj=jj:
                           E.transpose(PT[0:16, jj * 128:(jj + 1) * 128],
                                       b2nat[:, blk * 8 + jj, :], ident_bf[:, :]))
                tb2 = p.close("tensor")
                p.wait("vector", "T", tb2)
                p.wait_dma_all("vector", "sync")
                sv = p.step("vector", lambda E: E.tensor_copy(stg16, PT[0:16, :]))
                p.wait("sync", "V", sv)
                p.dma("sync", lambda E, e=e, blk=blk: E.dma_start(
                    out=b2tc[blk * 8:(blk + 1) * 8, e, :, :].transpose([1, 0, 2]),
                    in_=stg16.rearrange("p (a c) -> p a c", a=8)))
        p.wait_dma_all("tensor", "sync")
        p.wait_tok("tensor", dwr)
        p.wait_tok("scalar", dbr)
        for th in range(2):
            for dc in range(DC):
                p.pend("tensor", lambda E, th=th, dc=dc:
                       E.matmul(PSB[0:8, th * 512:(th + 1) * 512], wrch[:, dc, :],
                                xln[:, dc, th * 512:(th + 1) * 512],
                                start=(dc == 0), stop=(dc == DC - 1)))
            rm = p.close("tensor")
            p.wait("scalar", "T", rm)
            p.step("scalar", lambda E, th=th: E.activation(
                lgrow[:, th, :], PSB[0:8, th * 512:(th + 1) * 512], AF.Identity,
                bias=brcol))
        p.wait("tensor", "SC")
        p.wait("tensor", "V", p.cnt["V"])
        for tt in range(8):
            p.pend("tensor", lambda E, tt=tt:
                   E.transpose(PS2[:, tt * 8:(tt + 1) * 8],
                               lgrow.rearrange("p a c -> p (a c)")
                               [0:8, tt * 128:(tt + 1) * 128], ident[0:8, 0:8]))
        tl = p.close("tensor")
        p.wait("vector", "T", tl)
        p.step("vector", lambda E: E.tensor_copy(
            lgT.rearrange("p a c -> p (a c)"), PS2[:, 0:64]))
        for tt in range(8):
            p.wait("vector", "V", p.cnt["V"])
            p.step("vector", lambda E, tt=tt: E.reduce_max(
                nm8, lgT[:, tt, :], axis=AX.X, negate=True))
            p.wait("scalar", "V", p.cnt["V"])
            exm = p.step("scalar", lambda E, tt=tt: E.activation(
                prb, lgT[:, tt, :], AF.Exp, bias=nm8, accum_out=nm8))
            p.wait("vector", "SC", exm)
            p.step("vector", lambda E: E.reciprocal(nm8, nm8))
            p.wait("vector", "V", p.cnt["V"])
            p.step("vector", lambda E: E.tensor_scalar_mul(prb, prb, nm8))
            p.wait("vector", "V", p.cnt["V"])
            p.step("vector", lambda E: E.max(top8, prb))
            p.wait("vector", "V", p.cnt["V"])
            p.step("vector", lambda E, tt=tt: E.tensor_scalar(
                lgT[:, tt, :], prb, top8[:, 1:2], None, ALU.is_ge, ALU.bypass))
            p.wait("vector", "V", p.cnt["V"])
            p.step("vector", lambda E, tt=tt: E.tensor_mul(lgT[:, tt, :],
                                                           lgT[:, tt, :], prb))
            p.wait("tensor", "V", p.cnt["V"])
            t1m = p.step("tensor", lambda E, tt=tt:
                         E.transpose(PS2[0:8, 0:128], lgT[:, tt, :], ident[:, :]))
            p.wait("vector", "T", t1m)
            p.wait_dma_all("vector", "sync")
            sv = p.step("vector", lambda E: E.tensor_copy(crtmp[0:8, :],
                                                          PS2[0:8, 0:128]))
            p.wait("sync", "V", sv)
            p.dma("sync", lambda E, tt=tt: E.dma_start(
                out=crows_d[0:8, tt * 128:(tt + 1) * 128], in_=crtmp[0:8, :]))
        p.wait_dma_all("gpsimd", "sync")
        ring = Ring(p)
        basem = {0: 0, 1: 0}
        l1g = {0: 0, 1: 0}
        xg = {0: 0, 1: 0}
        for tq in range(4):
            tsl = slice(tq * TQ, (tq + 1) * TQ)
            for e in range(8):
                if e == 0:
                    p.wait("tensor", "V", p.cnt["V"])
                for dc in range(DC):
                    p.pend("tensor", lambda E, e=e, dc=dc, tsl=tsl:
                           E.matmul(PS2[0:16, 0:TQ], a1T[:, dc, e * 16:(e + 1) * 16],
                                    xln[:, dc, tsl],
                                    start=(dc == 0), stop=(dc == DC - 1)))
                t1m = p.close("tensor")
                p.wait("vector", "T", t1m)
                tv = p.step("vector", lambda E, e=e:
                            E.tensor_copy(t1sb[:, e, :], PS2[0:16, 0:TQ]))
                p.wait("tensor", "V", tv)
            for e in range(8):
                rr = e % 2
                p.wait("gpsimd", "T", p.cnt["T"])
                dcr = p.dma("gpsimd", lambda E, e=e, rr=rr, tsl=tsl: E.dma_start(
                    out=crt[:, rr, :], in_=crows_d[e:e + 1, tsl]))
                p.wait_tok("tensor", dcr)
                p.wait("tensor", "V", p.cnt["V"])
                cbm = p.step("tensor", lambda E, rr=rr:
                             E.matmul(PS2[:, 0:TQ], ones_rowb[0:1, 0:128],
                                      crt[0:1, rr, :], start=True, stop=True))
                p.wait("vector", "T", cbm)
                p.step("vector", lp(lambda E, e=e:
                                    E.tensor_copy(cbe[:, e, :], PS2[:, 0:TQ])))
            p.wait("vector", "V", p.cnt["V"])
            p.step("vector", lambda E: E.tensor_copy(csb, cbe[0:1, 0, :]))
            for e in range(1, 8):
                p.wait("vector", "V", p.cnt["V"])
                p.step("vector", lp(lambda E, e=e: E.tensor_add(
                    csb, csb, cbe[0:1, e, :])))
            for fh in range(2):
                t2first = {0: True, 1: True, 2: True, 3: True}
                for R in range(4):
                    fc0 = fh * 32 + R * 8
                    rrA = R % 2
                    p.wait("gpsimd", "T", p.cnt["T"])
                    da2 = p.dma("gpsimd", lambda E, fc0=fc0, rrA=rrA: E.dma_start(
                        out=a2r[:, rrA, :].rearrange("p (a c) -> p a c", a=8),
                        in_=a2tc[fc0:fc0 + 8, :, :].transpose([1, 0, 2])))
                    p.wait("gpsimd", "T", p.cnt["T"])
                    db1 = p.dma("gpsimd", lambda E, fc0=fc0: E.dma_start(
                        out=b1r.rearrange("p a b c -> p (a b c)"),
                        in_=b1tc[fc0 // 8, :, :]))
                    for pg in (0, 1):
                        if l1g[pg]:
                            p.wait("tensor", "V", l1g[pg])
                            l1g[pg] = 0
                        if xg[pg]:
                            p.wait("tensor", "V", xg[pg])
                            xg[pg] = 0
                    for j8 in range(8):
                        fc = fc0 + j8
                        par = ring.next()
                        ldw = p.dma("gpsimd", lambda E, fc=fc, par=par: E.dma_start(
                            out=wring[:, par * 16:(par + 1) * 16, :],
                            in_=W1[:, fc * 128:(fc + 1) * 128]
                            .rearrange("(a p) c -> p a c", p=128)))
                        p.wait_tok("tensor", ldw)
                        if basem[j8 % 2]:
                            p.wait("tensor", "SC", basem[j8 % 2])
                        psl = [PS0, PS1][j8 % 2]
                        for dc in range(DC):
                            p.pend("tensor", lambda E, psl=psl, dc=dc, par=par, tsl=tsl:
                                   E.matmul(psl[:, 0:TQ], wring[:, par * 16 + dc, :],
                                            xln[:, dc, tsl],
                                            start=(dc == 0), stop=(dc == DC - 1)))
                        bm = p.close("tensor")
                        ring.consumed(par, bm)
                        p.wait("scalar", "T", bm)
                        basem[j8 % 2] = p.step(
                            "scalar", lambda E, psl=psl, fc=fc, j8=j8:
                            E.activation(base_r[:, j8, :], psl[:, 0:TQ], AF.Identity,
                                         bias=b1col[:, fc:fc + 1]))
                    p.wait("tensor", "SC")
                    p.wait_tok("tensor", da2)
                    p.wait_tok("tensor", db1)
                    for j8 in range(8):
                        fc = fc0 + j8
                        for e in range(8):
                            psl = [PS0, PS1][e % 2]
                            if l1g[e % 2]:
                                p.wait("tensor", "V", l1g[e % 2])
                            lm = p.step("tensor", lambda E, e=e, j8=j8, psl=psl:
                                        E.matmul(psl[:, 0:TQ], b1r[:, e, j8, :],
                                                 t1sb[:, e, :], start=True, stop=True))
                            p.wait("vector", "T", lm)
                            p.wait("vector", "SC", p.cnt["SC"])
                            p.wait("vector", "V", p.cnt["V"])
                            sv = p.step("vector", lp(lambda E, j8=j8, psl=psl:
                                        E.scalar_tensor_tensor(
                                            aet[:, 0:TQ], psl[:, 0:TQ], 2.0,
                                            base_r[:, j8, :], ALU.mult, ALU.add)))
                            l1g[e % 2] = sv
                            p.wait("scalar", "V", sv)
                            gm = p.step("scalar", lambda E: E.activation(
                                aet[:, 0:TQ], aet[:, 0:TQ], AF.Gelu_apprx_tanh))
                            p.wait("vector", "SC", gm)
                            cm = p.step("vector", lp(lambda E, e=e: E.tensor_mul(
                                cae[:, 0:TQ], aet[:, 0:TQ], cbe[:, e, :])))
                            p.wait("vector", "V", cm)
                            if e == 0:
                                p.step("vector", lambda E: E.tensor_copy(
                                    mixw, cae[:, 0:TQ]))
                            elif e == 7:
                                p.step("vector", lp(lambda E, fc=fc:
                                       E.tensor_add(mixq[:, fc % 32, :], mixw,
                                                    cae[:, 0:TQ])))
                            else:
                                p.step("vector", lambda E: E.tensor_add(
                                    mixw, mixw, cae[:, 0:TQ]))
                            p.wait("tensor", "SC", gm)
                            stt2 = t2first[e // 2]
                            t2first[e // 2] = False
                            spt2 = (R == 3 and j8 == 7 and e % 2 == 1)
                            p.pend("tensor", lambda E, e=e, j8=j8, rrA=rrA,
                                   st=stt2, sp=spt2:
                                   E.matmul(PSB[0:16, e * 256:e * 256 + TQ],
                                            a2r[:, rrA, :]
                                            .rearrange("p (a c) -> p a c", a=8)
                                            [:, j8, e * 16:(e + 1) * 16],
                                            aet[:, 0:TQ],
                                            start=st, stop=sp))
                t2m = p.close("tensor")
                p.wait("vector", "T", t2m)
                for e in range(8):
                    p.wait("vector", "V", p.cnt["V"])
                    if fh == 0:
                        p.step("vector", lp(lambda E, e=e: E.tensor_copy(
                            t2acc[:, e, :], PSB[0:16, e * 256:e * 256 + TQ])))
                    else:
                        p.step("vector", lp(lambda E, e=e: E.tensor_add(
                            t2acc[:, e, :], t2acc[:, e, :],
                            PSB[0:16, e * 256:e * 256 + TQ])))
                t2v = p.mark("V")
                p.wait("tensor", "V", t2v)
                if fh == 1:
                    for e in range(8):
                        p.wait("vector", "V", p.cnt["V"])
                        p.step("vector", lp(lambda E, e=e: E.scalar_tensor_tensor(
                            t2c[:, e, :], t2acc[:, e, :], 2.0, cbe[0:16, e, :],
                            ALU.mult, ALU.mult)))
                    tcv = p.mark("V")
                    p.wait("tensor", "V", tcv)
                for dc in range(DC):
                    psw = [PS0, PS1][dc % 2]
                    if xg[dc % 2]:
                        p.wait("tensor", "V", xg[dc % 2])
                    if l1g[dc % 2]:
                        p.wait("tensor", "V", l1g[dc % 2])
                        l1g[dc % 2] = 0
                    first = True
                    for wh in range(2):
                        par = ring.next()
                        ldw2 = p.dma("gpsimd", lambda E, dc=dc, fh=fh, wh=wh, par=par:
                                     E.dma_start(
                                         out=wring[:, par * 16:(par + 1) * 16, :],
                                         in_=W2[fh * 4096 + wh * 2048:
                                                fh * 4096 + (wh + 1) * 2048,
                                                dc * 128:(dc + 1) * 128]
                                         .rearrange("(a p) c -> p a c", p=128)))
                        p.wait_tok("tensor", ldw2)
                        for j in range(16):
                            fcl = wh * 16 + j
                            sp = (fh == 0 and wh == 1 and j == 15)
                            p.pend("tensor",
                                   lambda E, psw=psw, j=j, par=par, fcl=fcl,
                                   st=first, sp=sp:
                                   E.matmul(psw[:, 0:TQ], wring[:, par * 16 + j, :],
                                            mixq[:, fcl, :], start=st, stop=sp))
                            first = False
                        wm = p.close("tensor")
                        ring.consumed(par, wm)
                    if fh == 1:
                        rrb2 = dc % 2
                        p.wait("gpsimd", "T", p.cnt["T"])
                        db2r = p.dma("gpsimd", lambda E, dc=dc, rrb2=rrb2:
                                     E.dma_start(
                                         out=b2r[:, rrb2, :, :],
                                         in_=b2tc[dc, :, :, :].transpose([1, 0, 2])))
                        p.wait_tok("tensor", db2r)
                        p.pend("tensor", lambda E, dc=dc, psw=psw: E.matmul(
                            psw[:, 0:TQ],
                            b2row[0:1, dc * 128:(dc + 1) * 128], csb,
                            start=False, stop=False))
                        for e in range(8):
                            p.pend("tensor", lambda E, e=e, rrb2=rrb2, psw=psw,
                                   sp=(e == 7):
                                   E.matmul(psw[:, 0:TQ], b2r[:, rrb2, e, :],
                                            t2c[:, e, :], start=False, stop=sp))
                        wm = p.close("tensor")
                    p.wait("vector", "T", wm)
                    xg[dc % 2] = p.step("vector", lambda E, dc=dc, psw=psw, tsl=tsl:
                                        E.tensor_add(x_res[:, dc, tsl],
                                                     x_res[:, dc, tsl],
                                                     psw[:, 0:TQ]))
        p.barrier()
        ABF.reset()
        AFA.reset()
    def phase_out():
        p.barrier()
        AFA.reset()
        stg = AFA.alloc(128, 1024)
        stg2 = AFA.alloc(128, 1024)
        guards = {0: 0, 1: 0}
        sguard = {0: 0, 1: 0}
        for tt in range(8):
            for dh in range(2):
                par = dh
                st = stg if par == 0 else stg2
                if sguard[par]:
                    p.wait_tok("vector", sguard[par])
                for j in range(8):
                    dc = dh * 8 + j
                    pst = [PS0, PS1][j % 2]
                    if guards[j % 2]:
                        p.wait("tensor", "V", guards[j % 2])
                    td = p.step("tensor", lambda E, pst=pst, dc=dc, tt=tt:
                                E.transpose(pst[:, 0:128],
                                            x_res[:, dc, tt * 128:(tt + 1) * 128],
                                            ident[:, :]))
                    p.wait("vector", "T", td)
                    ev = p.step("vector", lambda E, pst=pst, st=st, j=j:
                                E.tensor_copy(st[:, j * 128:(j + 1) * 128], pst[:, 0:128]))
                    guards[j % 2] = ev
                p.wait("sync", "V")
                sguard[par] = p.dma("sync", lambda E, tt=tt, dh=dh, st=st:
                                    E.dma_start(out=y_out[tt * 128:(tt + 1) * 128,
                                                          dh * 1024:(dh + 1) * 1024],
                                                in_=st))
        p.wait_dma_all("sync", "sync")

    def tap_xln_chunk(chunk):
        p.barrier()
        AFA.reset()
        stg = AFA.alloc(128, 1024)
        p.step("vector", lambda E: E.tensor_copy(stg, xln[:, chunk, :]))
        p.wait("sync", "V")
        p.dma("sync", lambda E: E.dma_start(out=dbg_out[0:128, 0:1024], in_=stg))
        p.wait_dma_all("sync", "sync")

    phase0()
    if tap == "ln1":
        layernorm()
        tap_xln_chunk(0)
    elif tap == "l0a":
        l0_attn()
    elif tap == "l0f":
        l0_attn()
        l0_ffn()
    elif tap == "l1a":
        l0_attn()
        l0_ffn()
        l1_attn()
    else:
        l0_attn()
        l0_ffn()
        l1_attn()
        l1_ffn()
    phase_out()

    with nc.Block() as block:
        @block.sync
        def _(eng):
            for fn in p.q["sync"]:
                fn(eng)

        @block.tensor
        def _(eng):
            for fn in p.q["tensor"]:
                fn(eng)

        @block.vector
        def _(eng):
            for fn in p.q["vector"]:
                fn(eng)

        @block.scalar
        def _(eng):
            for fn in p.q["scalar"]:
                fn(eng)

        @block.gpsimd
        def _(eng):
            for fn in p.q["gpsimd"]:
                fn(eng)

    return nc, meta


def make_inputs_for_core(core, inputs):
    b = core // 2
    half = core % 2
    d = {}
    x = np.asarray(inputs["L0_attn_hidden_states"])
    d["x"] = np.ascontiguousarray(x[b, half * S:(half + 1) * S, :])
    for k, v in inputs.items():
        if k == "L0_attn_hidden_states":
            continue
        d[k] = np.ascontiguousarray(np.asarray(v))
    d["ident128"] = np.eye(128, dtype=np.float32)
    import ml_dtypes
    pm = np.zeros((128, SK), dtype=ml_dtypes.bfloat16)
    pm[:, half * S:(half + 1) * S] = 1.0
    d["pmask"] = pm
    sel = np.zeros((32, 32, 64), dtype=np.float32)
    for h in range(32):
        sel[h, h, :] = 1.0
    d["selbc"] = sel.reshape(32, 2048).astype(ml_dtypes.bfloat16)
    return d


def kernel(**inputs):
    nc, _ = build()
    in_maps = [make_inputs_for_core(c, inputs) for c in range(NCORES)]
    res = run_bass_kernel_spmd(nc, in_maps, core_ids=list(range(NCORES)))
    out = np.empty((B, SK, D), dtype=np.float32)
    for c in range(NCORES):
        b = c // 2
        half = c % 2
        out[b, half * S:(half + 1) * S, :] = res.results[c]["y"]
    return out


# version 03
# speedup vs baseline: 1.7008x; 1.7008x over previous
import numpy as np

import concourse.bass as bass
import concourse.mybir as mybir
from concourse.bass_utils import run_bass_kernel_spmd
from concourse.bass_types import AP

F32 = mybir.dt.float32
BF16 = mybir.dt.bfloat16
AF = mybir.ActivationFunctionType
ALU = mybir.AluOpType
AX = mybir.AxisListType

D = 2048
DC = 16
S = 1024
SK = 2048
FF = 8192
FC = 64
B = 4
NCORES = 8

L1F_PARTS = frozenset({"t1", "base", "quads", "gp", "t2", "w2", "l2"})

ENGS = ("sync", "tensor", "vector", "scalar", "gpsimd")
PROG_SEM = {"sync": "SY", "tensor": "T", "vector": "V", "scalar": "SC", "gpsimd": "G"}
NDMASUB = 4
DMA_SEM = {q: [f"{q[:2].upper()}{i}" for i in range(NDMASUB)]
           for q in ("sync", "scalar", "gpsimd")}


class Prog:
    def __init__(self, nc):
        self.nc = nc
        self.q = {e: [] for e in ENGS}
        self.sem_handles = {}
        self.cnt = {}
        self._sem_ctxs = []
        self.dma_ctr = {q: 0 for q in DMA_SEM}
        self._pend = {e: None for e in ENGS}
        for s in list(PROG_SEM.values()) + ["CC"]:
            self.new_sem(s)
        for q, subs in DMA_SEM.items():
            for s in subs:
                self.new_sem(s)

    def new_sem(self, name):
        ctx = self.nc.semaphore(name)
        handle = ctx.__enter__()
        self._sem_ctxs.append(ctx)
        self.sem_handles[name] = handle
        self.cnt[name] = 0

    def emit(self, eng, fn):
        self._flush(eng)
        self.q[eng].append(fn)

    def pend(self, eng, fn):
        self._flush(eng)
        self._pend[eng] = fn

    def close(self, eng):
        fn = self._pend[eng]
        assert fn is not None, f"close({eng}) with no pending instruction"
        self._pend[eng] = None
        self.cnt[PROG_SEM[eng]] += 1
        tgt = self.cnt[PROG_SEM[eng]]
        h = self.sem_handles[PROG_SEM[eng]]
        self.q[eng].append(lambda E, f=fn, h=h: f(E).then_inc(h, 1))
        return tgt

    def _flush(self, eng):
        fn = self._pend.get(eng)
        if fn is not None:
            self._pend[eng] = None
            self.q[eng].append(fn)

    def inc(self, eng, sem, fn, n=1):
        self._flush(eng)
        self.cnt[sem] += n
        tgt = self.cnt[sem]
        h = self.sem_handles[sem]
        self.q[eng].append(lambda E, f=fn, h=h, n=n: f(E).then_inc(h, n))
        return tgt

    def step(self, eng, fn):
        return self.inc(eng, PROG_SEM[eng], fn)

    def dma(self, eng, fn):
        sub = DMA_SEM[eng][self.dma_ctr[eng] % NDMASUB]
        self.dma_ctr[eng] += 1
        if self.cnt[sub] > 0:
            self.wait(eng, sub, self.cnt[sub])
        v = self.inc(eng, sub, fn, 16)
        return (sub, v)

    def wait_tok(self, eng, tok):
        if tok:
            self.wait(eng, tok[0], tok[1])

    def wait_dma_all(self, eng, queue):
        for sub in DMA_SEM[queue]:
            self.wait(eng, sub, self.cnt[sub])

    def wait(self, eng, sem, val=None):
        v = self.cnt[sem] if val is None else val
        assert v <= self.cnt[sem], f"wait target {v} > emitted {self.cnt[sem]} on {sem}"
        if v <= 0:
            return
        self._flush(eng)
        h = self.sem_handles[sem]
        self.q[eng].append(lambda E, h=h, v=v: E.wait_ge(h, v))

    def mark(self, sem):
        return self.cnt[sem]

    def barrier(self, engs=ENGS):
        marks = {}
        for eng in engs:
            self.wait(eng, PROG_SEM[eng], self.cnt[PROG_SEM[eng]])
            marks[eng] = self.inc(eng, PROG_SEM[eng], lambda E: E.nop())
        for eng in engs:
            for oeng in engs:
                if oeng != eng:
                    self.wait(eng, PROG_SEM[oeng], marks[oeng])
            for q in DMA_SEM:
                self.wait_dma_all(eng, q)


class Arena:

    def __init__(self, tensor, total):
        self.t = tensor
        self.total = total
        self.off = 0

    def alloc(self, parts, n):
        assert self.off + n <= self.total, f"arena overflow: {self.off}+{n} > {self.total}"
        ap = self.t[0:parts, self.off:self.off + n]
        self.off += n
        return ap

    def alloc3(self, parts, a, b_):
        ap = self.alloc(parts, a * b_)
        return ap.rearrange("p (a b) -> p a b", a=a)

    def reset(self):
        self.off = 0


class Ring:

    def __init__(self, p, eng="gpsimd"):
        self.p = p
        self.eng = eng
        self.ctr = 0
        self.guard = [0, 0]

    def next(self):
        par = self.ctr % 2
        self.ctr += 1
        if self.guard[par]:
            self.p.wait(self.eng, "T", self.guard[par])
        return par

    def consumed(self, par, mark):
        self.guard[par] = mark


def build(tap=None):
    nc = bass.Bass("TRN2", target_bir_lowering=False)

    def din(name, shape, dtype=F32):
        return nc.dram_tensor(name, list(shape), dtype, kind="ExternalInput").ap()

    x_in = din("x", [S, D])
    Wq = din("L0_attn_Wq", [D, D])
    Wk = din("L0_attn_Wk", [D, D])
    Wv = din("L0_attn_Wv", [D, D])
    Wo = din("L0_attn_Wo", [D, D])
    bo_in = din("L0_attn_bo", [D])
    up_w = din("L0_ffn_up_weight", [FF, D])
    dn_w = din("L0_ffn_down_weight", [D, FF])
    up_s = din("L0_ffn_up_scale", [FF])
    dn_s = din("L0_ffn_down_scale", [D])
    out_s = din("L0_ffn_output_scale", [1])
    basis_in, phase_in, amp_in = {}, {}, {}
    for nm in ("q", "k", "v", "o"):
        basis_in[nm] = din(f"L1_attn_basis_{nm}", [64, D])
        phase_in[nm] = din(f"L1_attn_phase_{nm}", [D, 64])
        amp_in[nm] = din(f"L1_attn_amp_{nm}", [D, 64])
    Wr = din("L1_ffn_Wr", [D, 8])
    br_in = din("L1_ffn_br", [8])
    W1 = din("L1_ffn_W1", [D, FF])
    b1_in = din("L1_ffn_b1", [FF])
    W2 = din("L1_ffn_W2", [FF, D])
    b2_in = din("L1_ffn_b2", [D])
    A1 = din("L1_ffn_A1", [8, 16, D])
    B1 = din("L1_ffn_B1", [8, FF, 16])
    A2 = din("L1_ffn_A2", [8, 16, FF])
    B2 = din("L1_ffn_B2", [8, D, 16])
    ident_in = din("ident128", [128, 128])
    pmask_in = din("pmask", [128, SK], BF16)
    selbc_in = din("selbc", [32, 2048], BF16)

    y_out = nc.dram_tensor("y", [S, D], F32, kind="ExternalOutput").ap()
    dbg_out = nc.dram_tensor("dbg", [128, 4096], F32, kind="ExternalOutput").ap()

    cc_kv_in = nc.dram_tensor("cc_kv_in", [128, 2064], F32).ap()
    cc_kv_out = nc.dram_tensor("cc_kv_out", [128, 2064], F32).ap()
    cc_res_in = nc.dram_tensor("cc_res_in", [128, SK], BF16).ap()
    cc_res_out = nc.dram_tensor("cc_res_out", [128, SK], BF16).ap()
    upTc = nc.dram_tensor("upTc", [FC, DC, 128, 128], BF16).ap()
    dnTc = nc.dram_tensor("dnTc", [DC, FC, 128, 128], BF16).ap()
    w1c = nc.dram_tensor("w1c", [FC, DC, 128, 128], BF16).ap()
    w2c = nc.dram_tensor("w2c", [DC, FC, 128, 128], BF16).ap()
    b1tc = nc.dram_tensor("b1tc", [8, 16, 8192], BF16).ap()
    a2tc = nc.dram_tensor("a2tc", [FC, 128, 128], BF16).ap()
    b2tc = nc.dram_tensor("b2tc", [DC, 8, 16, 128], BF16).ap()
    crows_d = nc.dram_tensor("crows_d", [9, S], F32).ap()

    x_res = nc.alloc_sbuf_tensor("x_res", [128, DC, S], F32)
    xln = nc.alloc_sbuf_tensor("xln", [128, DC, S], BF16)
    bufA = nc.alloc_sbuf_tensor("bufA", [128, 16384], BF16)
    bufB = nc.alloc_sbuf_tensor("bufB", [128, 16384], BF16)
    wring = nc.alloc_sbuf_tensor("wring", [128, 32, 128], BF16)
    ident = nc.alloc_sbuf_tensor("ident", [128, 128], F32)
    ident_bf = nc.alloc_sbuf_tensor("ident_bf", [128, 128], BF16)
    ones_col = nc.alloc_sbuf_tensor("ones_col", [128, 1], F32)
    ones_colb = nc.alloc_sbuf_tensor("ones_colb", [128, 1], BF16)
    ones_row = nc.alloc_sbuf_tensor("ones_row", [1, 128], F32)
    pio2 = nc.alloc_sbuf_tensor("pio2", [128, 1], F32)
    ones_rowb = nc.alloc_sbuf_tensor("ones_rowb", [1, 512], BF16)
    arena_bf_t = nc.alloc_sbuf_tensor("arena_bf", [128, 13056], BF16)
    arena_f32_t = nc.alloc_sbuf_tensor("arena_f32", [128, 2624], F32)
    sbuf_left = nc.sbuf_bytes_remaining

    ABF = Arena(arena_bf_t, 13056)
    AFA = Arena(arena_f32_t, 2624)

    PSB = nc.alloc_psum_tensor("PSB", [128, 2048], F32)
    PS0 = nc.alloc_psum_tensor("PS0", [128, 512], F32)
    PS1 = nc.alloc_psum_tensor("PS1", [128, 512], F32)
    PS2 = nc.alloc_psum_tensor("PS2", [128, 512], F32)
    PT = nc.alloc_psum_tensor("PT", [128, 1024], BF16)

    p = Prog(nc)
    meta = dict(sbuf_left=sbuf_left)

    def lp(fn):
        def wrapped(E, fn=fn):
            with nc.allow_low_precision(reason="bf16 operand for subsequent matmul"):
                return fn(E)
        return wrapped

    def load_colvec(vec_dram, nch, dst, tmp):
        p.wait("sync", "T", p.cnt["T"])
        ld = p.dma("sync", lambda E: E.dma_start(
            out=tmp[0:nch, 0:128], in_=vec_dram.rearrange("(a b) -> a b", b=128)))
        p.wait_tok("tensor", ld)
        p.wait("tensor", "V", p.cnt["V"])
        tm = p.step("tensor", lambda E: E.transpose(PS2[:, 0:nch], tmp[0:nch, 0:128],
                                                    ident[0:nch, 0:nch]))
        p.wait("vector", "T", tm)
        return p.step("vector", lambda E: E.tensor_copy(dst, PS2[:, 0:nch]))

    def phase0():
        AFA.reset()
        stg = AFA.alloc(128, 1024)
        stg2 = AFA.alloc(128, 1024)
        for t_ in (ones_col, ones_colb, ones_row, ones_rowb):
            p.emit("vector", lambda E, t_=t_: E.memset(t_[:, :], 1.0))
        p.emit("vector", lambda E: E.memset(pio2[:, :], float(np.pi / 2)))
        p.step("vector", lambda E: E.memset(stg2, 0.0))
        p.dma("sync", lambda E: E.dma_start(out=ident[:, :], in_=ident_in[:, :]))
        p.dma("gpsimd", lambda E: E.dma_start(out=ident_bf[:, :], in_=ident_in[:, :]))
        p.wait("sync", "V")
        for c0 in range(0, 2064, 1024):
            w = min(1024, 2064 - c0)
            p.dma("sync", lambda E, c0=c0, w=w:
                  E.dma_start(out=cc_kv_in[:, c0:c0 + w], in_=stg2[:, 0:w]))
        p.wait_dma_all("sync", "sync")
        guards = {0: 0, 1: 0}
        sguard = {0: 0, 1: 0}
        for tt in range(8):
            for dh in range(2):
                par = (tt * 2 + dh) % 2
                st = stg if par == 0 else stg2
                if sguard[par]:
                    p.wait("sync", "T", sguard[par])
                ld = p.dma("sync", lambda E, tt=tt, dh=dh, st=st:
                           E.dma_start(out=st,
                                       in_=x_in[tt * 128:(tt + 1) * 128,
                                                dh * 1024:(dh + 1) * 1024]))
                p.wait_tok("tensor", ld)
                for j in range(8):
                    dc = dh * 8 + j
                    pst = [PS0, PS1][j % 2]
                    if guards[j % 2]:
                        p.wait("tensor", "V", guards[j % 2])
                    td = p.step("tensor", lambda E, pst=pst, st=st, j=j:
                                E.transpose(pst[:, 0:128], st[:, j * 128:(j + 1) * 128],
                                            ident[:, :]))
                    p.wait("vector", "T", td)
                    ev = p.step("vector", lambda E, pst=pst, dc=dc, tt=tt:
                                E.tensor_copy(x_res[:, dc, tt * 128:(tt + 1) * 128],
                                              pst[:, 0:128]))
                    guards[j % 2] = ev
                sguard[par] = p.mark("T")

    def layernorm():
        p.barrier()
        AFA.reset()
        strow = AFA.alloc3(1, 2, 1024)
        stg = AFA.alloc(128, 512)
        prev_mm = 0
        for dc in range(DC):
            for th in range(2):
                sl = slice(th * 512, (th + 1) * 512)
                if prev_mm:
                    p.wait("scalar", "T", prev_mm)
                sq = p.step("scalar", lambda E, dc=dc, sl=sl:
                            E.activation(stg, x_res[:, dc, :][:, sl], AF.Square))
                p.emit("tensor", lambda E, dc=dc, sl=sl:
                       E.matmul(PSB[0:1, 0:1024][:, sl], ones_col[:, :],
                                x_res[:, dc, :][:, sl], start=(dc == 0), stop=(dc == DC - 1)))
                p.wait("tensor", "SC", sq)
                prev_mm = p.step("tensor", lambda E, dc=dc, th=th:
                                 E.matmul(PSB[32:33, 1024 + th * 512:1536 + th * 512],
                                          ones_col[:, :], stg,
                                          start=(dc == 0), stop=(dc == DC - 1)))
        p.wait("vector", "T")
        p.step("vector", lambda E: E.tensor_scalar_mul(strow[:, 0, :], PSB[0:1, 0:1024], 1.0 / D))
        p.wait("vector", "V")
        p.step("vector", lambda E: E.tensor_mul(strow[:, 1, :], strow[:, 0, :], strow[:, 0, :]))
        p.wait("vector", "V")
        p.step("vector", lambda E:
               E.scalar_tensor_tensor(strow[:, 1, :], PSB[32:33, 1024:2048], 1.0 / D,
                                      strow[:, 1, :], ALU.mult, ALU.subtract))
        p.wait("vector", "V")
        vs = p.step("vector", lambda E: E.tensor_scalar_add(strow[:, 1, :], strow[:, 1, :], 1e-5))
        p.wait("scalar", "V", vs)
        sq = p.step("scalar", lambda E: E.activation(strow[:, 1, :], strow[:, 1, :], AF.Sqrt))
        p.wait("vector", "SC", sq)
        vb = p.step("vector", lambda E: E.reciprocal(strow[:, 1, :], strow[:, 1, :]))
        p.wait("tensor", "V", vb)
        for th in range(2):
            sl = slice(th * 512, (th + 1) * 512)
            rps = PS0 if th == 0 else PS1
            p.emit("tensor", lambda E, rps=rps, sl=sl:
                   E.matmul(rps[:, :], ones_row[0:1, :], strow[:, 1, :][:, sl],
                            start=True, stop=True))
            p.step("tensor", lambda E, th=th, sl=sl:
                   E.matmul(PSB[:, th * 512:(th + 1) * 512], ones_row[0:1, :],
                            strow[:, 0, :][:, sl], start=True, stop=True))
        p.wait("vector", "T")
        for dc in range(DC):
            for th in range(2):
                sl = slice(th * 512, (th + 1) * 512)
                mps = PSB[:, th * 512:(th + 1) * 512]
                fn = lambda E, dc=dc, sl=sl, mps=mps: \
                    E.tensor_sub(xln[:, dc, :][:, sl], x_res[:, dc, :][:, sl], mps)
                if dc == DC - 1 and th == 1:
                    p.step("vector", fn)
                else:
                    p.emit("vector", fn)
        p.wait("vector", "V")
        for dc in range(DC):
            for th in range(2):
                sl = slice(th * 512, (th + 1) * 512)
                rps = PS0 if th == 0 else PS1
                p.pend("vector", lambda E, dc=dc, sl=sl, rps=rps:
                       E.tensor_mul(xln[:, dc, :][:, sl], xln[:, dc, :][:, sl], rps[:, :]))
        p.close("vector")
        p.barrier()
        AFA.reset()

    def load_col_chunks(ring, Wap, ic):
        par = ring.next()
        mk = p.dma("gpsimd", lambda E, Wap=Wap, ic=ic, par=par:
                   E.dma_start(out=wring[:, par * 16:(par + 1) * 16, :],
                               in_=Wap[:, ic * 128:(ic + 1) * 128]
                               .rearrange("(a p) c -> p a c", p=128)))
        return par, mk

    def l0_attn():
        layernorm()
        ABF.reset()
        AFA.reset()
        kvp = AFA.alloc(64, 2048)
        kssb = AFA.alloc(128, 16)
        kvbf = ABF.alloc(128, 2048)
        ksD = ABF.alloc3(128, 16, 32)
        rnbf = ABF.alloc(32, 1024)
        selbc = ABF.alloc3(32, 32, 64)
        bocol = AFA.alloc(128, 16)
        kring = ABF.alloc3(128, 2, 1024)
        vring = ABF.alloc3(128, 2, 1024)
        ktm = ABF.alloc(128, 512)
        vtm = ABF.alloc(128, 512)
        rnb = ABF.alloc(64, 512)
        p.dma("sync", lambda E: E.dma_start(
            out=selbc.rearrange("p a b -> p (a b)"), in_=selbc_in[:, :]))
        tmpcv = AFA.alloc(64, 128)
        load_colvec(bo_in, 16, bocol, tmpcv)
        p.step("vector", lambda E: E.memset(ksD.rearrange("p a b -> p (a b)"), 0.0))
        ring = Ring(p)

        ks_first = {0: True, 1: True}
        n_ks = 0
        ev_guard = {"k": {}, "v": {}}
        kv_evac_g = 0
        tp_guard = [0]
        slotg = {0: 0, 1: 0}
        for ic in range(16):
            for which, Wap, dst, relu in (("k", Wk, kring, True), ("v", Wv, vring, False)):
                par, ld = load_col_chunks(ring, Wap, ic)
                psl = PSB[:, 0:1024] if which == "k" else PSB[:, 1024:2048]
                p.wait_tok("tensor", ld)
                if ic >= 1:
                    eng, mk = ev_guard[which][ic - 1]
                    p.wait("tensor", eng, mk)
                for dc in range(DC):
                    for th in range(2):
                        p.pend("tensor", lambda E, psl=psl, dc=dc, th=th, par=par:
                               E.matmul(psl[:, th * 512:(th + 1) * 512],
                                        wring[:, par * 16 + dc, :],
                                        xln[:, dc, th * 512:(th + 1) * 512],
                                        start=(dc == 0), stop=(dc == DC - 1)))
                mm = p.close("tensor")
                ring.consumed(par, mm)
                slot = ic % 2
                if relu:
                    p.wait("scalar", "T", mm)
                    p.wait("scalar", "T", slotg[slot])
                    ev = p.step("scalar", lambda E, psl=psl, dst=dst, slot=slot:
                                E.activation(dst[:, slot, :], psl[:, :], AF.Relu))
                    ev_guard[which][ic] = ("SC", ev)
                else:
                    p.wait("vector", "T", mm)
                    p.wait("vector", "T", slotg[slot])
                    ev = p.step("vector", lambda E, psl=psl, dst=dst, slot=slot:
                                E.tensor_copy(dst[:, slot, :], psl[:, :]))
                    ev_guard[which][ic] = ("V", ev)
            slot = ic % 2
            p.wait("tensor", "SC", ev_guard["k"][ic][1])
            p.wait("tensor", "V", ev_guard["v"][ic][1])
            for hp in range(2):
                h = 2 * ic + hp
                if tp_guard[0]:
                    p.wait("tensor", "V", tp_guard[0])
                idsl = ident_bf[64 * hp:64 * hp + 64, 64 * hp:64 * hp + 64]
                for tt in range(8):
                    p.pend("tensor", lambda E, hp=hp, tt=tt, slot=slot, idsl=idsl:
                           E.transpose(PT[:, tt * 64:(tt + 1) * 64],
                                       kring[64 * hp:64 * hp + 64, slot,
                                             tt * 128:(tt + 1) * 128],
                                       idsl))
                    p.pend("tensor", lambda E, hp=hp, tt=tt, slot=slot, idsl=idsl:
                           E.transpose(PT[:, 512 + tt * 64:512 + (tt + 1) * 64],
                                       vring[64 * hp:64 * hp + 64, slot,
                                             tt * 128:(tt + 1) * 128],
                                       idsl))
                td = p.close("tensor")
                p.wait("vector", "T", td)
                if kv_evac_g:
                    p.wait("vector", "T", kv_evac_g)
                evt = 0
                for half, dsttm in ((0, ktm), (1, vtm)):
                    evt = p.step("vector", lambda E, dsttm=dsttm, half=half:
                                 E.tensor_copy(dsttm, PT[:, half * 512:(half + 1) * 512]))
                tp_guard[0] = evt
                p.wait("tensor", "V", evt)
                for tt in range(8):
                    st = (tt == 0 and hp == 0)
                    sp = (tt == 7 and hp == 1)
                    p.pend("tensor", lambda E, tt=tt, hp=hp, st=st, sp=sp:
                           E.matmul(PS2[0:64, hp * 64:(hp + 1) * 64],
                                    ktm[:, tt * 64:(tt + 1) * 64],
                                    vtm[:, tt * 64:(tt + 1) * 64],
                                    start=st, stop=sp))
                for tt in range(8):
                    n_ks += 1
                    stk = ks_first[hp]
                    ks_first[hp] = False
                    spk = (ic == 15 and tt == 7)
                    ksdst = PS0[0:64, ic:ic + 1] if hp == 0 else PS1[64:128, ic:ic + 1]
                    p.pend("tensor", lambda E, tt=tt, ksdst=ksdst, st=stk, sp=spk:
                           E.matmul(ksdst,
                                    ktm[:, tt * 64:(tt + 1) * 64],
                                    ones_colb[0:128, :],
                                    start=st, stop=sp))
                kv_evac_g = p.close("tensor")
            slotg[slot] = kv_evac_g
            p.wait("vector", "T", kv_evac_g)
            p.step("vector", lambda E, ic=ic:
                   E.tensor_copy(kvp[:, ic * 128:(ic + 1) * 128], PS2[0:64, 0:128]))
        p.step("vector", lambda E: E.tensor_copy(kssb[0:64, :], PS0[0:64, 0:16]))
        evks = p.step("vector", lambda E: E.tensor_copy(kssb[64:128, :], PS1[64:128, 0:16]))
        p.wait("sync", "V", evks)
        p.dma("sync", lambda E: E.dma_start(out=cc_kv_in[0:64, 0:2048], in_=kvp))
        d2 = p.dma("sync", lambda E: E.dma_start(out=cc_kv_in[:, 2048:2064], in_=kssb))
        p.wait_dma_all("gpsimd", "sync")
        cc = p.inc("gpsimd", "CC", lambda E:
                   E.collective_compute("AllReduce", ALU.add,
                                        replica_groups=[[0, 1], [2, 3], [4, 5], [6, 7]],
                                        ins=[cc_kv_in[:, :]], outs=[cc_kv_out[:, :]]))
        qfm = bufA[:, :].rearrange("p (a b) -> p a b", a=16)
        qev = {}
        for ic in range(16):
            par, ld = load_col_chunks(ring, Wq, ic)
            psl = PSB[:, 0:1024] if ic % 2 == 0 else PSB[:, 1024:2048]
            p.wait_tok("tensor", ld)
            if ic >= 2:
                p.wait("tensor", "SC", qev[ic - 2])
            for dc in range(DC):
                for th in range(2):
                    p.pend("tensor", lambda E, psl=psl, dc=dc, th=th, par=par:
                           E.matmul(psl[:, th * 512:(th + 1) * 512],
                                    wring[:, par * 16 + dc, :],
                                    xln[:, dc, th * 512:(th + 1) * 512],
                                    start=(dc == 0), stop=(dc == DC - 1)))
            mq = p.close("tensor")
            ring.consumed(par, mq)
            p.wait("scalar", "T", mq)
            qev[ic] = p.step("scalar", lambda E, psl=psl, ic=ic:
                             E.activation(qfm[:, ic, :], psl[:, :], AF.Relu))
        p.wait("gpsimd", "CC", cc)
        dkv = p.dma("gpsimd", lambda E: E.dma_start(out=kvbf[0:64, :],
                                                    in_=cc_kv_out[0:64, 0:2048]))
        dkv2 = p.dma("gpsimd", lambda E: E.dma_start(out=kvbf[64:128, :],
                                                     in_=cc_kv_out[0:64, 0:2048]))
        p.wait("sync", "CC", cc)
        dks = p.dma("sync", lambda E: E.dma_start(out=kssb, in_=cc_kv_out[:, 2048:2064]))
        p.wait_tok("vector", dks)
        p.wait_tok("vector", dkv)
        p.wait_tok("vector", dkv2)
        for dc in range(DC):
            p.pend("vector", lambda E, dc=dc:
                   E.tensor_copy(ksD[0:64, dc, 2 * dc:2 * dc + 1], kssb[0:64, dc:dc + 1]))
            p.pend("vector", lambda E, dc=dc:
                   E.tensor_copy(ksD[64:128, dc, 2 * dc + 1:2 * dc + 2],
                                 kssb[64:128, dc:dc + 1]))
        evD = p.close("vector")
        p.wait("tensor", "V", evD)
        p.wait("tensor", "SC")
        for th in range(2):
            for dc in range(DC):
                p.pend("tensor", lambda E, th=th, dc=dc:
                       E.matmul(PSB[0:32, th * 512:(th + 1) * 512], ksD[:, dc, :],
                                qfm[:, dc, th * 512:(th + 1) * 512],
                                start=(dc == 0), stop=(dc == DC - 1)))
        nm = p.close("tensor")
        p.wait("vector", "T", nm)
        p.step("vector", lambda E: E.tensor_scalar_max(PSB[0:32, 0:1024],
                                                       PSB[0:32, 0:1024], 1e-6))
        p.wait("vector", "V")
        evn = p.step("vector", lp(lambda E: E.reciprocal(rnbf, PSB[0:32, 0:1024])))
        p.wait("tensor", "V", evn)
        ao = bufB[:, :].rearrange("p (a b) -> p a b", a=16)
        qkv_guard = {0: 0, 1: 0}
        rnb_guard = 0
        for h in range(32):
            hp = h % 2
            ic = h // 2
            ps = PS0 if h % 2 == 0 else PS1
            for th in range(2):
                if rnb_guard:
                    p.wait("tensor", "V", rnb_guard)
                mb = p.step("tensor", lambda E, h=h, th=th:
                            E.matmul(PS2[0:64, :], selbc[:, h, :],
                                     rnbf[:, th * 512:(th + 1) * 512],
                                     start=True, stop=True))
                p.wait("vector", "T", mb)
                ev = p.step("vector", lambda E, th=th:
                            E.tensor_copy(rnb, PS2[0:64, :]))
                p.wait("tensor", "V", ev)
                p.wait("vector", "V", ev)
                if qkv_guard[h % 2]:
                    p.wait("tensor", "V", qkv_guard[h % 2])
                mq2 = p.step("tensor", lambda E, h=h, hp=hp, ic=ic, th=th, ps=ps:
                             E.matmul(ps[0:64, :] if th == 0 else ps[64:128, :],
                                      kvbf[64 * hp:64 * hp + 64, h * 64:(h + 1) * 64],
                                      qfm[64 * hp:64 * hp + 64, ic,
                                          th * 512:(th + 1) * 512],
                                      start=True, stop=True))
                p.wait("vector", "T", mq2)
                src_ = ps[0:64, :] if th == 0 else ps[64:128, :]
                ev2 = p.step("vector", lambda E, src_=src_, hp=hp, ic=ic, th=th:
                             E.tensor_mul(ao[64 * hp:64 * hp + 64, ic,
                                             th * 512:(th + 1) * 512],
                                          src_, rnb))
                qkv_guard[h % 2] = ev2
                rnb_guard = ev2
        aog = p.mark("V")
        p.wait("tensor", "V", aog)
        og = {}
        for dc in range(DC):
            par, ldo = load_col_chunks(ring, Wo, dc)
            psl = PSB[:, 0:1024] if dc % 2 == 0 else PSB[:, 1024:2048]
            p.wait_tok("tensor", ldo)
            if dc >= 2:
                p.wait("tensor", "V", og[dc - 2])
            for ic in range(16):
                for th in range(2):
                    p.pend("tensor", lambda E, psl=psl, ic=ic, th=th, par=par:
                           E.matmul(psl[:, th * 512:(th + 1) * 512],
                                    wring[:, par * 16 + ic, :],
                                    ao[:, ic, th * 512:(th + 1) * 512],
                                    start=(ic == 0), stop=(ic == DC - 1)))
            mo = p.close("tensor")
            ring.consumed(par, mo)
            p.wait("vector", "T", mo)
            for th in range(2):
                og[dc] = p.step("vector", lambda E, psl=psl, dc=dc, th=th:
                                E.scalar_tensor_tensor(
                                    x_res[:, dc, th * 512:(th + 1) * 512],
                                    psl[:, th * 512:(th + 1) * 512],
                                    bocol[:, dc:dc + 1],
                                    x_res[:, dc, th * 512:(th + 1) * 512],
                                    ALU.add, ALU.add))
        p.barrier()
        ABF.reset()
        AFA.reset()

    def l0_ffn():
        layernorm()
        ABF.reset()
        AFA.reset()
        uscol = AFA.alloc(128, 64)
        dscol = AFA.alloc(128, 16)
        os_sb = AFA.alloc(1, 1)
        tmpcv = AFA.alloc(64, 128)
        load_colvec(up_s, 64, uscol, tmpcv)
        load_colvec(dn_s, 16, dscol, tmpcv)
        dos = p.dma("sync", lambda E: E.dma_start(out=os_sb, in_=out_s[0:1]))
        p.wait_tok("tensor", dos)
        p.wait("tensor", "V", p.cnt["V"])
        mos = p.step("tensor", lambda E:
                     E.matmul(PS2[:, 0:1], ones_row[0:1, :], os_sb,
                              start=True, stop=True))
        p.wait("vector", "T", mos)
        p.wait_dma_all("vector", "sync")
        p.step("vector", lambda E:
               E.tensor_scalar_mul(dscol, dscol, PS2[:, 0:1]))

        hA = bufA[:, :].rearrange("p (a b) -> p a b", b=512)
        hB = bufB[:, :].rearrange("p (a b) -> p a b", b=512)

        def h_slot(fc):
            return (hA if fc < 32 else hB)[:, fc % 32, :]

        natr = ABF.alloc3(128, 2, 2048)

        def prep_chunks(par, src_dram, n128, cache_out):
            nslot = natr[:, par, 0:n128 * 128]
            ldn = p.dma("gpsimd", lambda E, nslot=nslot, src_dram=src_dram:
                        E.dma_start(out=nslot, in_=src_dram))
            p.wait_tok("vector", ldn)
            p.step("vector", lambda E, nslot=nslot:
                   E.tensor_scalar(nslot, nslot, 0.0, 2.0, ALU.is_gt, ALU.mult))
            p.wait("vector", "V")
            sg = p.step("vector", lambda E, nslot=nslot:
                        E.tensor_scalar_add(nslot, nslot, -1.0))
            p.wait("tensor", "V", sg)
            for half in range((n128 + 7) // 8):
                jn = min(8, n128 - half * 8)
                p.wait("tensor", "V", ptg[0])
                for j in range(jn):
                    p.pend("tensor", lambda E, nslot=nslot, half=half, j=j:
                           E.transpose(PT[:, j * 128:(j + 1) * 128],
                                       nslot[:, (8 * half + j) * 128:(8 * half + j + 1) * 128],
                                       ident_bf[:, :]))
                tdu = p.close("tensor")
                p.wait("vector", "T", tdu)
                ptg[0] = p.step(
                    "vector", lambda E, par=par, half=half, jn=jn:
                    E.tensor_copy(wring[:, par * 16 + half * 8:par * 16 + half * 8 + jn, :]
                                  .rearrange("p a b -> p (a b)"), PT[:, 0:jn * 128]))
            wvm = p.mark("V")
            p.wait("sync", "V", wvm)
            cd = p.dma("sync", lambda E, cache_out=cache_out, par=par:
                       E.dma_start(out=cache_out,
                                   in_=wring[:, par * 16:par * 16 + 16, :]))
            return p.mark("T"), cd, wvm

        ring = Ring(p)
        ptg = [0]
        cache_rd = {}
        evh = {}
        dvg = 0
        for th in range(2):
            build = (th == 0)
            natg = [0, 0]
            for fc in range(FC):
                if build:
                    parn = fc % 2
                    if natg[parn]:
                        p.wait("gpsimd", "T", natg[parn])
                    par = ring.next()
                    if cache_rd.get(par):
                        p.wait_tok("gpsimd", cache_rd[par])
                    tm, cd, wvm = prep_chunks(
                        par,
                        up_w[fc * 128:(fc + 1) * 128, :],
                        16,
                        upTc[fc, :, :, :].transpose([1, 0, 2]))
                    natg[parn] = p.mark("T")
                    cache_rd[par] = cd
                    p.wait("tensor", "V", wvm)
                else:
                    par = ring.next()
                    if cache_rd.get(par):
                        p.wait_tok("gpsimd", cache_rd[par])
                        cache_rd[par] = 0
                    ld = p.dma("gpsimd", lambda E, fc=fc, par=par:
                               E.dma_start(out=wring[:, par * 16:(par + 1) * 16, :],
                                           in_=upTc[fc, :, :, :].transpose([1, 0, 2])))
                    p.wait_tok("tensor", ld)
                psl = [PS0, PS1][fc % 2]
                if fc >= 2:
                    p.wait("tensor", "SC", evh[fc - 2])
                for dc in range(DC):
                    p.pend("tensor", lambda E, psl=psl, dc=dc, th=th, par=par:
                           E.matmul(psl[:, :], wring[:, par * 16 + dc, :],
                                    xln[:, dc, th * 512:(th + 1) * 512],
                                    start=(dc == 0), stop=(dc == DC - 1)))
                mh = p.close("tensor")
                ring.consumed(par, mh)
                p.wait("scalar", "T", mh)
                evh[fc] = p.step("scalar", lambda E, psl=psl, fc=fc:
                                 E.activation(h_slot(fc), psl[:, :], AF.Relu,
                                              scale=uscol[:, fc:fc + 1]))
            p.wait("tensor", "SC")
            for dc in range(DC):
                for g in range(4):
                    if build:
                        parn = (dc * 4 + g) % 2
                        if natg[parn]:
                            p.wait("gpsimd", "T", natg[parn])
                        par = ring.next()
                        if cache_rd.get(par):
                            p.wait_tok("gpsimd", cache_rd[par])
                        tm, cd, wvm = prep_chunks(
                            par,
                            dn_w[dc * 128:(dc + 1) * 128, g * 2048:(g + 1) * 2048],
                            16,
                            dnTc[dc, g * 16:(g + 1) * 16, :, :].transpose([1, 0, 2]))
                        natg[parn] = p.mark("T")
                        cache_rd[par] = cd
                        p.wait("tensor", "V", wvm)
                    else:
                        par = ring.next()
                        if cache_rd.get(par):
                            p.wait_tok("gpsimd", cache_rd[par])
                            cache_rd[par] = 0
                        ld = p.dma("gpsimd", lambda E, dc=dc, g=g, par=par:
                                   E.dma_start(out=wring[:, par * 16:(par + 1) * 16, :],
                                               in_=dnTc[dc, g * 16:(g + 1) * 16, :, :]
                                               .transpose([1, 0, 2])))
                        p.wait_tok("tensor", ld)
                    if g == 0 and dvg:
                        p.wait("tensor", "V", dvg)
                    for j in range(16):
                        fc = g * 16 + j
                        p.pend("tensor", lambda E, fc=fc, j=j, par=par:
                               E.matmul(PS2[:, :], wring[:, par * 16 + j, :], h_slot(fc),
                                        start=(fc == 0), stop=(fc == FC - 1)))
                    mdg = p.close("tensor")
                    ring.consumed(par, mdg)
                md = p.mark("T")
                p.wait("vector", "T", md)
                dvg = p.step("vector", lambda E, dc=dc, th=th:
                             E.scalar_tensor_tensor(x_res[:, dc, th * 512:(th + 1) * 512],
                                                    PS2[:, :], dscol[:, dc:dc + 1],
                                                    x_res[:, dc, th * 512:(th + 1) * 512],
                                                    ALU.mult, ALU.add))
        p.barrier()
        ABF.reset()
        AFA.reset()

    def l1_attn():
        layernorm()
        ABF.reset()
        AFA.reset()
        res_kT = ABF.alloc(64, 2048)
        res_vT = ABF.alloc(64, 2048)
        res_q = ABF.alloc(64, 1024)
        basTo = ABF.alloc3(128, 16, 64)
        sums = AFA.alloc(128, 8)
        recip = AFA.alloc(128, 8)
        negmax = AFA.alloc(128, 1)
        tmpa = AFA.alloc(128, 512)
        tmpp = AFA.alloc(128, 512)
        mark0 = ABF.off
        wnat = ABF.alloc3(128, 8, 64)
        basn = ABF.alloc3(64, 16, 128)
        basTt = ABF.alloc3(128, 16, 64)
        pmh = ABF.alloc(128, 1024)
        exr = ABF.alloc(128, 2048)
        wT = bufB[0:64, 8192:16384].rearrange("p (j a c) -> p j a c", j=4, a=16)
        AT = bufB[:, 0:8192].rearrange("p (a c) -> p a c", a=16)
        qfm = bufA[:, :].rearrange("p (a b) -> p a b", a=16)
        ao = qfm
        for j, nm in enumerate("qkvo"):
            scl = (1.0 / float(np.sqrt(128.0))) if nm == "q" else 1.0
            for half in range(2):
                rows = slice(half * 1024, (half + 1) * 1024)
                p.wait("sync", "V", p.cnt["V"])
                lda = p.dma("sync", lambda E, nm=nm, rows=rows: E.dma_start(
                    out=tmpa.rearrange("p (a c) -> p a c", a=8),
                    in_=amp_in[nm][rows, :].rearrange("(a p) c -> p a c", p=128)))
                ldp = p.dma("sync", lambda E, nm=nm, rows=rows: E.dma_start(
                    out=tmpp.rearrange("p (a c) -> p a c", a=8),
                    in_=phase_in[nm][rows, :].rearrange("(a p) c -> p a c", p=128)))
                p.wait_tok("scalar", ldp)
                p.wait("scalar", "V", p.cnt["V"])
                sn = p.step("scalar", lambda E: E.activation(
                    tmpp, tmpp, AF.Sin, bias=pio2[:, :]))
                p.wait_tok("vector", lda)
                p.wait("scalar", "SC", sn)
                p.wait("vector", "SC", sn)
                p.wait("vector", "T", p.cnt["T"])
                wv = p.step("vector", lp(lambda E, scl=scl: E.scalar_tensor_tensor(
                    wnat.rearrange("p a c -> p (a c)"), tmpp, scl, tmpa,
                    ALU.mult, ALU.mult)))
                p.wait("tensor", "V", wv)
                for d8 in range(8):
                    p.pend("tensor", lambda E, d8=d8:
                           E.transpose(PT[0:64, d8 * 128:(d8 + 1) * 128],
                                       wnat[:, d8, :], ident_bf[:, :]))
                tw = p.close("tensor")
                p.wait("vector", "T", tw)
                p.step("vector", lambda E, j=j, half=half:
                       E.tensor_copy(wT[0:64, j, half * 8:(half + 1) * 8, :]
                                     .rearrange("p a c -> p (a c)"), PT[0:64, :]))
        kv_res_mark = {}
        for j, nm in enumerate("qkvo"):
            p.wait("gpsimd", "T", p.cnt["T"])
            ldb = p.dma("gpsimd", lambda E, nm=nm: E.dma_start(
                out=basn.rearrange("p a c -> p (a c)"), in_=basis_in[nm][:, :]))
            p.wait_tok("tensor", ldb)
            p.wait("tensor", "V", p.cnt["V"])
            for dc in range(DC):
                p.pend("tensor", lambda E, dc=dc:
                       E.transpose(PT[:, dc * 64:(dc + 1) * 64],
                                   basn[:, dc, :], ident_bf[0:64, 0:64]))
            tb = p.close("tensor")
            p.wait("vector", "T", tb)
            dstT = basTo if nm == "o" else basTt
            bv = p.step("vector", lambda E, dstT=dstT:
                        E.tensor_copy(dstT.rearrange("p a c -> p (a c)"), PT[:, :]))
            if nm == "o":
                continue
            p.wait("tensor", "V", bv)
            for th in range(2):
                psl = PSB[:, th * 512:(th + 1) * 512]
                for dc in range(DC):
                    if nm == "q":
                        out = psl[0:64, :]
                    elif nm == "k":
                        out = psl[0:64, :]
                    else:
                        out = psl[64:128, :]
                    p.pend("tensor", lambda E, out=out, dc=dc, th=th:
                           E.matmul(out, basTt[:, dc, :],
                                    xln[:, dc, th * 512:(th + 1) * 512],
                                    start=(dc == 0), stop=(dc == DC - 1)))
                kv_res_mark[(nm, th)] = p.close("tensor")
            if nm == "q":
                m = kv_res_mark[("q", 1)]
                p.wait("vector", "T", m)
                p.step("vector", lambda E:
                       E.tensor_copy(res_q[:, 0:512], PSB[0:64, 0:512]))
                qv = p.step("vector", lambda E:
                            E.tensor_copy(res_q[:, 512:1024], PSB[0:64, 512:1024]))
                p.wait("tensor", "V", qv)
        ldm0 = p.dma("sync", lambda E: E.dma_start(out=pmh, in_=pmask_in[:, 0:1024]))
        m = kv_res_mark[("v", 1)]
        p.wait("vector", "T", m)
        p.step("vector", lambda E: E.tensor_copy(exr[:, 0:1024], PSB[:, 0:1024]))
        cv = p.step("vector", lambda E: E.tensor_copy(exr[:, 1024:2048], PSB[:, 0:1024]))
        p.wait_tok("vector", ldm0)
        p.wait("vector", "V", cv)
        m0 = p.step("vector", lambda E: E.tensor_mul(exr[:, 0:1024], exr[:, 0:1024], pmh))
        p.wait("sync", "V", m0)
        ldm1 = p.dma("sync", lambda E: E.dma_start(out=pmh, in_=pmask_in[:, 1024:2048]))
        p.wait_tok("vector", ldm1)
        p.wait("vector", "V", p.cnt["V"])
        mv = p.step("vector", lambda E:
                    E.tensor_mul(exr[:, 1024:2048], exr[:, 1024:2048], pmh))
        p.wait("sync", "V", mv)
        dex = p.dma("sync", lambda E: E.dma_start(out=cc_res_in[:, :], in_=exr))
        p.wait_dma_all("gpsimd", "sync")
        ccr = p.inc("gpsimd", "CC", lambda E:
                    E.collective_compute("AllReduce", ALU.add,
                                         replica_groups=[[0, 1], [2, 3], [4, 5], [6, 7]],
                                         ins=[cc_res_in[:, :]], outs=[cc_res_out[:, :]]))
        qg = {0: 0, 1: 0}
        for ic in range(16):
            ps = PS0 if ic % 2 == 0 else PS1
            for th in range(2):
                if qg[ic % 2]:
                    p.wait("tensor", "V", qg[ic % 2])
                mqe = p.step("tensor", lambda E, ic=ic, th=th, ps=ps:
                             E.matmul(ps[:, :],
                                      wT[0:64, 0, ic, :],
                                      res_q[:, th * 512:(th + 1) * 512],
                                      start=True, stop=True))
                p.wait("vector", "T", mqe)
                qg[ic % 2] = p.step("vector", lambda E, ic=ic, th=th, ps=ps:
                                    E.tensor_copy(qfm[:, ic, th * 512:(th + 1) * 512],
                                                  ps[:, :]))
        p.wait("sync", "CC", ccr)
        dk = p.dma("sync", lambda E: E.dma_start(out=res_kT, in_=cc_res_out[0:64, :]))
        dv = p.dma("sync", lambda E: E.dma_start(out=res_vT, in_=cc_res_out[64:128, :]))
        p.wait_tok("tensor", dk)
        p.wait_tok("tensor", dv)
        p.wait("tensor", "V", p.cnt["V"])
        ABF.off = mark0
        Kh = ABF.alloc(128, 2048)
        Vtok = ABF.alloc3(128, 16, 128)
        AT = bufB[:, 0:4096].rearrange("p (a c) -> p a c", a=16)
        Aexp2 = bufB[:, 4096:8192].rearrange("p (r c) -> p r c", r=2)
        negmax2 = negmax
        negmaxB = AFA.alloc(128, 1)
        pump_state = {"i": 0}
        pump_guard = {}

        def pump_step():
            i = pump_state["i"]
            if i >= 128:
                return
            pump_state["i"] += 1
            par = i % 2
            if pump_guard.get(par):
                p.wait_tok("gpsimd", pump_guard[par])
            if i < 64:
                fc = i
                tk = p.dma("gpsimd", lambda E, fc=fc, par=par: E.dma_start(
                    out=wring[:, par * 16:(par + 1) * 16, :],
                    in_=W1[:, fc * 128:(fc + 1) * 128]
                    .rearrange("(a p) c -> p a c", p=128)))
                p.wait_tok("sync", tk)
                pump_guard[par] = p.dma("sync", lambda E, fc=fc, par=par: E.dma_start(
                    out=w1c[fc, :, :, :].transpose([1, 0, 2]),
                    in_=wring[:, par * 16:(par + 1) * 16, :]))
            else:
                k = i - 64
                dc, q = k // 4, k % 4
                tk = p.dma("gpsimd", lambda E, dc=dc, q=q, par=par: E.dma_start(
                    out=wring[:, par * 16:(par + 1) * 16, :],
                    in_=W2[q * 2048:(q + 1) * 2048, dc * 128:(dc + 1) * 128]
                    .rearrange("(a p) c -> p a c", p=128)))
                p.wait_tok("sync", tk)
                pump_guard[par] = p.dma("sync", lambda E, dc=dc, q=q, par=par:
                                        E.dma_start(
                                            out=w2c[dc, q * 16:(q + 1) * 16, :, :]
                                            .transpose([1, 0, 2]),
                                            in_=wring[:, par * 16:(par + 1) * 16, :]))

        khg = [0]
        scg = {0: 0, 1: 0}
        mxg = {0: 0, 1: 0}
        nrm = {0: 0, 1: 0}
        avg = {0: 0, 1: 0}
        av_done = [0]
        tp_toks = {}
        ptev = [0]
        trq = {0: 0, 1: 0}
        for h in range(16):
            for _ in range(8):
                pump_step()
            for k4 in range(4):
                ps = PS0 if k4 % 2 == 0 else PS1
                if avg[k4 % 2]:
                    p.wait("tensor", "V", avg[k4 % 2])
                mk = p.step("tensor", lambda E, h=h, k4=k4, ps=ps:
                            E.matmul(ps[:, :], wT[0:64, 1, h, :],
                                     res_kT[:, k4 * 512:(k4 + 1) * 512],
                                     start=True, stop=True))
                p.wait("vector", "T", mk)
                avg[k4 % 2] = p.step("vector", lambda E, h=h, k4=k4, ps=ps:
                                     E.tensor_copy(Kh[:, k4 * 512:(k4 + 1) * 512],
                                                   ps[:, :]))
            khg[0] = avg[1]
            for grp in range(4):
                if grp > 0 or h > 0:
                    p.wait("tensor", "V", p.cnt["V"])
                for jj in range(4):
                    kc = grp * 4 + jj
                    p.pend("tensor", lambda E, h=h, kc=kc, jj=jj:
                           E.matmul(PS2[:, jj * 128:(jj + 1) * 128],
                                    res_vT[:, kc * 128:(kc + 1) * 128],
                                    wT[0:64, 2, h, :],
                                    start=True, stop=True))
                mv2 = p.close("tensor")
                p.wait("vector", "T", mv2)
                p.step("vector", lambda E, grp=grp:
                       E.tensor_copy(Vtok[:, grp * 4:(grp + 1) * 4, :]
                                     .rearrange("p a c -> p (a c)"), PS2[:, :]))
            vtg = p.mark("V")
            p.wait("tensor", "V", vtg)
            for qq in range(4):
                for qx in range(2):
                    qt = qq * 2 + qx
                    ar = qt % 2
                    nmb = negmax if ar == 0 else negmaxB
                    for sl_ in (0, 1):
                        if scg[sl_]:
                            p.wait("tensor", "SC", scg[sl_])
                        if mxg[sl_]:
                            p.wait("tensor", "V", mxg[sl_])
                    for k4 in range(4):
                        p.pend("tensor", lambda E, h=h, qt=qt, k4=k4:
                               E.matmul(PSB[:, k4 * 512:(k4 + 1) * 512],
                                        qfm[:, h, qt * 128:(qt + 1) * 128],
                                        Kh[:, k4 * 512:(k4 + 1) * 512],
                                        start=True, stop=True))
                    sm = p.close("tensor")
                    p.wait("vector", "T", sm)
                    mxg[ar] = p.step("vector", lambda E, nmb=nmb: E.reduce_max(
                        nmb, PSB[:, :], axis=AX.X, negate=True))
                    p.wait("scalar", "V", mxg[ar])
                    if trq[ar]:
                        p.wait("scalar", "T", trq[ar])
                    ex = p.step("scalar", lambda E, qt=qt, ar=ar, nmb=nmb:
                                E.activation(Aexp2[:, ar, :], PSB[:, :], AF.Exp,
                                             bias=nmb,
                                             accum_out=sums[:, qt:qt + 1]))
                    scg[ar] = ex
                    p.wait("vector", "SC", ex)
                    p.step("vector", lambda E, qt=qt:
                           E.reciprocal(recip[:, qt:qt + 1], sums[:, qt:qt + 1]))
                    p.wait("vector", "V", p.cnt["V"])
                    an = p.step("vector", lp(lambda E, qt=qt, ar=ar:
                                E.tensor_scalar_mul(Aexp2[:, ar, :], Aexp2[:, ar, :],
                                                    recip[:, qt:qt + 1])))
                    nrm[ar] = an
                    evs = []
                    tr = 0
                    for r in range(2):
                        p.wait("tensor", "V", an)
                        if ptev[0]:
                            p.wait("tensor", "V", ptev[0])
                        for j in range(8):
                            kc = r * 8 + j
                            p.pend("tensor", lambda E, kc=kc, j=j, ar=ar:
                                   E.transpose(PT[:, j * 128:(j + 1) * 128],
                                               Aexp2[:, ar,
                                                     kc * 128:(kc + 1) * 128],
                                               ident_bf[:, :]))
                        tr = p.close("tensor")
                        p.wait("vector", "T", tr)
                        if av_done[0]:
                            p.wait("vector", "T", av_done[0])
                        ev = p.step("vector", lambda E, r=r, qx=qx:
                                    E.tensor_copy(
                                        AT[:, r * 8:(r + 1) * 8,
                                           qx * 128:(qx + 1) * 128],
                                        PT[:, :].rearrange(
                                            "p (a c) -> p a c", a=8)))
                        ptev[0] = ev
                        evs.append(ev)
                    trq[ar] = tr
                    tp_toks[qt] = evs
                for ev in tp_toks.get(qq * 2, ()):
                    p.wait("tensor", "V", ev)
                for ev in tp_toks.get(qq * 2 + 1, ()):
                    p.wait("tensor", "V", ev)
                ps = PS0 if qq % 2 == 0 else PS1
                if avg[qq % 2]:
                    p.wait("tensor", "V", avg[qq % 2])
                for kc in range(16):
                    p.pend("tensor", lambda E, kc=kc, ps=ps:
                           E.matmul(ps[:, 0:256], Vtok[:, kc, :], AT[:, kc, :],
                                    start=(kc == 0), stop=(kc == 15)))
                av = p.close("tensor")
                av_done[0] = av
                p.wait("vector", "T", av)
                avg[qq % 2] = p.step("vector", lambda E, h=h, qq=qq, ps=ps:
                                     E.tensor_copy(ao[:, h, qq * 256:(qq + 1) * 256],
                                                   ps[:, 0:256]))
        reso = res_q
        p.wait("tensor", "V", p.cnt["V"])
        for th in range(2):
            for ic in range(16):
                p.pend("tensor", lambda E, ic=ic, th=th:
                       E.matmul(PSB[0:64, th * 512:(th + 1) * 512], basTo[:, ic, :],
                                ao[:, ic, th * 512:(th + 1) * 512],
                                start=(ic == 0), stop=(ic == DC - 1)))
            ro = p.close("tensor")
            p.wait("vector", "T", ro)
            rv = p.step("vector", lambda E, th=th:
                        E.tensor_copy(reso[:, th * 512:(th + 1) * 512],
                                      PSB[0:64, th * 512:(th + 1) * 512]))
            p.wait("tensor", "V", rv)
        fg = {0: 0, 1: 0}
        for dc in range(DC):
            ps = PS0 if dc % 2 == 0 else PS1
            for th in range(2):
                if fg[dc % 2]:
                    p.wait("tensor", "V", fg[dc % 2])
                fm = p.step("tensor", lambda E, dc=dc, th=th, ps=ps:
                            E.matmul(ps[:, :], wT[0:64, 3, dc, :],
                                     reso[:, th * 512:(th + 1) * 512],
                                     start=True, stop=True))
                p.wait("vector", "T", fm)
                fg[dc % 2] = p.step("vector", lambda E, dc=dc, th=th, ps=ps:
                                    E.tensor_add(x_res[:, dc, th * 512:(th + 1) * 512],
                                                 x_res[:, dc, th * 512:(th + 1) * 512],
                                                 ps[:, :]))
        p.barrier()
        ABF.reset()
        AFA.reset()

    def l1_ffn():
        layernorm()
        ABF.reset()
        AFA.reset()
        b1col = AFA.alloc(128, 64)
        lgrow = AFA.alloc3(8, 2, 512)
        lgT = AFA.alloc3(128, 8, 8)
        top8 = AFA.alloc(128, 8)
        nm8 = AFA.alloc(128, 1)
        prb = AFA.alloc(128, 8)
        crtmp = AFA.alloc(16, 128)
        tmpcv2 = AFA.alloc(64, 128)
        brcol = AFA.alloc(8, 1)
        a2r = ABF.alloc3(128, 2, 1024)
        wrch = ABF.alloc3(128, 16, 8)
        crt = ABF.alloc3(1, 2, 1024)
        csb = ABF.alloc(1, 1024)
        b2row = ABF.alloc(1, 2048)
        gtmp2 = ABF.alloc3(128, 2, 1024)
        base_r = ABF.alloc3(128, 4, 512)
        t1sb = bufB[0:16, 0:4096].rearrange("p (a c) -> p a c", a=8)
        t2accA = bufB[32:48, 0:2048].rearrange("p (a c) -> p a c", a=4)
        t2accB = bufB[64:80, 0:2048].rearrange("p (a c) -> p a c", a=4)
        b1r = bufB[0:16, 4096:12288].rearrange("p (a b c) -> p a b c", a=8, b=8)
        cbe = bufB[:, 12288:16384].rearrange("p (t a c) -> p t a c", t=2, a=8)
        t2all = bufA[:, 1024:1536]
        b2stk = bufA[:, 1536:1792].rearrange("p (r c) -> p r c", r=2)
        aetA = bufA[:, 2048:4096].rearrange("p (r c) -> p r c", r=2)
        aetB = bufA[:, 4096:6144].rearrange("p (r c) -> p r c", r=2)
        caeq = bufA[:, 6144:10240].rearrange("p (r a c) -> p r a c", r=2, a=8)
        a1T = bufA[:, 10240:12288].rearrange("p (a c) -> p a c", a=16)
        mixR = bufA[:, 12288:16384].rearrange("p (a c) -> p a c", a=8)
        load_colvec(b1_in, 64, b1col, tmpcv2)
        dbr = p.dma("sync", lambda E: E.dma_start(
            out=brcol, in_=br_in.rearrange("(a b) -> a b", b=1)))
        db2 = p.dma("gpsimd", lambda E: E.dma_start(
            out=b2row, in_=b2_in.rearrange("(a b) -> a b", a=1)))
        dwr = p.dma("gpsimd", lambda E: E.dma_start(
            out=wrch[:, :, :], in_=Wr.rearrange("(a p) c -> p a c", p=128)))
        a1nat = bufA[:, 2048:4096]
        da1 = p.dma("gpsimd", lambda E: E.dma_start(
            out=a1nat, in_=A1.rearrange("e r d -> (e r) d")))
        p.wait_tok("tensor", da1)
        for rnd in range(2):
            p.wait("tensor", "V", p.cnt["V"])
            for jj in range(8):
                p.pend("tensor", lambda E, rnd=rnd, jj=jj:
                       E.transpose(PT[:, jj * 128:(jj + 1) * 128],
                                   a1nat.rearrange("p (a c) -> p a c", a=16)
                                   [:, rnd * 8 + jj, :], ident_bf[:, :]))
            ta = p.close("tensor")
            p.wait("vector", "T", ta)
            p.step("vector", lambda E, rnd=rnd:
                   E.tensor_copy(a1T[:, rnd * 8:(rnd + 1) * 8, :]
                                 .rearrange("p a c -> p (a c)"), PT[:, :]))
        bnat = bufA[:, 2048:3072].rearrange("p (a c) -> p a c", a=64)
        stg16 = bufA[0:16, 3072:4096]
        for e in range(8):
            p.wait("gpsimd", "T", p.cnt["T"])
            p.wait("gpsimd", "V", p.cnt["V"])
            p.wait_dma_all("gpsimd", "sync")
            dbn = p.dma("gpsimd", lambda E, e=e: E.dma_start(
                out=bnat.rearrange("p a c -> p (a c)"),
                in_=B1[e, :, :].rearrange("(a p) c -> p a c", p=128)))
            p.wait_tok("tensor", dbn)
            for blk in range(8):
                p.wait("tensor", "V", p.cnt["V"])
                for jj in range(8):
                    p.pend("tensor", lambda E, blk=blk, jj=jj:
                           E.transpose(PT[0:16, jj * 128:(jj + 1) * 128],
                                       bnat[:, blk * 8 + jj, :], ident_bf[:, :]))
                tb1 = p.close("tensor")
                p.wait("vector", "T", tb1)
                p.wait_dma_all("vector", "sync")
                sv = p.step("vector", lambda E: E.tensor_copy(stg16, PT[0:16, :]))
                p.wait("sync", "V", sv)
                p.dma("sync", lambda E, e=e, blk=blk: E.dma_start(
                    out=b1tc[blk, :, e * 1024:(e + 1) * 1024], in_=stg16))
        a2stg = bufA[:, 2048:4096]
        aev1 = bufA[:, 4096:4608]
        aev2 = bufA[:, 4608:5120]
        for half in range(4):
            p.wait("gpsimd", "T", p.cnt["T"])
            p.wait("gpsimd", "V", p.cnt["V"])
            p.wait_dma_all("gpsimd", "sync")
            dan = p.dma("gpsimd", lambda E, half=half: E.dma_start(
                out=a2stg, in_=A2.rearrange("e r f -> (e r) f")
                [:, half * 2048:(half + 1) * 2048]))
            p.wait_tok("tensor", dan)
            for blk in range(2):
                p.wait("tensor", "V", p.cnt["V"])
                for jj in range(8):
                    p.pend("tensor", lambda E, blk=blk, jj=jj:
                           E.transpose(PT[:, jj * 128:(jj + 1) * 128],
                                       a2stg.rearrange("p (a c) -> p a c", a=16)
                                       [:, blk * 8 + jj, :], ident_bf[:, :]))
                ta2 = p.close("tensor")
                p.wait("vector", "T", ta2)
                p.wait_dma_all("vector", "sync")
                sv = p.step("vector", lambda E: E.tensor_copy(aev1, PT[:, 0:512]))
                sv2 = p.step("vector", lambda E: E.tensor_copy(aev2, PT[:, 512:1024]))
                p.wait("sync", "V", sv2)
                for q4 in range(8):
                    fcx = half * 16 + blk * 8 + q4
                    srcb = aev1 if q4 < 4 else aev2
                    p.dma("sync", lambda E, fcx=fcx, srcb=srcb, q4=q4: E.dma_start(
                        out=a2tc[fcx, :, :],
                        in_=srcb.rearrange("p (a c) -> p a c", a=4)[:, q4 % 4, :]))
        b2nat = bufA[:, 2048:2304].rearrange("p (a c) -> p a c", a=16)
        for e in range(8):
            p.wait("gpsimd", "T", p.cnt["T"])
            p.wait("gpsimd", "V", p.cnt["V"])
            p.wait_dma_all("gpsimd", "sync")
            dbn = p.dma("gpsimd", lambda E, e=e: E.dma_start(
                out=b2nat.rearrange("p a c -> p (a c)"),
                in_=B2[e, :, :].rearrange("(a p) c -> p a c", p=128)))
            p.wait_tok("tensor", dbn)
            for blk in range(2):
                p.wait("tensor", "V", p.cnt["V"])
                for jj in range(8):
                    p.pend("tensor", lambda E, blk=blk, jj=jj:
                           E.transpose(PT[0:16, jj * 128:(jj + 1) * 128],
                                       b2nat[:, blk * 8 + jj, :], ident_bf[:, :]))
                tb2 = p.close("tensor")
                p.wait("vector", "T", tb2)
                p.wait_dma_all("vector", "sync")
                sv = p.step("vector", lambda E: E.tensor_copy(stg16, PT[0:16, :]))
                p.wait("sync", "V", sv)
                p.dma("sync", lambda E, e=e, blk=blk: E.dma_start(
                    out=b2tc[blk * 8:(blk + 1) * 8, e, :, :].transpose([1, 0, 2]),
                    in_=stg16.rearrange("p (a c) -> p a c", a=8)))
        p.wait_dma_all("tensor", "sync")
        p.wait_tok("tensor", dwr)
        p.wait_tok("scalar", dbr)
        for th in range(2):
            for dc in range(DC):
                p.pend("tensor", lambda E, th=th, dc=dc:
                       E.matmul(PSB[0:8, th * 512:(th + 1) * 512], wrch[:, dc, :],
                                xln[:, dc, th * 512:(th + 1) * 512],
                                start=(dc == 0), stop=(dc == DC - 1)))
            rm = p.close("tensor")
            p.wait("scalar", "T", rm)
            p.step("scalar", lambda E, th=th: E.activation(
                lgrow[:, th, :], PSB[0:8, th * 512:(th + 1) * 512], AF.Identity,
                bias=brcol))
        p.wait("tensor", "SC")
        p.wait("tensor", "V", p.cnt["V"])
        for tt in range(8):
            p.pend("tensor", lambda E, tt=tt:
                   E.transpose(PS2[:, tt * 8:(tt + 1) * 8],
                               lgrow.rearrange("p a c -> p (a c)")
                               [0:8, tt * 128:(tt + 1) * 128], ident[0:8, 0:8]))
        tl = p.close("tensor")
        p.wait("vector", "T", tl)
        p.step("vector", lambda E: E.tensor_copy(
            lgT.rearrange("p a c -> p (a c)"), PS2[:, 0:64]))
        for tt in range(8):
            p.wait("vector", "V", p.cnt["V"])
            p.step("vector", lambda E, tt=tt: E.reduce_max(
                nm8, lgT[:, tt, :], axis=AX.X, negate=True))
            p.wait("scalar", "V", p.cnt["V"])
            exm = p.step("scalar", lambda E, tt=tt: E.activation(
                prb, lgT[:, tt, :], AF.Exp, bias=nm8, accum_out=nm8))
            p.wait("vector", "SC", exm)
            p.step("vector", lambda E: E.reciprocal(nm8, nm8))
            p.wait("vector", "V", p.cnt["V"])
            p.step("vector", lambda E: E.tensor_scalar_mul(prb, prb, nm8))
            p.wait("vector", "V", p.cnt["V"])
            p.step("vector", lambda E: E.max(top8, prb))
            p.wait("vector", "V", p.cnt["V"])
            p.step("vector", lambda E, tt=tt: E.tensor_scalar(
                lgT[:, tt, :], prb, top8[:, 1:2], None, ALU.is_ge, ALU.bypass))
            p.wait("vector", "V", p.cnt["V"])
            p.step("vector", lambda E, tt=tt: E.tensor_mul(lgT[:, tt, :],
                                                           lgT[:, tt, :], prb))
            p.wait("tensor", "V", p.cnt["V"])
            t1m = p.step("tensor", lambda E, tt=tt:
                         E.transpose(PS2[0:8, 0:128], lgT[:, tt, :], ident[:, :]))
            p.wait("vector", "T", t1m)
            p.wait_dma_all("vector", "sync")
            sv = p.step("vector", lambda E: E.tensor_copy(crtmp[0:8, :],
                                                          PS2[0:8, 0:128]))
            p.wait("sync", "V", sv)
            p.dma("sync", lambda E, tt=tt: E.dma_start(
                out=crows_d[0:8, tt * 128:(tt + 1) * 128], in_=crtmp[0:8, :]))
        p.wait_dma_all("gpsimd", "sync")
        first = [True]
        for e in range(8):
            rr = e % 2
            p.wait("gpsimd", "V", p.cnt["V"])
            dcr = p.dma("gpsimd", lambda E, e=e, rr=rr: E.dma_start(
                out=crt[:, rr, :], in_=crows_d[e:e + 1, :]))
            p.wait_tok("vector", dcr)
            p.wait("vector", "V", p.cnt["V"])
            if first[0]:
                p.step("vector", lambda E, rr=rr: E.tensor_copy(csb, crt[0:1, rr, :]))
                first[0] = False
            else:
                p.step("vector", lp(lambda E, rr=rr: E.tensor_add(
                    csb, csb, crt[0:1, rr, :])))
        wring_r = Ring(p, "sync")
        state = dict(basem=0, qidx=0)
        base_sl = [0, 0, 0, 0]
        l1g = {0: 0, 1: 0}
        geg = {0: 0, 1: 0}
        cag = {0: 0, 1: 0}
        t2g = {0: 0, 1: 0}
        trg = {0: 0, 1: 0}
        xg = {0: 0, 1: 0}
        pend_ca = []
        sh_prev = []
        PB2 = PSB[:, 1024:1536]
        PB3 = PSB[:, 1536:2048]
        QSET = [(PS0, PS1), (PB2, PB3)]

        def t2reg(e):
            if e < 4:
                return PSB[32:48, e * 256:e * 256 + 256]
            return PSB[64:80, (e - 4) * 256:(e - 4) * 256 + 256]

        def drain_quad(item):
            (j8, eq, tq, sl, cslot, gm, tqs_i, rrA_i, R_i) = item
            if "gp" in L1F_PARTS:
                p.wait("vector", "SC", gm)
                if trg[cslot]:
                    p.wait("vector", "G", trg[cslot])
                cm = p.step("vector", lp(
                    lambda E, eq=eq, sl=sl, cslot=cslot, tq=tq:
                    E.tensor_mul(caeq[:, cslot, eq * 4:(eq + 1) * 4, :]
                                 .rearrange("p a c -> p (a c)"),
                                 aetB[:, sl, :],
                                 cbe[:, tq, eq * 4:(eq + 1) * 4, :]
                                 .rearrange("p a c -> p (a c)"))))
                cag[sl] = cm
            if "t2" in L1F_PARTS:
                p.wait("tensor", "SC", gm)
                for ee in range(4):
                    e = eq * 4 + ee
                    st = (j8 == 0 and e in (0, 2, 4, 6))
                    sp = (j8 == 7 and e in (1, 3, 5, 7))
                    p.pend("tensor", lambda E, e=e, ee=ee, j8=j8, rrA_i=rrA_i,
                           sl=sl, st=st, sp=sp:
                           E.matmul(t2reg(e),
                                    a2r[:, rrA_i, :]
                                    .rearrange("p (a c) -> p a c", a=8)
                                    [:, j8, e * 16:(e + 1) * 16],
                                    aetB[:, sl, ee * 256:(ee + 1) * 256],
                                    start=st, stop=sp))
                t2g[sl] = p.close("tensor")
            if eq == 1 and "gp" in L1F_PARTS:
                p.wait("gpsimd", "V", cm)
                if trg[cslot]:
                    p.wait("gpsimd", "G", trg[cslot])
                s1 = p.step("gpsimd", lp(lambda E, cslot=cslot: E.tensor_add(
                    gtmp2[:, cslot, 0:1024],
                    caeq[:, cslot, 0:4, :].rearrange("p a c -> p (a c)"),
                    caeq[:, cslot, 4:8, :].rearrange("p a c -> p (a c)"))))
                p.wait("gpsimd", "G", s1)
                s2 = p.step("gpsimd", lp(lambda E, cslot=cslot: E.tensor_add(
                    gtmp2[:, cslot, 0:512], gtmp2[:, cslot, 0:512],
                    gtmp2[:, cslot, 512:1024])))
                p.wait("gpsimd", "G", s2)
                s3 = p.step("gpsimd", lp(lambda E, j8=j8, tqs_i=tqs_i, cslot=cslot:
                            E.tensor_add(mixR[:, j8, tqs_i],
                                         gtmp2[:, cslot, 0:256],
                                         gtmp2[:, cslot, 256:512])))
                trg[cslot] = s3

        for th in range(2):
            tsl5 = slice(th * 512, (th + 1) * 512)
            for e in range(8 if "t1" in L1F_PARTS else 0):
                if e == 0:
                    p.wait("tensor", "V", p.cnt["V"])
                for dc in range(DC):
                    p.pend("tensor", lambda E, e=e, dc=dc, tsl5=tsl5:
                           E.matmul(PS2[0:16, :], a1T[:, dc, e * 16:(e + 1) * 16],
                                    xln[:, dc, tsl5],
                                    start=(dc == 0), stop=(dc == DC - 1)))
                t1m = p.close("tensor")
                p.wait("vector", "T", t1m)
                tv = p.step("vector", lambda E, e=e:
                            E.tensor_copy(t1sb[:, e, :], PS2[0:16, :]))
                p.wait("tensor", "V", tv)
            for R in range(8):
                fc0 = R * 8
                rrA = R % 2
                p.wait("gpsimd", "T", p.cnt["T"])
                da2 = p.dma("gpsimd", lambda E, fc0=fc0, rrA=rrA: E.dma_start(
                    out=a2r[:, rrA, :].rearrange("p (a c) -> p a c", a=8),
                    in_=a2tc[fc0:fc0 + 8, :, :].transpose([1, 0, 2])))
                p.wait("gpsimd", "T", p.cnt["T"])
                db1 = p.dma("gpsimd", lambda E, R=R: E.dma_start(
                    out=b1r.rearrange("p a b c -> p (a b c)"), in_=b1tc[R, :, :]))
                for j8 in range(8 if "base" in L1F_PARTS else 0):
                    fc = fc0 + j8
                    par = wring_r.next()
                    ldw = p.dma("sync", lambda E, fc=fc, par=par: E.dma_start(
                        out=wring[:, par * 16:(par + 1) * 16, :],
                        in_=w1c[fc, :, :, :].transpose([1, 0, 2])))
                    p.wait_tok("tensor", ldw)
                    bsl = j8 % 4
                    if base_sl[bsl]:
                        p.wait("tensor", "V", base_sl[bsl])
                    if state["basem"]:
                        p.wait("tensor", "SC", state["basem"])
                    for dc in range(DC):
                        p.pend("tensor", lambda E, dc=dc, par=par, tsl5=tsl5:
                               E.matmul(PS2[:, :], wring[:, par * 16 + dc, :],
                                        xln[:, dc, tsl5],
                                        start=(dc == 0), stop=(dc == DC - 1)))
                    bm = p.close("tensor")
                    wring_r.consumed(par, bm)
                    p.wait("scalar", "T", bm)
                    state["basem"] = p.step("scalar", lambda E, fc=fc, bsl=bsl:
                                            E.activation(base_r[:, bsl, :],
                                                         PS2[:, :], AF.Identity,
                                                         bias=b1col[:, fc:fc + 1]))
                    base_sl[bsl] = 0
                p.wait("tensor", "SC")
                p.wait_tok("tensor", da2)
                p.wait_tok("tensor", db1)
                for pg in (0, 1):
                    if xg[pg]:
                        p.wait("tensor", "V", xg[pg])
                        xg[pg] = 0
                for tq in range(2):
                    tq_abs = th * 2 + tq
                    tqs = slice(tq * 256, (tq + 1) * 256)
                    if R == 0 and "quads" in L1F_PARTS:
                        for e in range(8):
                            rr = e % 2
                            p.wait("gpsimd", "V", p.cnt["V"])
                            p.wait("gpsimd", "T", p.cnt["T"])
                            dcr = p.dma("gpsimd",
                                        lambda E, e=e, rr=rr, tq_abs=tq_abs:
                                        E.dma_start(
                                            out=crt[0:1, rr, 0:256],
                                            in_=crows_d[e:e + 1,
                                                        tq_abs * 256:
                                                        (tq_abs + 1) * 256]))
                            p.wait_tok("tensor", dcr)
                            p.wait("tensor", "V", p.cnt["V"])
                            cbm = p.step("tensor", lambda E, rr=rr:
                                         E.matmul(PS2[:, 0:256],
                                                  ones_rowb[0:1, 0:128],
                                                  crt[0:1, rr, 0:256],
                                                  start=True, stop=True))
                            p.wait("vector", "T", cbm)
                            cb = p.step("vector", lp(lambda E, e=e, tq=tq:
                                        E.tensor_copy(cbe[:, tq, e, :],
                                                      PS2[:, 0:256])))
                        p.wait("tensor", "V", p.cnt["V"])
                    for j8 in range(8 if "quads" in L1F_PARTS else 0):
                        fc = fc0 + j8
                        bsl = j8 % 4
                        for eq in range(2):
                            sl = state["qidx"] % 2
                            cslot = j8 % 2
                            if l1g[sl]:
                                p.wait("tensor", "V", l1g[sl])
                            qa, qb = QSET[sl]
                            for ee in range(4):
                                e = eq * 4 + ee
                                psx = qa if ee < 2 else qb
                                p.pend("tensor",
                                       lambda E, e=e, ee=ee, j8=j8, psx=psx,
                                       tqs=tqs:
                                       E.matmul(psx[:, (ee % 2) * 256:
                                                    (ee % 2) * 256 + 256],
                                                b1r[:, e, j8, :],
                                                t1sb[0:16, e, tqs],
                                                start=True, stop=True))
                            lm = p.close("tensor")
                            p.wait("vector", "T", lm)
                            if geg[sl]:
                                p.wait("vector", "SC", geg[sl])
                            bs = base_r[:, bsl, tqs]
                            brep = AP(bs.tensor, bs.offset,
                                      [list(bs.ap[0])] + [[0, 2]] +
                                      [list(d) for d in bs.ap[1:]])
                            p.step("vector", lp(lambda E, sl=sl, brep=brep, qa=qa:
                                   E.scalar_tensor_tensor(
                                       aetA[:, sl, 0:512], qa[:, :], 2.0,
                                       brep, ALU.mult, ALU.add)))
                            sv = p.step("vector", lp(lambda E, sl=sl, brep=brep, qb=qb:
                                        E.scalar_tensor_tensor(
                                            aetA[:, sl, 512:1024], qb[:, :], 2.0,
                                            brep, ALU.mult, ALU.add)))
                            l1g[sl] = sv
                            base_sl[bsl] = sv
                            p.wait("scalar", "V", sv)
                            if cag[sl]:
                                p.wait("scalar", "V", cag[sl])
                            if t2g[sl]:
                                p.wait("scalar", "T", t2g[sl])
                            gm = p.step("scalar", lambda E, sl=sl: E.activation(
                                aetB[:, sl, :], aetA[:, sl, :],
                                AF.Gelu_apprx_tanh))
                            geg[sl] = gm
                            pend_ca.append((j8, eq, tq, sl, cslot, gm, tqs, rrA, R))
                            state["qidx"] += 1
                            while len(pend_ca) > 1:
                                drain_quad(pend_ca.pop(0))
                    while pend_ca:
                        drain_quad(pend_ca.pop(0))
                    t2m = p.mark("T")
                    p.wait("vector", "T", t2m)
                    if R == 0 and tq == 0 and sh_prev:
                        for tok in sh_prev:
                            p.wait_tok("vector", tok)
                        sh_prev.clear()
                    for e in range(8 if "t2" in L1F_PARTS else 0):
                        tac = t2accA if e < 4 else t2accB
                        ei = e % 4
                        p.wait("vector", "V", p.cnt["V"])
                        if R == 0:
                            p.step("vector", lp(lambda E, e=e, ei=ei, tac=tac,
                                   tqs=tqs:
                                   E.tensor_copy(tac[:, ei, tqs], t2reg(e))))
                        else:
                            p.step("vector", lp(lambda E, e=e, ei=ei, tac=tac,
                                   tqs=tqs:
                                   E.scalar_tensor_tensor(
                                       tac[:, ei, tqs], t2reg(e), 1.0,
                                       tac[:, ei, tqs], ALU.mult, ALU.add)))
                    t2ev = p.mark("V")
                    p.wait("tensor", "V", t2ev)
                p.wait("tensor", "G", p.cnt["G"])
                for dc in range(DC if "w2" in L1F_PARTS else 0):
                    psw = PS0 if dc % 2 == 0 else PS1
                    par = wring_r.next()
                    ldw2 = p.dma("sync", lambda E, dc=dc, fc0=fc0, par=par:
                                 E.dma_start(
                                     out=wring[:, par * 16:par * 16 + 8, :],
                                     in_=w2c[dc, fc0:fc0 + 8, :, :]
                                     .transpose([1, 0, 2])))
                    p.wait_tok("tensor", ldw2)
                    if xg[dc % 2]:
                        p.wait("tensor", "V", xg[dc % 2])
                    if l1g[dc % 2]:
                        p.wait("tensor", "V", l1g[dc % 2])
                        l1g[dc % 2] = 0
                    for j in range(8):
                        p.pend("tensor", lambda E, psw=psw, j=j, par=par:
                               E.matmul(psw[:, :], wring[:, par * 16 + j, :],
                                        mixR[:, j, :], start=(j == 0),
                                        stop=(j == 7)))
                    wm = p.close("tensor")
                    wring_r.consumed(par, wm)
                    p.wait("vector", "T", wm)
                    xg[dc % 2] = p.step("vector",
                                        lambda E, dc=dc, psw=psw, tsl5=tsl5:
                                        E.tensor_add(x_res[:, dc, tsl5],
                                                     x_res[:, dc, tsl5],
                                                     psw[:, :]))
            l2e_on = "l2" in L1F_PARTS or "l2e" in L1F_PARTS
            l2d_on = "l2" in L1F_PARTS or "l2dma" in L1F_PARTS
            l2m_on = "l2" in L1F_PARTS or "l2mm" in L1F_PARTS
            for e in range(8 if l2e_on else 0):
                rr = e % 2
                p.wait("gpsimd", "V", p.cnt["V"])
                p.wait("gpsimd", "T", p.cnt["T"])
                dcr = p.dma("gpsimd", lambda E, e=e, rr=rr, th=th: E.dma_start(
                    out=crt[0:1, rr, 0:512],
                    in_=crows_d[e:e + 1, th * 512:(th + 1) * 512]))
                p.wait_tok("tensor", dcr)
                p.wait("tensor", "V", p.cnt["V"])
                outr = PS2[32:48, :] if e < 4 else PS2[64:80, :]
                tac = t2accA if e < 4 else t2accB
                ei = e % 4
                cbm = p.step("tensor", lambda E, rr=rr, outr=outr:
                             E.matmul(outr, ones_rowb[0:1, 0:16],
                                      crt[0:1, rr, 0:512], start=True, stop=True))
                p.wait("vector", "T", cbm)
                p.step("vector", lp(lambda E, ei=ei, tac=tac, outr=outr:
                       E.scalar_tensor_tensor(
                           tac[:, ei, :], tac[:, ei, :], 2.0, outr,
                           ALU.mult, ALU.mult)))
            tcv = p.mark("V")
            p.wait("tensor", "V", tcv)
            shtoks = []
            if l2d_on:
                p.wait("gpsimd", "T", p.cnt["T"])
                p.wait("gpsimd", "V", tcv)
                for e in range(8):
                    tac = t2accA if e < 4 else t2accB
                    ei = e % 4
                    shtoks.append(p.dma("gpsimd", lambda E, e=e, ei=ei, tac=tac:
                                        E.dma_start(
                                            out=t2all[e * 16:(e + 1) * 16, :],
                                            in_=tac[:, ei, :])))
                sh_prev[:] = shtoks
            lg2 = xg
            for dc in range(DC if (l2d_on or l2m_on) else 0):
                psw = PS0 if dc % 2 == 0 else PS1
                rrb2 = dc % 2
                if l2d_on:
                    if dc >= 2:
                        p.wait("gpsimd", "T", p.cnt["T"])
                    db2r = p.dma("gpsimd", lambda E, dc=dc, rrb2=rrb2:
                                 E.dma_start(
                                     out=b2stk[:, rrb2, :],
                                     in_=b2tc[dc, :, :, :]
                                     .rearrange("e r d -> (e r) d")))
                    p.wait_tok("tensor", db2r)
                if not l2m_on:
                    continue
                if dc == 0:
                    for tok in shtoks:
                        p.wait_tok("tensor", tok)
                if lg2[dc % 2]:
                    p.wait("tensor", "V", lg2[dc % 2])
                if xg[dc % 2]:
                    p.wait("tensor", "V", xg[dc % 2])
                    xg[dc % 2] = 0
                p.pend("tensor", lambda E, dc=dc, psw=psw, th=th: E.matmul(
                    psw[:, :], b2row[0:1, dc * 128:(dc + 1) * 128],
                    csb[0:1, th * 512:(th + 1) * 512], start=True, stop=False))
                p.pend("tensor", lambda E, rrb2=rrb2, psw=psw: E.matmul(
                    psw[:, :], b2stk[:, rrb2, :], t2all[:, :],
                    start=False, stop=True))
                wm = p.close("tensor")
                p.wait("vector", "T", wm)
                lg2[dc % 2] = p.step("vector", lambda E, dc=dc, psw=psw, tsl5=tsl5:
                                     E.tensor_add(x_res[:, dc, tsl5],
                                                  x_res[:, dc, tsl5], psw[:, :]))
        p.barrier()
        ABF.reset()
        AFA.reset()

    def phase_out():
        p.barrier()
        AFA.reset()
        stg = AFA.alloc(128, 1024)
        stg2 = AFA.alloc(128, 1024)
        guards = {0: 0, 1: 0}
        sguard = {0: 0, 1: 0}
        for tt in range(8):
            for dh in range(2):
                par = dh
                st = stg if par == 0 else stg2
                if sguard[par]:
                    p.wait_tok("vector", sguard[par])
                for j in range(8):
                    dc = dh * 8 + j
                    pst = [PS0, PS1][j % 2]
                    if guards[j % 2]:
                        p.wait("tensor", "V", guards[j % 2])
                    td = p.step("tensor", lambda E, pst=pst, dc=dc, tt=tt:
                                E.transpose(pst[:, 0:128],
                                            x_res[:, dc, tt * 128:(tt + 1) * 128],
                                            ident[:, :]))
                    p.wait("vector", "T", td)
                    ev = p.step("vector", lambda E, pst=pst, st=st, j=j:
                                E.tensor_copy(st[:, j * 128:(j + 1) * 128], pst[:, 0:128]))
                    guards[j % 2] = ev
                p.wait("sync", "V")
                sguard[par] = p.dma("sync", lambda E, tt=tt, dh=dh, st=st:
                                    E.dma_start(out=y_out[tt * 128:(tt + 1) * 128,
                                                          dh * 1024:(dh + 1) * 1024],
                                                in_=st))
        p.wait_dma_all("sync", "sync")

    def tap_xln_chunk(chunk):
        p.barrier()
        AFA.reset()
        stg = AFA.alloc(128, 1024)
        p.step("vector", lambda E: E.tensor_copy(stg, xln[:, chunk, :]))
        p.wait("sync", "V")
        p.dma("sync", lambda E: E.dma_start(out=dbg_out[0:128, 0:1024], in_=stg))
        p.wait_dma_all("sync", "sync")

    phase0()
    if tap == "ln1":
        layernorm()
        tap_xln_chunk(0)
    elif tap == "l0a":
        l0_attn()
    elif tap == "l0f":
        l0_attn()
        l0_ffn()
    elif tap == "l1a":
        l0_attn()
        l0_ffn()
        l1_attn()
    else:
        l0_attn()
        l0_ffn()
        l1_attn()
        l1_ffn()
    phase_out()

    with nc.Block() as block:
        @block.sync
        def _(eng):
            for fn in p.q["sync"]:
                fn(eng)

        @block.tensor
        def _(eng):
            for fn in p.q["tensor"]:
                fn(eng)

        @block.vector
        def _(eng):
            for fn in p.q["vector"]:
                fn(eng)

        @block.scalar
        def _(eng):
            for fn in p.q["scalar"]:
                fn(eng)

        @block.gpsimd
        def _(eng):
            for fn in p.q["gpsimd"]:
                fn(eng)

    return nc, meta


def make_inputs_for_core(core, inputs):
    b = core // 2
    half = core % 2
    d = {}
    x = np.asarray(inputs["L0_attn_hidden_states"])
    d["x"] = np.ascontiguousarray(x[b, half * S:(half + 1) * S, :])
    for k, v in inputs.items():
        if k == "L0_attn_hidden_states":
            continue
        d[k] = np.ascontiguousarray(np.asarray(v))
    d["ident128"] = np.eye(128, dtype=np.float32)
    import ml_dtypes
    pm = np.zeros((128, SK), dtype=ml_dtypes.bfloat16)
    pm[:, half * S:(half + 1) * S] = 1.0
    d["pmask"] = pm
    sel = np.zeros((32, 32, 64), dtype=np.float32)
    for h in range(32):
        sel[h, h, :] = 1.0
    d["selbc"] = sel.reshape(32, 2048).astype(ml_dtypes.bfloat16)
    return d


def kernel(**inputs):
    nc, _ = build()
    in_maps = [make_inputs_for_core(c, inputs) for c in range(NCORES)]
    res = run_bass_kernel_spmd(nc, in_maps, core_ids=list(range(NCORES)))
    out = np.empty((B, SK, D), dtype=np.float32)
    for c in range(NCORES):
        b = c // 2
        half = c % 2
        out[b, half * S:(half + 1) * S, :] = res.results[c]["y"]
    return out


# version 04
# speedup vs baseline: 1.7363x; 1.0209x over previous
import numpy as np

import concourse.bass as bass
import concourse.mybir as mybir
from concourse.bass_utils import run_bass_kernel_spmd
from concourse.bass_types import AP

F32 = mybir.dt.float32
BF16 = mybir.dt.bfloat16
AF = mybir.ActivationFunctionType
ALU = mybir.AluOpType
AX = mybir.AxisListType

D = 2048
DC = 16
S = 1024
SK = 2048
FF = 8192
FC = 64
B = 4
NCORES = 8

L1F_PARTS = frozenset({"t1", "base", "quads", "gp", "t2", "w2", "l2"})

ENGS = ("sync", "tensor", "vector", "scalar", "gpsimd")
PROG_SEM = {"sync": "SY", "tensor": "T", "vector": "V", "scalar": "SC", "gpsimd": "G"}
NDMASUB = 4
DMA_SEM = {q: [f"{q[:2].upper()}{i}" for i in range(NDMASUB)]
           for q in ("sync", "scalar", "gpsimd")}


class Prog:
    def __init__(self, nc):
        self.nc = nc
        self.q = {e: [] for e in ENGS}
        self.sem_handles = {}
        self.cnt = {}
        self._sem_ctxs = []
        self.dma_ctr = {q: 0 for q in DMA_SEM}
        self._pend = {e: None for e in ENGS}
        for s in list(PROG_SEM.values()) + ["CC"]:
            self.new_sem(s)
        for q, subs in DMA_SEM.items():
            for s in subs:
                self.new_sem(s)

    def new_sem(self, name):
        ctx = self.nc.semaphore(name)
        handle = ctx.__enter__()
        self._sem_ctxs.append(ctx)
        self.sem_handles[name] = handle
        self.cnt[name] = 0

    def emit(self, eng, fn):
        self._flush(eng)
        self.q[eng].append(fn)

    def pend(self, eng, fn):
        self._flush(eng)
        self._pend[eng] = fn

    def close(self, eng):
        fn = self._pend[eng]
        assert fn is not None, f"close({eng}) with no pending instruction"
        self._pend[eng] = None
        self.cnt[PROG_SEM[eng]] += 1
        tgt = self.cnt[PROG_SEM[eng]]
        h = self.sem_handles[PROG_SEM[eng]]
        self.q[eng].append(lambda E, f=fn, h=h: f(E).then_inc(h, 1))
        return tgt

    def _flush(self, eng):
        fn = self._pend.get(eng)
        if fn is not None:
            self._pend[eng] = None
            self.q[eng].append(fn)

    def inc(self, eng, sem, fn, n=1):
        self._flush(eng)
        self.cnt[sem] += n
        tgt = self.cnt[sem]
        h = self.sem_handles[sem]
        self.q[eng].append(lambda E, f=fn, h=h, n=n: f(E).then_inc(h, n))
        return tgt

    def step(self, eng, fn):
        return self.inc(eng, PROG_SEM[eng], fn)

    def dma(self, eng, fn):
        sub = DMA_SEM[eng][self.dma_ctr[eng] % NDMASUB]
        self.dma_ctr[eng] += 1
        if self.cnt[sub] > 0:
            self.wait(eng, sub, self.cnt[sub])
        v = self.inc(eng, sub, fn, 16)
        return (sub, v)

    def wait_tok(self, eng, tok):
        if tok:
            self.wait(eng, tok[0], tok[1])

    def wait_dma_all(self, eng, queue):
        for sub in DMA_SEM[queue]:
            self.wait(eng, sub, self.cnt[sub])

    def wait(self, eng, sem, val=None):
        v = self.cnt[sem] if val is None else val
        assert v <= self.cnt[sem], f"wait target {v} > emitted {self.cnt[sem]} on {sem}"
        if v <= 0:
            return
        self._flush(eng)
        h = self.sem_handles[sem]
        self.q[eng].append(lambda E, h=h, v=v: E.wait_ge(h, v))

    def mark(self, sem):
        return self.cnt[sem]

    def barrier(self, engs=ENGS):
        marks = {}
        for eng in engs:
            self.wait(eng, PROG_SEM[eng], self.cnt[PROG_SEM[eng]])
            marks[eng] = self.inc(eng, PROG_SEM[eng], lambda E: E.nop())
        for eng in engs:
            for oeng in engs:
                if oeng != eng:
                    self.wait(eng, PROG_SEM[oeng], marks[oeng])
            for q in DMA_SEM:
                self.wait_dma_all(eng, q)


class Arena:

    def __init__(self, tensor, total):
        self.t = tensor
        self.total = total
        self.off = 0

    def alloc(self, parts, n):
        assert self.off + n <= self.total, f"arena overflow: {self.off}+{n} > {self.total}"
        ap = self.t[0:parts, self.off:self.off + n]
        self.off += n
        return ap

    def alloc3(self, parts, a, b_):
        ap = self.alloc(parts, a * b_)
        return ap.rearrange("p (a b) -> p a b", a=a)

    def reset(self):
        self.off = 0


class Ring:

    def __init__(self, p, eng="gpsimd"):
        self.p = p
        self.eng = eng
        self.ctr = 0
        self.guard = [0, 0]

    def next(self):
        par = self.ctr % 2
        self.ctr += 1
        if self.guard[par]:
            self.p.wait(self.eng, "T", self.guard[par])
        return par

    def consumed(self, par, mark):
        self.guard[par] = mark


def build(tap=None):
    nc = bass.Bass("TRN2", target_bir_lowering=False)

    def din(name, shape, dtype=F32):
        return nc.dram_tensor(name, list(shape), dtype, kind="ExternalInput").ap()

    x_in = din("x", [S, D])
    Wq = din("L0_attn_Wq", [D, D])
    Wk = din("L0_attn_Wk", [D, D])
    Wv = din("L0_attn_Wv", [D, D])
    Wo = din("L0_attn_Wo", [D, D])
    bo_in = din("L0_attn_bo", [D])
    up_w = din("L0_ffn_up_weight", [FF, D])
    dn_w = din("L0_ffn_down_weight", [D, FF])
    up_s = din("L0_ffn_up_scale", [FF])
    dn_s = din("L0_ffn_down_scale", [D])
    out_s = din("L0_ffn_output_scale", [1])
    basis_in, phase_in, amp_in = {}, {}, {}
    for nm in ("q", "k", "v", "o"):
        basis_in[nm] = din(f"L1_attn_basis_{nm}", [64, D])
        phase_in[nm] = din(f"L1_attn_phase_{nm}", [D, 64])
        amp_in[nm] = din(f"L1_attn_amp_{nm}", [D, 64])
    Wr = din("L1_ffn_Wr", [D, 8])
    br_in = din("L1_ffn_br", [8])
    W1 = din("L1_ffn_W1", [D, FF])
    b1_in = din("L1_ffn_b1", [FF])
    W2 = din("L1_ffn_W2", [FF, D])
    b2_in = din("L1_ffn_b2", [D])
    A1 = din("L1_ffn_A1", [8, 16, D])
    B1 = din("L1_ffn_B1", [8, FF, 16])
    A2 = din("L1_ffn_A2", [8, 16, FF])
    B2 = din("L1_ffn_B2", [8, D, 16])
    ident_in = din("ident128", [128, 128])
    pmask_in = din("pmask", [128, SK], BF16)
    selbc_in = din("selbc", [32, 2048], BF16)

    y_out = nc.dram_tensor("y", [S, D], F32, kind="ExternalOutput").ap()
    dbg_out = nc.dram_tensor("dbg", [128, 4096], F32, kind="ExternalOutput").ap()

    cc_kv_in = nc.dram_tensor("cc_kv_in", [128, 2064], F32).ap()
    cc_kv_out = nc.dram_tensor("cc_kv_out", [128, 2064], F32).ap()
    cc_res_in = nc.dram_tensor("cc_res_in", [128, SK], BF16).ap()
    cc_res_out = nc.dram_tensor("cc_res_out", [128, SK], BF16).ap()
    upTc = nc.dram_tensor("upTc", [FC, DC, 128, 128], BF16).ap()
    dnTc = nc.dram_tensor("dnTc", [DC, FC, 128, 128], BF16).ap()
    w1c = nc.dram_tensor("w1c", [FC, DC, 128, 128], BF16).ap()
    w2c = nc.dram_tensor("w2c", [DC, FC, 128, 128], BF16).ap()
    b1tc = nc.dram_tensor("b1tc", [8, 16, 8192], BF16).ap()
    a2tc = nc.dram_tensor("a2tc", [FC, 128, 128], BF16).ap()
    b2tc = nc.dram_tensor("b2tc", [DC, 8, 16, 128], BF16).ap()
    crows_d = nc.dram_tensor("crows_d", [9, S], F32).ap()

    x_res = nc.alloc_sbuf_tensor("x_res", [128, DC, S], F32)
    xln = nc.alloc_sbuf_tensor("xln", [128, DC, S], BF16)
    bufA = nc.alloc_sbuf_tensor("bufA", [128, 16384], BF16)
    bufB = nc.alloc_sbuf_tensor("bufB", [128, 16384], BF16)
    wring = nc.alloc_sbuf_tensor("wring", [128, 32, 128], BF16)
    ident = nc.alloc_sbuf_tensor("ident", [128, 128], F32)
    ident_bf = nc.alloc_sbuf_tensor("ident_bf", [128, 128], BF16)
    ones_col = nc.alloc_sbuf_tensor("ones_col", [128, 1], F32)
    ones_colb = nc.alloc_sbuf_tensor("ones_colb", [128, 1], BF16)
    ones_row = nc.alloc_sbuf_tensor("ones_row", [1, 128], F32)
    pio2 = nc.alloc_sbuf_tensor("pio2", [128, 1], F32)
    ones_rowb = nc.alloc_sbuf_tensor("ones_rowb", [1, 512], BF16)
    arena_bf_t = nc.alloc_sbuf_tensor("arena_bf", [128, 13056], BF16)
    arena_f32_t = nc.alloc_sbuf_tensor("arena_f32", [128, 2624], F32)
    sbuf_left = nc.sbuf_bytes_remaining

    ABF = Arena(arena_bf_t, 13056)
    AFA = Arena(arena_f32_t, 2624)

    PSB = nc.alloc_psum_tensor("PSB", [128, 2048], F32)
    PS0 = nc.alloc_psum_tensor("PS0", [128, 512], F32)
    PS1 = nc.alloc_psum_tensor("PS1", [128, 512], F32)
    PS2 = nc.alloc_psum_tensor("PS2", [128, 512], F32)
    PT = nc.alloc_psum_tensor("PT", [128, 1024], BF16)

    p = Prog(nc)
    meta = dict(sbuf_left=sbuf_left)

    def lp(fn):
        def wrapped(E, fn=fn):
            with nc.allow_low_precision(reason="bf16 operand for subsequent matmul"):
                return fn(E)
        return wrapped

    def load_colvec(vec_dram, nch, dst, tmp):
        p.wait("sync", "T", p.cnt["T"])
        ld = p.dma("sync", lambda E: E.dma_start(
            out=tmp[0:nch, 0:128], in_=vec_dram.rearrange("(a b) -> a b", b=128)))
        p.wait_tok("tensor", ld)
        p.wait("tensor", "V", p.cnt["V"])
        tm = p.step("tensor", lambda E: E.transpose(PS2[:, 0:nch], tmp[0:nch, 0:128],
                                                    ident[0:nch, 0:nch]))
        p.wait("vector", "T", tm)
        return p.step("vector", lambda E: E.tensor_copy(dst, PS2[:, 0:nch]))

    def phase0():
        AFA.reset()
        stg = AFA.alloc(128, 1024)
        stg2 = AFA.alloc(128, 1024)
        for t_ in (ones_col, ones_colb, ones_row, ones_rowb):
            p.emit("vector", lambda E, t_=t_: E.memset(t_[:, :], 1.0))
        p.emit("vector", lambda E: E.memset(pio2[:, :], float(np.pi / 2)))
        p.step("vector", lambda E: E.memset(stg2, 0.0))
        p.dma("sync", lambda E: E.dma_start(out=ident[:, :], in_=ident_in[:, :]))
        p.dma("gpsimd", lambda E: E.dma_start(out=ident_bf[:, :], in_=ident_in[:, :]))
        p.wait("sync", "V")
        for c0 in range(0, 2064, 1024):
            w = min(1024, 2064 - c0)
            p.dma("sync", lambda E, c0=c0, w=w:
                  E.dma_start(out=cc_kv_in[:, c0:c0 + w], in_=stg2[:, 0:w]))
        p.wait_dma_all("sync", "sync")
        guards = {0: 0, 1: 0}
        sguard = {0: 0, 1: 0}
        for tt in range(8):
            for dh in range(2):
                par = (tt * 2 + dh) % 2
                st = stg if par == 0 else stg2
                if sguard[par]:
                    p.wait("sync", "T", sguard[par])
                ld = p.dma("sync", lambda E, tt=tt, dh=dh, st=st:
                           E.dma_start(out=st,
                                       in_=x_in[tt * 128:(tt + 1) * 128,
                                                dh * 1024:(dh + 1) * 1024]))
                p.wait_tok("tensor", ld)
                for j in range(8):
                    dc = dh * 8 + j
                    pst = [PS0, PS1][j % 2]
                    if guards[j % 2]:
                        p.wait("tensor", "V", guards[j % 2])
                    td = p.step("tensor", lambda E, pst=pst, st=st, j=j:
                                E.transpose(pst[:, 0:128], st[:, j * 128:(j + 1) * 128],
                                            ident[:, :]))
                    p.wait("vector", "T", td)
                    ev = p.step("vector", lambda E, pst=pst, dc=dc, tt=tt:
                                E.tensor_copy(x_res[:, dc, tt * 128:(tt + 1) * 128],
                                              pst[:, 0:128]))
                    guards[j % 2] = ev
                sguard[par] = p.mark("T")

    def layernorm():
        p.barrier()
        AFA.reset()
        strow = AFA.alloc3(1, 2, 1024)
        stg = AFA.alloc(128, 512)
        prev_mm = 0
        for dc in range(DC):
            for th in range(2):
                sl = slice(th * 512, (th + 1) * 512)
                if prev_mm:
                    p.wait("scalar", "T", prev_mm)
                sq = p.step("scalar", lambda E, dc=dc, sl=sl:
                            E.activation(stg, x_res[:, dc, :][:, sl], AF.Square))
                p.emit("tensor", lambda E, dc=dc, sl=sl:
                       E.matmul(PSB[0:1, 0:1024][:, sl], ones_col[:, :],
                                x_res[:, dc, :][:, sl], start=(dc == 0), stop=(dc == DC - 1)))
                p.wait("tensor", "SC", sq)
                prev_mm = p.step("tensor", lambda E, dc=dc, th=th:
                                 E.matmul(PSB[32:33, 1024 + th * 512:1536 + th * 512],
                                          ones_col[:, :], stg,
                                          start=(dc == 0), stop=(dc == DC - 1)))
        p.wait("vector", "T")
        p.step("vector", lambda E: E.tensor_scalar_mul(strow[:, 0, :], PSB[0:1, 0:1024], 1.0 / D))
        p.wait("vector", "V")
        p.step("vector", lambda E: E.tensor_mul(strow[:, 1, :], strow[:, 0, :], strow[:, 0, :]))
        p.wait("vector", "V")
        p.step("vector", lambda E:
               E.scalar_tensor_tensor(strow[:, 1, :], PSB[32:33, 1024:2048], 1.0 / D,
                                      strow[:, 1, :], ALU.mult, ALU.subtract))
        p.wait("vector", "V")
        vs = p.step("vector", lambda E: E.tensor_scalar_add(strow[:, 1, :], strow[:, 1, :], 1e-5))
        p.wait("scalar", "V", vs)
        sq = p.step("scalar", lambda E: E.activation(strow[:, 1, :], strow[:, 1, :], AF.Sqrt))
        p.wait("vector", "SC", sq)
        vb = p.step("vector", lambda E: E.reciprocal(strow[:, 1, :], strow[:, 1, :]))
        p.wait("tensor", "V", vb)
        for th in range(2):
            sl = slice(th * 512, (th + 1) * 512)
            rps = PS0 if th == 0 else PS1
            p.emit("tensor", lambda E, rps=rps, sl=sl:
                   E.matmul(rps[:, :], ones_row[0:1, :], strow[:, 1, :][:, sl],
                            start=True, stop=True))
            p.step("tensor", lambda E, th=th, sl=sl:
                   E.matmul(PSB[:, th * 512:(th + 1) * 512], ones_row[0:1, :],
                            strow[:, 0, :][:, sl], start=True, stop=True))
        p.wait("vector", "T")
        for dc in range(DC):
            for th in range(2):
                sl = slice(th * 512, (th + 1) * 512)
                mps = PSB[:, th * 512:(th + 1) * 512]
                fn = lambda E, dc=dc, sl=sl, mps=mps: \
                    E.tensor_sub(xln[:, dc, :][:, sl], x_res[:, dc, :][:, sl], mps)
                if dc == DC - 1 and th == 1:
                    p.step("vector", fn)
                else:
                    p.emit("vector", fn)
        p.wait("vector", "V")
        for dc in range(DC):
            for th in range(2):
                sl = slice(th * 512, (th + 1) * 512)
                rps = PS0 if th == 0 else PS1
                p.pend("vector", lambda E, dc=dc, sl=sl, rps=rps:
                       E.tensor_mul(xln[:, dc, :][:, sl], xln[:, dc, :][:, sl], rps[:, :]))
        p.close("vector")
        p.barrier()
        AFA.reset()

    def load_col_chunks(ring, Wap, ic):
        par = ring.next()
        mk = p.dma("gpsimd", lambda E, Wap=Wap, ic=ic, par=par:
                   E.dma_start(out=wring[:, par * 16:(par + 1) * 16, :],
                               in_=Wap[:, ic * 128:(ic + 1) * 128]
                               .rearrange("(a p) c -> p a c", p=128)))
        return par, mk

    def l0_attn():
        layernorm()
        ABF.reset()
        AFA.reset()
        kvp = AFA.alloc(64, 2048)
        kssb = AFA.alloc(128, 16)
        kvbf = ABF.alloc(128, 2048)
        ksD = ABF.alloc3(128, 16, 32)
        rnbf = ABF.alloc(32, 1024)
        selbc = ABF.alloc3(32, 32, 64)
        bocol = AFA.alloc(128, 16)
        kring = ABF.alloc3(128, 2, 1024)
        vring = ABF.alloc3(128, 2, 1024)
        ktm = ABF.alloc(128, 512)
        vtm = ABF.alloc(128, 512)
        rnb = ABF.alloc(64, 512)
        p.dma("sync", lambda E: E.dma_start(
            out=selbc.rearrange("p a b -> p (a b)"), in_=selbc_in[:, :]))
        tmpcv = AFA.alloc(64, 128)
        load_colvec(bo_in, 16, bocol, tmpcv)
        p.step("vector", lambda E: E.memset(ksD.rearrange("p a b -> p (a b)"), 0.0))
        ring = Ring(p)

        ks_first = {0: True, 1: True}
        n_ks = 0
        ev_guard = {"k": {}, "v": {}}
        kv_evac_g = 0
        tp_guard = [0]
        slotg = {0: 0, 1: 0}
        for ic in range(16):
            for which, Wap, dst, relu in (("k", Wk, kring, True), ("v", Wv, vring, False)):
                par, ld = load_col_chunks(ring, Wap, ic)
                psl = PSB[:, 0:1024] if which == "k" else PSB[:, 1024:2048]
                p.wait_tok("tensor", ld)
                if ic >= 1:
                    eng, mk = ev_guard[which][ic - 1]
                    p.wait("tensor", eng, mk)
                for dc in range(DC):
                    for th in range(2):
                        p.pend("tensor", lambda E, psl=psl, dc=dc, th=th, par=par:
                               E.matmul(psl[:, th * 512:(th + 1) * 512],
                                        wring[:, par * 16 + dc, :],
                                        xln[:, dc, th * 512:(th + 1) * 512],
                                        start=(dc == 0), stop=(dc == DC - 1)))
                mm = p.close("tensor")
                ring.consumed(par, mm)
                slot = ic % 2
                if relu:
                    p.wait("scalar", "T", mm)
                    p.wait("scalar", "T", slotg[slot])
                    ev = p.step("scalar", lambda E, psl=psl, dst=dst, slot=slot:
                                E.activation(dst[:, slot, :], psl[:, :], AF.Relu))
                    ev_guard[which][ic] = ("SC", ev)
                else:
                    p.wait("vector", "T", mm)
                    p.wait("vector", "T", slotg[slot])
                    ev = p.step("vector", lambda E, psl=psl, dst=dst, slot=slot:
                                E.tensor_copy(dst[:, slot, :], psl[:, :]))
                    ev_guard[which][ic] = ("V", ev)
            slot = ic % 2
            p.wait("tensor", "SC", ev_guard["k"][ic][1])
            p.wait("tensor", "V", ev_guard["v"][ic][1])
            for hp in range(2):
                h = 2 * ic + hp
                if tp_guard[0]:
                    p.wait("tensor", "V", tp_guard[0])
                idsl = ident_bf[64 * hp:64 * hp + 64, 64 * hp:64 * hp + 64]
                for tt in range(8):
                    p.pend("tensor", lambda E, hp=hp, tt=tt, slot=slot, idsl=idsl:
                           E.transpose(PT[:, tt * 64:(tt + 1) * 64],
                                       kring[64 * hp:64 * hp + 64, slot,
                                             tt * 128:(tt + 1) * 128],
                                       idsl))
                    p.pend("tensor", lambda E, hp=hp, tt=tt, slot=slot, idsl=idsl:
                           E.transpose(PT[:, 512 + tt * 64:512 + (tt + 1) * 64],
                                       vring[64 * hp:64 * hp + 64, slot,
                                             tt * 128:(tt + 1) * 128],
                                       idsl))
                td = p.close("tensor")
                p.wait("vector", "T", td)
                if kv_evac_g:
                    p.wait("vector", "T", kv_evac_g)
                evt = 0
                for half, dsttm in ((0, ktm), (1, vtm)):
                    evt = p.step("vector", lambda E, dsttm=dsttm, half=half:
                                 E.tensor_copy(dsttm, PT[:, half * 512:(half + 1) * 512]))
                tp_guard[0] = evt
                p.wait("tensor", "V", evt)
                for tt in range(8):
                    st = (tt == 0 and hp == 0)
                    sp = (tt == 7 and hp == 1)
                    p.pend("tensor", lambda E, tt=tt, hp=hp, st=st, sp=sp:
                           E.matmul(PS2[0:64, hp * 64:(hp + 1) * 64],
                                    ktm[:, tt * 64:(tt + 1) * 64],
                                    vtm[:, tt * 64:(tt + 1) * 64],
                                    start=st, stop=sp))
                for tt in range(8):
                    n_ks += 1
                    stk = ks_first[hp]
                    ks_first[hp] = False
                    spk = (ic == 15 and tt == 7)
                    ksdst = PS0[0:64, ic:ic + 1] if hp == 0 else PS1[64:128, ic:ic + 1]
                    p.pend("tensor", lambda E, tt=tt, ksdst=ksdst, st=stk, sp=spk:
                           E.matmul(ksdst,
                                    ktm[:, tt * 64:(tt + 1) * 64],
                                    ones_colb[0:128, :],
                                    start=st, stop=sp))
                kv_evac_g = p.close("tensor")
            slotg[slot] = kv_evac_g
            p.wait("vector", "T", kv_evac_g)
            p.step("vector", lambda E, ic=ic:
                   E.tensor_copy(kvp[:, ic * 128:(ic + 1) * 128], PS2[0:64, 0:128]))
        p.step("vector", lambda E: E.tensor_copy(kssb[0:64, :], PS0[0:64, 0:16]))
        evks = p.step("vector", lambda E: E.tensor_copy(kssb[64:128, :], PS1[64:128, 0:16]))
        p.wait("sync", "V", evks)
        p.dma("sync", lambda E: E.dma_start(out=cc_kv_in[0:64, 0:2048], in_=kvp))
        d2 = p.dma("sync", lambda E: E.dma_start(out=cc_kv_in[:, 2048:2064], in_=kssb))
        p.wait_dma_all("gpsimd", "sync")
        cc = p.inc("gpsimd", "CC", lambda E:
                   E.collective_compute("AllReduce", ALU.add,
                                        replica_groups=[[0, 1], [2, 3], [4, 5], [6, 7]],
                                        ins=[cc_kv_in[:, :]], outs=[cc_kv_out[:, :]]))
        qfm = bufA[:, :].rearrange("p (a b) -> p a b", a=16)
        qev = {}
        for ic in range(16):
            par, ld = load_col_chunks(ring, Wq, ic)
            psl = PSB[:, 0:1024] if ic % 2 == 0 else PSB[:, 1024:2048]
            p.wait_tok("tensor", ld)
            if ic >= 2:
                p.wait("tensor", "SC", qev[ic - 2])
            for dc in range(DC):
                for th in range(2):
                    p.pend("tensor", lambda E, psl=psl, dc=dc, th=th, par=par:
                           E.matmul(psl[:, th * 512:(th + 1) * 512],
                                    wring[:, par * 16 + dc, :],
                                    xln[:, dc, th * 512:(th + 1) * 512],
                                    start=(dc == 0), stop=(dc == DC - 1)))
            mq = p.close("tensor")
            ring.consumed(par, mq)
            p.wait("scalar", "T", mq)
            qev[ic] = p.step("scalar", lambda E, psl=psl, ic=ic:
                             E.activation(qfm[:, ic, :], psl[:, :], AF.Relu))
        p.wait("gpsimd", "CC", cc)
        dkv = p.dma("gpsimd", lambda E: E.dma_start(out=kvbf[0:64, :],
                                                    in_=cc_kv_out[0:64, 0:2048]))
        dkv2 = p.dma("gpsimd", lambda E: E.dma_start(out=kvbf[64:128, :],
                                                     in_=cc_kv_out[0:64, 0:2048]))
        p.wait("sync", "CC", cc)
        dks = p.dma("sync", lambda E: E.dma_start(out=kssb, in_=cc_kv_out[:, 2048:2064]))
        p.wait_tok("vector", dks)
        p.wait_tok("vector", dkv)
        p.wait_tok("vector", dkv2)
        for dc in range(DC):
            p.pend("vector", lambda E, dc=dc:
                   E.tensor_copy(ksD[0:64, dc, 2 * dc:2 * dc + 1], kssb[0:64, dc:dc + 1]))
            p.pend("vector", lambda E, dc=dc:
                   E.tensor_copy(ksD[64:128, dc, 2 * dc + 1:2 * dc + 2],
                                 kssb[64:128, dc:dc + 1]))
        evD = p.close("vector")
        p.wait("tensor", "V", evD)
        p.wait("tensor", "SC")
        for th in range(2):
            for dc in range(DC):
                p.pend("tensor", lambda E, th=th, dc=dc:
                       E.matmul(PSB[0:32, th * 512:(th + 1) * 512], ksD[:, dc, :],
                                qfm[:, dc, th * 512:(th + 1) * 512],
                                start=(dc == 0), stop=(dc == DC - 1)))
        nm = p.close("tensor")
        p.wait("vector", "T", nm)
        p.step("vector", lambda E: E.tensor_scalar_max(PSB[0:32, 0:1024],
                                                       PSB[0:32, 0:1024], 1e-6))
        p.wait("vector", "V")
        evn = p.step("vector", lp(lambda E: E.reciprocal(rnbf, PSB[0:32, 0:1024])))
        p.wait("tensor", "V", evn)
        ao = bufB[:, :].rearrange("p (a b) -> p a b", a=16)
        qkv_guard = {0: 0, 1: 0}
        rnb_guard = 0
        for h in range(32):
            hp = h % 2
            ic = h // 2
            ps = PS0 if h % 2 == 0 else PS1
            for th in range(2):
                if rnb_guard:
                    p.wait("tensor", "V", rnb_guard)
                mb = p.step("tensor", lambda E, h=h, th=th:
                            E.matmul(PS2[0:64, :], selbc[:, h, :],
                                     rnbf[:, th * 512:(th + 1) * 512],
                                     start=True, stop=True))
                p.wait("vector", "T", mb)
                ev = p.step("vector", lambda E, th=th:
                            E.tensor_copy(rnb, PS2[0:64, :]))
                p.wait("tensor", "V", ev)
                p.wait("vector", "V", ev)
                if qkv_guard[h % 2]:
                    p.wait("tensor", "V", qkv_guard[h % 2])
                mq2 = p.step("tensor", lambda E, h=h, hp=hp, ic=ic, th=th, ps=ps:
                             E.matmul(ps[0:64, :] if th == 0 else ps[64:128, :],
                                      kvbf[64 * hp:64 * hp + 64, h * 64:(h + 1) * 64],
                                      qfm[64 * hp:64 * hp + 64, ic,
                                          th * 512:(th + 1) * 512],
                                      start=True, stop=True))
                p.wait("vector", "T", mq2)
                src_ = ps[0:64, :] if th == 0 else ps[64:128, :]
                ev2 = p.step("vector", lambda E, src_=src_, hp=hp, ic=ic, th=th:
                             E.tensor_mul(ao[64 * hp:64 * hp + 64, ic,
                                             th * 512:(th + 1) * 512],
                                          src_, rnb))
                qkv_guard[h % 2] = ev2
                rnb_guard = ev2
        aog = p.mark("V")
        p.wait("tensor", "V", aog)
        og = {}
        for dc in range(DC):
            par, ldo = load_col_chunks(ring, Wo, dc)
            psl = PSB[:, 0:1024] if dc % 2 == 0 else PSB[:, 1024:2048]
            p.wait_tok("tensor", ldo)
            if dc >= 2:
                p.wait("tensor", "V", og[dc - 2])
            for ic in range(16):
                for th in range(2):
                    p.pend("tensor", lambda E, psl=psl, ic=ic, th=th, par=par:
                           E.matmul(psl[:, th * 512:(th + 1) * 512],
                                    wring[:, par * 16 + ic, :],
                                    ao[:, ic, th * 512:(th + 1) * 512],
                                    start=(ic == 0), stop=(ic == DC - 1)))
            mo = p.close("tensor")
            ring.consumed(par, mo)
            p.wait("vector", "T", mo)
            for th in range(2):
                og[dc] = p.step("vector", lambda E, psl=psl, dc=dc, th=th:
                                E.scalar_tensor_tensor(
                                    x_res[:, dc, th * 512:(th + 1) * 512],
                                    psl[:, th * 512:(th + 1) * 512],
                                    bocol[:, dc:dc + 1],
                                    x_res[:, dc, th * 512:(th + 1) * 512],
                                    ALU.add, ALU.add))
        p.barrier()
        ABF.reset()
        AFA.reset()

    def l0_ffn():
        layernorm()
        ABF.reset()
        AFA.reset()
        uscol = AFA.alloc(128, 64)
        dscol = AFA.alloc(128, 16)
        os_sb = AFA.alloc(1, 1)
        tmpcv = AFA.alloc(64, 128)
        load_colvec(up_s, 64, uscol, tmpcv)
        load_colvec(dn_s, 16, dscol, tmpcv)
        dos = p.dma("sync", lambda E: E.dma_start(out=os_sb, in_=out_s[0:1]))
        p.wait_tok("tensor", dos)
        p.wait("tensor", "V", p.cnt["V"])
        mos = p.step("tensor", lambda E:
                     E.matmul(PS2[:, 0:1], ones_row[0:1, :], os_sb,
                              start=True, stop=True))
        p.wait("vector", "T", mos)
        p.wait_dma_all("vector", "sync")
        p.step("vector", lambda E:
               E.tensor_scalar_mul(dscol, dscol, PS2[:, 0:1]))

        hA = bufA[:, :].rearrange("p (a b) -> p a b", b=512)
        hB = bufB[:, :].rearrange("p (a b) -> p a b", b=512)

        def h_slot(fc):
            return (hA if fc < 32 else hB)[:, fc % 32, :]

        natr = ABF.alloc3(128, 2, 2048)

        def prep_chunks(par, src_dram, n128, cache_out):
            nslot = natr[:, par, 0:n128 * 128]
            ldn = p.dma("gpsimd", lambda E, nslot=nslot, src_dram=src_dram:
                        E.dma_start(out=nslot, in_=src_dram))
            p.wait_tok("vector", ldn)
            p.step("vector", lambda E, nslot=nslot:
                   E.tensor_scalar(nslot, nslot, 0.0, 2.0, ALU.is_gt, ALU.mult))
            p.wait("vector", "V")
            sg = p.step("vector", lambda E, nslot=nslot:
                        E.tensor_scalar_add(nslot, nslot, -1.0))
            p.wait("tensor", "V", sg)
            for half in range((n128 + 7) // 8):
                jn = min(8, n128 - half * 8)
                p.wait("tensor", "V", ptg[0])
                for j in range(jn):
                    p.pend("tensor", lambda E, nslot=nslot, half=half, j=j:
                           E.transpose(PT[:, j * 128:(j + 1) * 128],
                                       nslot[:, (8 * half + j) * 128:(8 * half + j + 1) * 128],
                                       ident_bf[:, :]))
                tdu = p.close("tensor")
                p.wait("vector", "T", tdu)
                ptg[0] = p.step(
                    "vector", lambda E, par=par, half=half, jn=jn:
                    E.tensor_copy(wring[:, par * 16 + half * 8:par * 16 + half * 8 + jn, :]
                                  .rearrange("p a b -> p (a b)"), PT[:, 0:jn * 128]))
            wvm = p.mark("V")
            p.wait("sync", "V", wvm)
            cd = p.dma("sync", lambda E, cache_out=cache_out, par=par:
                       E.dma_start(out=cache_out,
                                   in_=wring[:, par * 16:par * 16 + 16, :]))
            return p.mark("T"), cd, wvm

        ring = Ring(p)
        ptg = [0]
        cache_rd = {}
        evh = {}
        dvg = 0
        for th in range(2):
            build = (th == 0)
            natg = [0, 0]
            for fc in range(FC):
                if build:
                    parn = fc % 2
                    if natg[parn]:
                        p.wait("gpsimd", "T", natg[parn])
                    par = ring.next()
                    if cache_rd.get(par):
                        p.wait_tok("gpsimd", cache_rd[par])
                    tm, cd, wvm = prep_chunks(
                        par,
                        up_w[fc * 128:(fc + 1) * 128, :],
                        16,
                        upTc[fc, :, :, :].transpose([1, 0, 2]))
                    natg[parn] = p.mark("T")
                    cache_rd[par] = cd
                    p.wait("tensor", "V", wvm)
                else:
                    par = ring.next()
                    if cache_rd.get(par):
                        p.wait_tok("gpsimd", cache_rd[par])
                        cache_rd[par] = 0
                    ld = p.dma("gpsimd", lambda E, fc=fc, par=par:
                               E.dma_start(out=wring[:, par * 16:(par + 1) * 16, :],
                                           in_=upTc[fc, :, :, :].transpose([1, 0, 2])))
                    p.wait_tok("tensor", ld)
                psl = [PS0, PS1][fc % 2]
                if fc >= 2:
                    p.wait("tensor", "SC", evh[fc - 2])
                for dc in range(DC):
                    p.pend("tensor", lambda E, psl=psl, dc=dc, th=th, par=par:
                           E.matmul(psl[:, :], wring[:, par * 16 + dc, :],
                                    xln[:, dc, th * 512:(th + 1) * 512],
                                    start=(dc == 0), stop=(dc == DC - 1)))
                mh = p.close("tensor")
                ring.consumed(par, mh)
                p.wait("scalar", "T", mh)
                evh[fc] = p.step("scalar", lambda E, psl=psl, fc=fc:
                                 E.activation(h_slot(fc), psl[:, :], AF.Relu,
                                              scale=uscol[:, fc:fc + 1]))
            p.wait("tensor", "SC")
            for dc in range(DC):
                for g in range(4):
                    if build:
                        parn = (dc * 4 + g) % 2
                        if natg[parn]:
                            p.wait("gpsimd", "T", natg[parn])
                        par = ring.next()
                        if cache_rd.get(par):
                            p.wait_tok("gpsimd", cache_rd[par])
                        tm, cd, wvm = prep_chunks(
                            par,
                            dn_w[dc * 128:(dc + 1) * 128, g * 2048:(g + 1) * 2048],
                            16,
                            dnTc[dc, g * 16:(g + 1) * 16, :, :].transpose([1, 0, 2]))
                        natg[parn] = p.mark("T")
                        cache_rd[par] = cd
                        p.wait("tensor", "V", wvm)
                    else:
                        par = ring.next()
                        if cache_rd.get(par):
                            p.wait_tok("gpsimd", cache_rd[par])
                            cache_rd[par] = 0
                        ld = p.dma("gpsimd", lambda E, dc=dc, g=g, par=par:
                                   E.dma_start(out=wring[:, par * 16:(par + 1) * 16, :],
                                               in_=dnTc[dc, g * 16:(g + 1) * 16, :, :]
                                               .transpose([1, 0, 2])))
                        p.wait_tok("tensor", ld)
                    if g == 0 and dvg:
                        p.wait("tensor", "V", dvg)
                    for j in range(16):
                        fc = g * 16 + j
                        p.pend("tensor", lambda E, fc=fc, j=j, par=par:
                               E.matmul(PS2[:, :], wring[:, par * 16 + j, :], h_slot(fc),
                                        start=(fc == 0), stop=(fc == FC - 1)))
                    mdg = p.close("tensor")
                    ring.consumed(par, mdg)
                md = p.mark("T")
                p.wait("vector", "T", md)
                dvg = p.step("vector", lambda E, dc=dc, th=th:
                             E.scalar_tensor_tensor(x_res[:, dc, th * 512:(th + 1) * 512],
                                                    PS2[:, :], dscol[:, dc:dc + 1],
                                                    x_res[:, dc, th * 512:(th + 1) * 512],
                                                    ALU.mult, ALU.add))
        p.barrier()
        ABF.reset()
        AFA.reset()

    def l1_attn():
        layernorm()
        ABF.reset()
        AFA.reset()
        res_kT = ABF.alloc(64, 2048)
        res_vT = ABF.alloc(64, 2048)
        res_q = ABF.alloc(64, 1024)
        basTo = ABF.alloc3(128, 16, 64)
        sums = AFA.alloc(128, 8)
        recip = AFA.alloc(128, 8)
        negmax = AFA.alloc(128, 1)
        tmpa = AFA.alloc(128, 512)
        tmpp = AFA.alloc(128, 512)
        mark0 = ABF.off
        wnat = ABF.alloc3(128, 8, 64)
        basn = ABF.alloc3(64, 16, 128)
        basTt = ABF.alloc3(128, 16, 64)
        pmh = ABF.alloc(128, 1024)
        exr = ABF.alloc(128, 2048)
        wT = bufB[0:64, 8192:16384].rearrange("p (j a c) -> p j a c", j=4, a=16)
        AT = bufB[:, 0:8192].rearrange("p (a c) -> p a c", a=16)
        qfm = bufA[:, :].rearrange("p (a b) -> p a b", a=16)
        ao = qfm
        for j, nm in enumerate("qkvo"):
            scl = (1.0 / float(np.sqrt(128.0))) if nm == "q" else 1.0
            for half in range(2):
                rows = slice(half * 1024, (half + 1) * 1024)
                p.wait("sync", "V", p.cnt["V"])
                lda = p.dma("sync", lambda E, nm=nm, rows=rows: E.dma_start(
                    out=tmpa.rearrange("p (a c) -> p a c", a=8),
                    in_=amp_in[nm][rows, :].rearrange("(a p) c -> p a c", p=128)))
                ldp = p.dma("sync", lambda E, nm=nm, rows=rows: E.dma_start(
                    out=tmpp.rearrange("p (a c) -> p a c", a=8),
                    in_=phase_in[nm][rows, :].rearrange("(a p) c -> p a c", p=128)))
                p.wait_tok("scalar", ldp)
                p.wait("scalar", "V", p.cnt["V"])
                sn = p.step("scalar", lambda E: E.activation(
                    tmpp, tmpp, AF.Sin, bias=pio2[:, :]))
                p.wait_tok("vector", lda)
                p.wait("scalar", "SC", sn)
                p.wait("vector", "SC", sn)
                p.wait("vector", "T", p.cnt["T"])
                wv = p.step("vector", lp(lambda E, scl=scl: E.scalar_tensor_tensor(
                    wnat.rearrange("p a c -> p (a c)"), tmpp, scl, tmpa,
                    ALU.mult, ALU.mult)))
                p.wait("tensor", "V", wv)
                for d8 in range(8):
                    p.pend("tensor", lambda E, d8=d8:
                           E.transpose(PT[0:64, d8 * 128:(d8 + 1) * 128],
                                       wnat[:, d8, :], ident_bf[:, :]))
                tw = p.close("tensor")
                p.wait("vector", "T", tw)
                p.step("vector", lambda E, j=j, half=half:
                       E.tensor_copy(wT[0:64, j, half * 8:(half + 1) * 8, :]
                                     .rearrange("p a c -> p (a c)"), PT[0:64, :]))
        kv_res_mark = {}
        for j, nm in enumerate("qkvo"):
            p.wait("gpsimd", "T", p.cnt["T"])
            ldb = p.dma("gpsimd", lambda E, nm=nm: E.dma_start(
                out=basn.rearrange("p a c -> p (a c)"), in_=basis_in[nm][:, :]))
            p.wait_tok("tensor", ldb)
            p.wait("tensor", "V", p.cnt["V"])
            for dc in range(DC):
                p.pend("tensor", lambda E, dc=dc:
                       E.transpose(PT[:, dc * 64:(dc + 1) * 64],
                                   basn[:, dc, :], ident_bf[0:64, 0:64]))
            tb = p.close("tensor")
            p.wait("vector", "T", tb)
            dstT = basTo if nm == "o" else basTt
            bv = p.step("vector", lambda E, dstT=dstT:
                        E.tensor_copy(dstT.rearrange("p a c -> p (a c)"), PT[:, :]))
            if nm == "o":
                continue
            p.wait("tensor", "V", bv)
            for th in range(2):
                psl = PSB[:, th * 512:(th + 1) * 512]
                for dc in range(DC):
                    if nm == "q":
                        out = psl[0:64, :]
                    elif nm == "k":
                        out = psl[0:64, :]
                    else:
                        out = psl[64:128, :]
                    p.pend("tensor", lambda E, out=out, dc=dc, th=th:
                           E.matmul(out, basTt[:, dc, :],
                                    xln[:, dc, th * 512:(th + 1) * 512],
                                    start=(dc == 0), stop=(dc == DC - 1)))
                kv_res_mark[(nm, th)] = p.close("tensor")
            if nm == "q":
                m = kv_res_mark[("q", 1)]
                p.wait("vector", "T", m)
                p.step("vector", lambda E:
                       E.tensor_copy(res_q[:, 0:512], PSB[0:64, 0:512]))
                qv = p.step("vector", lambda E:
                            E.tensor_copy(res_q[:, 512:1024], PSB[0:64, 512:1024]))
                p.wait("tensor", "V", qv)
        ldm0 = p.dma("sync", lambda E: E.dma_start(out=pmh, in_=pmask_in[:, 0:1024]))
        m = kv_res_mark[("v", 1)]
        p.wait("vector", "T", m)
        p.step("vector", lambda E: E.tensor_copy(exr[:, 0:1024], PSB[:, 0:1024]))
        cv = p.step("vector", lambda E: E.tensor_copy(exr[:, 1024:2048], PSB[:, 0:1024]))
        p.wait_tok("vector", ldm0)
        p.wait("vector", "V", cv)
        m0 = p.step("vector", lambda E: E.tensor_mul(exr[:, 0:1024], exr[:, 0:1024], pmh))
        p.wait("sync", "V", m0)
        ldm1 = p.dma("sync", lambda E: E.dma_start(out=pmh, in_=pmask_in[:, 1024:2048]))
        p.wait_tok("vector", ldm1)
        p.wait("vector", "V", p.cnt["V"])
        mv = p.step("vector", lambda E:
                    E.tensor_mul(exr[:, 1024:2048], exr[:, 1024:2048], pmh))
        p.wait("sync", "V", mv)
        dex = p.dma("sync", lambda E: E.dma_start(out=cc_res_in[:, :], in_=exr))
        p.wait_dma_all("gpsimd", "sync")
        ccr = p.inc("gpsimd", "CC", lambda E:
                    E.collective_compute("AllReduce", ALU.add,
                                         replica_groups=[[0, 1], [2, 3], [4, 5], [6, 7]],
                                         ins=[cc_res_in[:, :]], outs=[cc_res_out[:, :]]))
        qg = {0: 0, 1: 0}
        for ic in range(16):
            ps = PS0 if ic % 2 == 0 else PS1
            for th in range(2):
                if qg[ic % 2]:
                    p.wait("tensor", "V", qg[ic % 2])
                mqe = p.step("tensor", lambda E, ic=ic, th=th, ps=ps:
                             E.matmul(ps[:, :],
                                      wT[0:64, 0, ic, :],
                                      res_q[:, th * 512:(th + 1) * 512],
                                      start=True, stop=True))
                p.wait("vector", "T", mqe)
                qg[ic % 2] = p.step("vector", lambda E, ic=ic, th=th, ps=ps:
                                    E.tensor_copy(qfm[:, ic, th * 512:(th + 1) * 512],
                                                  ps[:, :]))
        p.wait("sync", "CC", ccr)
        dk = p.dma("sync", lambda E: E.dma_start(out=res_kT, in_=cc_res_out[0:64, :]))
        dv = p.dma("sync", lambda E: E.dma_start(out=res_vT, in_=cc_res_out[64:128, :]))
        p.wait_tok("tensor", dk)
        p.wait_tok("tensor", dv)
        p.wait("tensor", "V", p.cnt["V"])
        ABF.off = mark0
        Kh = ABF.alloc(128, 2048)
        Vtok = ABF.alloc3(128, 16, 128)
        AT = bufB[:, 0:4096].rearrange("p (a c) -> p a c", a=16)
        AT2 = bufB[:, 8192:12288].rearrange("p (a c) -> p a c", a=16)
        Aexp2 = bufB[:, 4096:8192].rearrange("p (r c) -> p r c", r=2)
        negmax2 = negmax
        negmaxB = AFA.alloc(128, 1)
        pump_state = {"i": 0}
        pump_guard = {}

        def pump_step():
            i = pump_state["i"]
            if i >= 128:
                return
            pump_state["i"] += 1
            par = i % 2
            if pump_guard.get(par):
                p.wait_tok("gpsimd", pump_guard[par])
            if i < 64:
                fc = i
                tk = p.dma("gpsimd", lambda E, fc=fc, par=par: E.dma_start(
                    out=wring[:, par * 16:(par + 1) * 16, :],
                    in_=W1[:, fc * 128:(fc + 1) * 128]
                    .rearrange("(a p) c -> p a c", p=128)))
                p.wait_tok("sync", tk)
                pump_guard[par] = p.dma("sync", lambda E, fc=fc, par=par: E.dma_start(
                    out=w1c[fc, :, :, :].transpose([1, 0, 2]),
                    in_=wring[:, par * 16:(par + 1) * 16, :]))
            else:
                k = i - 64
                dc, q = k // 4, k % 4
                tk = p.dma("gpsimd", lambda E, dc=dc, q=q, par=par: E.dma_start(
                    out=wring[:, par * 16:(par + 1) * 16, :],
                    in_=W2[q * 2048:(q + 1) * 2048, dc * 128:(dc + 1) * 128]
                    .rearrange("(a p) c -> p a c", p=128)))
                p.wait_tok("sync", tk)
                pump_guard[par] = p.dma("sync", lambda E, dc=dc, q=q, par=par:
                                        E.dma_start(
                                            out=w2c[dc, q * 16:(q + 1) * 16, :, :]
                                            .transpose([1, 0, 2]),
                                            in_=wring[:, par * 16:(par + 1) * 16, :]))

        khg = [0]
        scg = {0: 0, 1: 0}
        mxg = {0: 0, 1: 0}
        nrm = {0: 0, 1: 0}
        avg = {0: 0, 1: 0}
        av_dn = {0: 0, 1: 0}
        tp_toks = {}
        ptev = [0]
        trq = {0: 0, 1: 0}
        an_s = {0: 0, 1: 0}
        for h in range(16):
            for _ in range(8):
                pump_step()
            for k4 in range(4):
                ps = PS0 if k4 % 2 == 0 else PS1
                if avg[k4 % 2]:
                    p.wait("tensor", "V", avg[k4 % 2])
                mk = p.step("tensor", lambda E, h=h, k4=k4, ps=ps:
                            E.matmul(ps[:, :], wT[0:64, 1, h, :],
                                     res_kT[:, k4 * 512:(k4 + 1) * 512],
                                     start=True, stop=True))
                p.wait("vector", "T", mk)
                avg[k4 % 2] = p.step("vector", lambda E, h=h, k4=k4, ps=ps:
                                     E.tensor_copy(Kh[:, k4 * 512:(k4 + 1) * 512],
                                                   ps[:, :]))
            khg[0] = avg[1]
            for grp in range(4):
                if grp > 0 or h > 0:
                    p.wait("tensor", "V", p.cnt["V"])
                for jj in range(4):
                    kc = grp * 4 + jj
                    p.pend("tensor", lambda E, h=h, kc=kc, jj=jj:
                           E.matmul(PS2[:, jj * 128:(jj + 1) * 128],
                                    res_vT[:, kc * 128:(kc + 1) * 128],
                                    wT[0:64, 2, h, :],
                                    start=True, stop=True))
                mv2 = p.close("tensor")
                p.wait("vector", "T", mv2)
                p.step("vector", lambda E, grp=grp:
                       E.tensor_copy(Vtok[:, grp * 4:(grp + 1) * 4, :]
                                     .rearrange("p a c -> p (a c)"), PS2[:, :]))
            vtg = p.mark("V")
            p.wait("tensor", "V", vtg)

            def emit_sc(qt, h=h):
                ar = qt % 2
                nmb = negmax if ar == 0 else negmaxB
                for sl_ in (0, 1):
                    if scg[sl_]:
                        p.wait("tensor", "SC", scg[sl_])
                    if mxg[sl_]:
                        p.wait("tensor", "V", mxg[sl_])
                for k4 in range(4):
                    p.pend("tensor", lambda E, h=h, qt=qt, k4=k4:
                           E.matmul(PSB[:, k4 * 512:(k4 + 1) * 512],
                                    qfm[:, h, qt * 128:(qt + 1) * 128],
                                    Kh[:, k4 * 512:(k4 + 1) * 512],
                                    start=True, stop=True))
                sm = p.close("tensor")
                p.wait("vector", "T", sm)
                mxg[ar] = p.step("vector", lambda E, nmb=nmb: E.reduce_max(
                    nmb, PSB[:, :], axis=AX.X, negate=True))
                p.wait("scalar", "V", mxg[ar])
                if trq[ar]:
                    p.wait("scalar", "T", trq[ar])
                ex = p.step("scalar", lambda E, qt=qt, ar=ar, nmb=nmb:
                            E.activation(Aexp2[:, ar, :], PSB[:, :], AF.Exp,
                                         bias=nmb,
                                         accum_out=sums[:, qt:qt + 1]))
                scg[ar] = ex
                p.wait("vector", "SC", ex)
                p.step("vector", lambda E, qt=qt:
                       E.reciprocal(recip[:, qt:qt + 1], sums[:, qt:qt + 1]))
                p.wait("vector", "V", p.cnt["V"])
                an_s[ar] = p.step("vector", lp(lambda E, qt=qt, ar=ar:
                                  E.tensor_scalar_mul(Aexp2[:, ar, :],
                                                      Aexp2[:, ar, :],
                                                      recip[:, qt:qt + 1])))

            def emit_tp(jq):
                ar = jq % 2
                qx = jq % 2
                qq = jq // 2
                at_b = AT if qq % 2 == 0 else AT2
                evs = []
                tr = 0
                for r in range(2):
                    p.wait("tensor", "V", an_s[ar])
                    if ptev[0]:
                        p.wait("tensor", "V", ptev[0])
                    for j in range(8):
                        kc = r * 8 + j
                        p.pend("tensor", lambda E, kc=kc, j=j, ar=ar:
                               E.transpose(PT[:, j * 128:(j + 1) * 128],
                                           Aexp2[:, ar,
                                                 kc * 128:(kc + 1) * 128],
                                           ident_bf[:, :]))
                    tr = p.close("tensor")
                    p.wait("vector", "T", tr)
                    if av_dn[qq % 2]:
                        p.wait("vector", "T", av_dn[qq % 2])
                    ev = p.step("vector", lambda E, r=r, qx=qx, at_b=at_b:
                                E.tensor_copy(
                                    at_b[:, r * 8:(r + 1) * 8,
                                         qx * 128:(qx + 1) * 128],
                                    PT[:, :].rearrange(
                                        "p (a c) -> p a c", a=8)))
                    ptev[0] = ev
                    evs.append(ev)
                trq[ar] = tr
                tp_toks[jq] = evs

            def emit_av(qq, h=h):
                at_b = AT if qq % 2 == 0 else AT2
                for ev in tp_toks.get(qq * 2, ()):
                    p.wait("tensor", "V", ev)
                for ev in tp_toks.get(qq * 2 + 1, ()):
                    p.wait("tensor", "V", ev)
                ps = PS0 if qq % 2 == 0 else PS1
                if avg[qq % 2]:
                    p.wait("tensor", "V", avg[qq % 2])
                for kc in range(16):
                    p.pend("tensor", lambda E, kc=kc, ps=ps, at_b=at_b:
                           E.matmul(ps[:, 0:256], Vtok[:, kc, :],
                                    at_b[:, kc, :],
                                    start=(kc == 0), stop=(kc == 15)))
                av = p.close("tensor")
                av_dn[qq % 2] = av
                p.wait("vector", "T", av)
                avg[qq % 2] = p.step("vector", lambda E, h=h, qq=qq, ps=ps:
                                     E.tensor_copy(
                                         ao[:, h, qq * 256:(qq + 1) * 256],
                                         ps[:, 0:256]))

            for qt in range(8):
                emit_sc(qt)
                if qt >= 1:
                    emit_tp(qt - 1)
                if qt >= 2 and qt % 2 == 0:
                    emit_av((qt - 2) // 2)
            emit_tp(7)
            emit_av(3)
        reso = res_q
        p.wait("tensor", "V", p.cnt["V"])
        for th in range(2):
            for ic in range(16):
                p.pend("tensor", lambda E, ic=ic, th=th:
                       E.matmul(PSB[0:64, th * 512:(th + 1) * 512], basTo[:, ic, :],
                                ao[:, ic, th * 512:(th + 1) * 512],
                                start=(ic == 0), stop=(ic == DC - 1)))
            ro = p.close("tensor")
            p.wait("vector", "T", ro)
            rv = p.step("vector", lambda E, th=th:
                        E.tensor_copy(reso[:, th * 512:(th + 1) * 512],
                                      PSB[0:64, th * 512:(th + 1) * 512]))
            p.wait("tensor", "V", rv)
        fg = {0: 0, 1: 0}
        for dc in range(DC):
            ps = PS0 if dc % 2 == 0 else PS1
            for th in range(2):
                if fg[dc % 2]:
                    p.wait("tensor", "V", fg[dc % 2])
                fm = p.step("tensor", lambda E, dc=dc, th=th, ps=ps:
                            E.matmul(ps[:, :], wT[0:64, 3, dc, :],
                                     reso[:, th * 512:(th + 1) * 512],
                                     start=True, stop=True))
                p.wait("vector", "T", fm)
                fg[dc % 2] = p.step("vector", lambda E, dc=dc, th=th, ps=ps:
                                    E.tensor_add(x_res[:, dc, th * 512:(th + 1) * 512],
                                                 x_res[:, dc, th * 512:(th + 1) * 512],
                                                 ps[:, :]))
        p.barrier()
        ABF.reset()
        AFA.reset()

    def l1_ffn():
        layernorm()
        ABF.reset()
        AFA.reset()
        b1col = AFA.alloc(128, 64)
        lgrow = AFA.alloc3(8, 2, 512)
        lgT = AFA.alloc3(128, 8, 8)
        top8 = AFA.alloc(128, 8)
        nm8 = AFA.alloc(128, 1)
        prb = AFA.alloc(128, 8)
        crtmp = AFA.alloc(16, 128)
        tmpcv2 = AFA.alloc(64, 128)
        brcol = AFA.alloc(8, 1)
        a2r = ABF.alloc3(128, 2, 1024)
        wrch = ABF.alloc3(128, 16, 8)
        crt = ABF.alloc3(1, 2, 1024)
        csb = ABF.alloc(1, 1024)
        b2row = ABF.alloc(1, 2048)
        gtmp2 = ABF.alloc3(128, 2, 1024)
        base_r = ABF.alloc3(128, 4, 512)
        t1sb = bufB[0:16, 0:4096].rearrange("p (a c) -> p a c", a=8)
        t2accA = bufB[32:48, 0:2048].rearrange("p (a c) -> p a c", a=4)
        t2accB = bufB[64:80, 0:2048].rearrange("p (a c) -> p a c", a=4)
        b1r = bufB[0:16, 4096:12288].rearrange("p (a b c) -> p a b c", a=8, b=8)
        cbe = bufB[:, 12288:16384].rearrange("p (t a c) -> p t a c", t=2, a=8)
        t2all = bufA[:, 1024:1536]
        b2stk = bufA[:, 1536:1792].rearrange("p (r c) -> p r c", r=2)
        aetA = bufA[:, 2048:4096].rearrange("p (r c) -> p r c", r=2)
        aetB = bufA[:, 4096:6144].rearrange("p (r c) -> p r c", r=2)
        caeq = bufA[:, 6144:10240].rearrange("p (r a c) -> p r a c", r=2, a=8)
        a1T = bufA[:, 10240:12288].rearrange("p (a c) -> p a c", a=16)
        mixR = bufA[:, 12288:16384].rearrange("p (a c) -> p a c", a=8)
        load_colvec(b1_in, 64, b1col, tmpcv2)
        dbr = p.dma("sync", lambda E: E.dma_start(
            out=brcol, in_=br_in.rearrange("(a b) -> a b", b=1)))
        db2 = p.dma("gpsimd", lambda E: E.dma_start(
            out=b2row, in_=b2_in.rearrange("(a b) -> a b", a=1)))
        dwr = p.dma("gpsimd", lambda E: E.dma_start(
            out=wrch[:, :, :], in_=Wr.rearrange("(a p) c -> p a c", p=128)))
        a1nat = bufA[:, 2048:4096]
        da1 = p.dma("gpsimd", lambda E: E.dma_start(
            out=a1nat, in_=A1.rearrange("e r d -> (e r) d")))
        p.wait_tok("tensor", da1)
        for rnd in range(2):
            p.wait("tensor", "V", p.cnt["V"])
            for jj in range(8):
                p.pend("tensor", lambda E, rnd=rnd, jj=jj:
                       E.transpose(PT[:, jj * 128:(jj + 1) * 128],
                                   a1nat.rearrange("p (a c) -> p a c", a=16)
                                   [:, rnd * 8 + jj, :], ident_bf[:, :]))
            ta = p.close("tensor")
            p.wait("vector", "T", ta)
            p.step("vector", lambda E, rnd=rnd:
                   E.tensor_copy(a1T[:, rnd * 8:(rnd + 1) * 8, :]
                                 .rearrange("p a c -> p (a c)"), PT[:, :]))
        bnat = bufA[:, 2048:3072].rearrange("p (a c) -> p a c", a=64)
        stg16 = bufA[0:16, 3072:4096]
        for e in range(8):
            p.wait("gpsimd", "T", p.cnt["T"])
            p.wait("gpsimd", "V", p.cnt["V"])
            p.wait_dma_all("gpsimd", "sync")
            dbn = p.dma("gpsimd", lambda E, e=e: E.dma_start(
                out=bnat.rearrange("p a c -> p (a c)"),
                in_=B1[e, :, :].rearrange("(a p) c -> p a c", p=128)))
            p.wait_tok("tensor", dbn)
            for blk in range(8):
                p.wait("tensor", "V", p.cnt["V"])
                for jj in range(8):
                    p.pend("tensor", lambda E, blk=blk, jj=jj:
                           E.transpose(PT[0:16, jj * 128:(jj + 1) * 128],
                                       bnat[:, blk * 8 + jj, :], ident_bf[:, :]))
                tb1 = p.close("tensor")
                p.wait("vector", "T", tb1)
                p.wait_dma_all("vector", "sync")
                sv = p.step("vector", lambda E: E.tensor_copy(stg16, PT[0:16, :]))
                p.wait("sync", "V", sv)
                p.dma("sync", lambda E, e=e, blk=blk: E.dma_start(
                    out=b1tc[blk, :, e * 1024:(e + 1) * 1024], in_=stg16))
        a2stg = bufA[:, 2048:4096]
        aev1 = bufA[:, 4096:4608]
        aev2 = bufA[:, 4608:5120]
        for half in range(4):
            p.wait("gpsimd", "T", p.cnt["T"])
            p.wait("gpsimd", "V", p.cnt["V"])
            p.wait_dma_all("gpsimd", "sync")
            dan = p.dma("gpsimd", lambda E, half=half: E.dma_start(
                out=a2stg, in_=A2.rearrange("e r f -> (e r) f")
                [:, half * 2048:(half + 1) * 2048]))
            p.wait_tok("tensor", dan)
            for blk in range(2):
                p.wait("tensor", "V", p.cnt["V"])
                for jj in range(8):
                    p.pend("tensor", lambda E, blk=blk, jj=jj:
                           E.transpose(PT[:, jj * 128:(jj + 1) * 128],
                                       a2stg.rearrange("p (a c) -> p a c", a=16)
                                       [:, blk * 8 + jj, :], ident_bf[:, :]))
                ta2 = p.close("tensor")
                p.wait("vector", "T", ta2)
                p.wait_dma_all("vector", "sync")
                sv = p.step("vector", lambda E: E.tensor_copy(aev1, PT[:, 0:512]))
                sv2 = p.step("vector", lambda E: E.tensor_copy(aev2, PT[:, 512:1024]))
                p.wait("sync", "V", sv2)
                for q4 in range(8):
                    fcx = half * 16 + blk * 8 + q4
                    srcb = aev1 if q4 < 4 else aev2
                    p.dma("sync", lambda E, fcx=fcx, srcb=srcb, q4=q4: E.dma_start(
                        out=a2tc[fcx, :, :],
                        in_=srcb.rearrange("p (a c) -> p a c", a=4)[:, q4 % 4, :]))
        b2nat = bufA[:, 2048:2304].rearrange("p (a c) -> p a c", a=16)
        for e in range(8):
            p.wait("gpsimd", "T", p.cnt["T"])
            p.wait("gpsimd", "V", p.cnt["V"])
            p.wait_dma_all("gpsimd", "sync")
            dbn = p.dma("gpsimd", lambda E, e=e: E.dma_start(
                out=b2nat.rearrange("p a c -> p (a c)"),
                in_=B2[e, :, :].rearrange("(a p) c -> p a c", p=128)))
            p.wait_tok("tensor", dbn)
            for blk in range(2):
                p.wait("tensor", "V", p.cnt["V"])
                for jj in range(8):
                    p.pend("tensor", lambda E, blk=blk, jj=jj:
                           E.transpose(PT[0:16, jj * 128:(jj + 1) * 128],
                                       b2nat[:, blk * 8 + jj, :], ident_bf[:, :]))
                tb2 = p.close("tensor")
                p.wait("vector", "T", tb2)
                p.wait_dma_all("vector", "sync")
                sv = p.step("vector", lambda E: E.tensor_copy(stg16, PT[0:16, :]))
                p.wait("sync", "V", sv)
                p.dma("sync", lambda E, e=e, blk=blk: E.dma_start(
                    out=b2tc[blk * 8:(blk + 1) * 8, e, :, :].transpose([1, 0, 2]),
                    in_=stg16.rearrange("p (a c) -> p a c", a=8)))
        p.wait_dma_all("tensor", "sync")
        p.wait_tok("tensor", dwr)
        p.wait_tok("scalar", dbr)
        for th in range(2):
            for dc in range(DC):
                p.pend("tensor", lambda E, th=th, dc=dc:
                       E.matmul(PSB[0:8, th * 512:(th + 1) * 512], wrch[:, dc, :],
                                xln[:, dc, th * 512:(th + 1) * 512],
                                start=(dc == 0), stop=(dc == DC - 1)))
            rm = p.close("tensor")
            p.wait("scalar", "T", rm)
            p.step("scalar", lambda E, th=th: E.activation(
                lgrow[:, th, :], PSB[0:8, th * 512:(th + 1) * 512], AF.Identity,
                bias=brcol))
        p.wait("tensor", "SC")
        p.wait("tensor", "V", p.cnt["V"])
        for tt in range(8):
            p.pend("tensor", lambda E, tt=tt:
                   E.transpose(PS2[:, tt * 8:(tt + 1) * 8],
                               lgrow.rearrange("p a c -> p (a c)")
                               [0:8, tt * 128:(tt + 1) * 128], ident[0:8, 0:8]))
        tl = p.close("tensor")
        p.wait("vector", "T", tl)
        p.step("vector", lambda E: E.tensor_copy(
            lgT.rearrange("p a c -> p (a c)"), PS2[:, 0:64]))
        for tt in range(8):
            p.wait("vector", "V", p.cnt["V"])
            p.step("vector", lambda E, tt=tt: E.reduce_max(
                nm8, lgT[:, tt, :], axis=AX.X, negate=True))
            p.wait("scalar", "V", p.cnt["V"])
            exm = p.step("scalar", lambda E, tt=tt: E.activation(
                prb, lgT[:, tt, :], AF.Exp, bias=nm8, accum_out=nm8))
            p.wait("vector", "SC", exm)
            p.step("vector", lambda E: E.reciprocal(nm8, nm8))
            p.wait("vector", "V", p.cnt["V"])
            p.step("vector", lambda E: E.tensor_scalar_mul(prb, prb, nm8))
            p.wait("vector", "V", p.cnt["V"])
            p.step("vector", lambda E: E.max(top8, prb))
            p.wait("vector", "V", p.cnt["V"])
            p.step("vector", lambda E, tt=tt: E.tensor_scalar(
                lgT[:, tt, :], prb, top8[:, 1:2], None, ALU.is_ge, ALU.bypass))
            p.wait("vector", "V", p.cnt["V"])
            p.step("vector", lambda E, tt=tt: E.tensor_mul(lgT[:, tt, :],
                                                           lgT[:, tt, :], prb))
            p.wait("tensor", "V", p.cnt["V"])
            t1m = p.step("tensor", lambda E, tt=tt:
                         E.transpose(PS2[0:8, 0:128], lgT[:, tt, :], ident[:, :]))
            p.wait("vector", "T", t1m)
            p.wait_dma_all("vector", "sync")
            sv = p.step("vector", lambda E: E.tensor_copy(crtmp[0:8, :],
                                                          PS2[0:8, 0:128]))
            p.wait("sync", "V", sv)
            p.dma("sync", lambda E, tt=tt: E.dma_start(
                out=crows_d[0:8, tt * 128:(tt + 1) * 128], in_=crtmp[0:8, :]))
        p.wait_dma_all("gpsimd", "sync")
        first = [True]
        for e in range(8):
            rr = e % 2
            p.wait("gpsimd", "V", p.cnt["V"])
            dcr = p.dma("gpsimd", lambda E, e=e, rr=rr: E.dma_start(
                out=crt[:, rr, :], in_=crows_d[e:e + 1, :]))
            p.wait_tok("vector", dcr)
            p.wait("vector", "V", p.cnt["V"])
            if first[0]:
                p.step("vector", lambda E, rr=rr: E.tensor_copy(csb, crt[0:1, rr, :]))
                first[0] = False
            else:
                p.step("vector", lp(lambda E, rr=rr: E.tensor_add(
                    csb, csb, crt[0:1, rr, :])))
        wring_r = Ring(p, "sync")
        state = dict(basem=0, qidx=0)
        base_sl = [0, 0, 0, 0]
        l1g = {0: 0, 1: 0}
        geg = {0: 0, 1: 0}
        cag = {0: 0, 1: 0}
        t2g = {0: 0, 1: 0}
        trg = {0: 0, 1: 0}
        xg = {0: 0, 1: 0}
        pend_ca = []
        sh_prev = []
        PB2 = PSB[:, 1024:1536]
        PB3 = PSB[:, 1536:2048]
        QSET = [(PS0, PS1), (PB2, PB3)]

        def t2reg(e):
            if e < 4:
                return PSB[32:48, e * 256:e * 256 + 256]
            return PSB[64:80, (e - 4) * 256:(e - 4) * 256 + 256]

        def drain_quad(item):
            (j8, eq, tq, sl, cslot, gm, tqs_i, rrA_i, R_i) = item
            if "gp" in L1F_PARTS:
                p.wait("vector", "SC", gm)
                if trg[cslot]:
                    p.wait("vector", "G", trg[cslot])
                cm = p.step("vector", lp(
                    lambda E, eq=eq, sl=sl, cslot=cslot, tq=tq:
                    E.tensor_mul(caeq[:, cslot, eq * 4:(eq + 1) * 4, :]
                                 .rearrange("p a c -> p (a c)"),
                                 aetB[:, sl, :],
                                 cbe[:, tq, eq * 4:(eq + 1) * 4, :]
                                 .rearrange("p a c -> p (a c)"))))
                cag[sl] = cm
            if "t2" in L1F_PARTS:
                p.wait("tensor", "SC", gm)
                for ee in range(4):
                    e = eq * 4 + ee
                    st = (j8 == 0 and e in (0, 2, 4, 6))
                    sp = (j8 == 7 and e in (1, 3, 5, 7))
                    p.pend("tensor", lambda E, e=e, ee=ee, j8=j8, rrA_i=rrA_i,
                           sl=sl, st=st, sp=sp:
                           E.matmul(t2reg(e),
                                    a2r[:, rrA_i, :]
                                    .rearrange("p (a c) -> p a c", a=8)
                                    [:, j8, e * 16:(e + 1) * 16],
                                    aetB[:, sl, ee * 256:(ee + 1) * 256],
                                    start=st, stop=sp))
                t2g[sl] = p.close("tensor")
            if eq == 1 and "gp" in L1F_PARTS:
                p.wait("gpsimd", "V", cm)
                if trg[cslot]:
                    p.wait("gpsimd", "G", trg[cslot])
                s1 = p.step("gpsimd", lp(lambda E, cslot=cslot: E.tensor_add(
                    gtmp2[:, cslot, 0:1024],
                    caeq[:, cslot, 0:4, :].rearrange("p a c -> p (a c)"),
                    caeq[:, cslot, 4:8, :].rearrange("p a c -> p (a c)"))))
                p.wait("gpsimd", "G", s1)
                s2 = p.step("gpsimd", lp(lambda E, cslot=cslot: E.tensor_add(
                    gtmp2[:, cslot, 0:512], gtmp2[:, cslot, 0:512],
                    gtmp2[:, cslot, 512:1024])))
                p.wait("gpsimd", "G", s2)
                s3 = p.step("gpsimd", lp(lambda E, j8=j8, tqs_i=tqs_i, cslot=cslot:
                            E.tensor_add(mixR[:, j8, tqs_i],
                                         gtmp2[:, cslot, 0:256],
                                         gtmp2[:, cslot, 256:512])))
                trg[cslot] = s3

        for th in range(2):
            tsl5 = slice(th * 512, (th + 1) * 512)
            for e in range(8 if "t1" in L1F_PARTS else 0):
                if e == 0:
                    p.wait("tensor", "V", p.cnt["V"])
                for dc in range(DC):
                    p.pend("tensor", lambda E, e=e, dc=dc, tsl5=tsl5:
                           E.matmul(PS2[0:16, :], a1T[:, dc, e * 16:(e + 1) * 16],
                                    xln[:, dc, tsl5],
                                    start=(dc == 0), stop=(dc == DC - 1)))
                t1m = p.close("tensor")
                p.wait("vector", "T", t1m)
                tv = p.step("vector", lambda E, e=e:
                            E.tensor_copy(t1sb[:, e, :], PS2[0:16, :]))
                p.wait("tensor", "V", tv)
            for R in range(8):
                fc0 = R * 8
                rrA = R % 2
                p.wait("gpsimd", "T", p.cnt["T"])
                da2 = p.dma("gpsimd", lambda E, fc0=fc0, rrA=rrA: E.dma_start(
                    out=a2r[:, rrA, :].rearrange("p (a c) -> p a c", a=8),
                    in_=a2tc[fc0:fc0 + 8, :, :].transpose([1, 0, 2])))
                p.wait("gpsimd", "T", p.cnt["T"])
                db1 = p.dma("gpsimd", lambda E, R=R: E.dma_start(
                    out=b1r.rearrange("p a b c -> p (a b c)"), in_=b1tc[R, :, :]))
                for j8 in range(8 if "base" in L1F_PARTS else 0):
                    fc = fc0 + j8
                    par = wring_r.next()
                    ldw = p.dma("sync", lambda E, fc=fc, par=par: E.dma_start(
                        out=wring[:, par * 16:(par + 1) * 16, :],
                        in_=w1c[fc, :, :, :].transpose([1, 0, 2])))
                    p.wait_tok("tensor", ldw)
                    bsl = j8 % 4
                    if base_sl[bsl]:
                        p.wait("tensor", "V", base_sl[bsl])
                    if state["basem"]:
                        p.wait("tensor", "SC", state["basem"])
                    for dc in range(DC):
                        p.pend("tensor", lambda E, dc=dc, par=par, tsl5=tsl5:
                               E.matmul(PS2[:, :], wring[:, par * 16 + dc, :],
                                        xln[:, dc, tsl5],
                                        start=(dc == 0), stop=(dc == DC - 1)))
                    bm = p.close("tensor")
                    wring_r.consumed(par, bm)
                    p.wait("scalar", "T", bm)
                    state["basem"] = p.step("scalar", lambda E, fc=fc, bsl=bsl:
                                            E.activation(base_r[:, bsl, :],
                                                         PS2[:, :], AF.Identity,
                                                         bias=b1col[:, fc:fc + 1]))
                    base_sl[bsl] = 0
                p.wait("tensor", "SC")
                p.wait_tok("tensor", da2)
                p.wait_tok("tensor", db1)
                for pg in (0, 1):
                    if xg[pg]:
                        p.wait("tensor", "V", xg[pg])
                        xg[pg] = 0
                for tq in range(2):
                    tq_abs = th * 2 + tq
                    tqs = slice(tq * 256, (tq + 1) * 256)
                    if R == 0 and "quads" in L1F_PARTS:
                        for e in range(8):
                            rr = e % 2
                            p.wait("gpsimd", "V", p.cnt["V"])
                            p.wait("gpsimd", "T", p.cnt["T"])
                            dcr = p.dma("gpsimd",
                                        lambda E, e=e, rr=rr, tq_abs=tq_abs:
                                        E.dma_start(
                                            out=crt[0:1, rr, 0:256],
                                            in_=crows_d[e:e + 1,
                                                        tq_abs * 256:
                                                        (tq_abs + 1) * 256]))
                            p.wait_tok("tensor", dcr)
                            p.wait("tensor", "V", p.cnt["V"])
                            cbm = p.step("tensor", lambda E, rr=rr:
                                         E.matmul(PS2[:, 0:256],
                                                  ones_rowb[0:1, 0:128],
                                                  crt[0:1, rr, 0:256],
                                                  start=True, stop=True))
                            p.wait("vector", "T", cbm)
                            cb = p.step("vector", lp(lambda E, e=e, tq=tq:
                                        E.tensor_copy(cbe[:, tq, e, :],
                                                      PS2[:, 0:256])))
                        p.wait("tensor", "V", p.cnt["V"])
                    for j8 in range(8 if "quads" in L1F_PARTS else 0):
                        fc = fc0 + j8
                        bsl = j8 % 4
                        for eq in range(2):
                            sl = state["qidx"] % 2
                            cslot = j8 % 2
                            if l1g[sl]:
                                p.wait("tensor", "V", l1g[sl])
                            qa, qb = QSET[sl]
                            for ee in range(4):
                                e = eq * 4 + ee
                                psx = qa if ee < 2 else qb
                                p.pend("tensor",
                                       lambda E, e=e, ee=ee, j8=j8, psx=psx,
                                       tqs=tqs:
                                       E.matmul(psx[:, (ee % 2) * 256:
                                                    (ee % 2) * 256 + 256],
                                                b1r[:, e, j8, :],
                                                t1sb[0:16, e, tqs],
                                                start=True, stop=True))
                            lm = p.close("tensor")
                            p.wait("vector", "T", lm)
                            if geg[sl]:
                                p.wait("vector", "SC", geg[sl])
                            bs = base_r[:, bsl, tqs]
                            brep = AP(bs.tensor, bs.offset,
                                      [list(bs.ap[0])] + [[0, 2]] +
                                      [list(d) for d in bs.ap[1:]])
                            p.step("vector", lp(lambda E, sl=sl, brep=brep, qa=qa:
                                   E.scalar_tensor_tensor(
                                       aetA[:, sl, 0:512], qa[:, :], 2.0,
                                       brep, ALU.mult, ALU.add)))
                            sv = p.step("vector", lp(lambda E, sl=sl, brep=brep, qb=qb:
                                        E.scalar_tensor_tensor(
                                            aetA[:, sl, 512:1024], qb[:, :], 2.0,
                                            brep, ALU.mult, ALU.add)))
                            l1g[sl] = sv
                            base_sl[bsl] = sv
                            p.wait("scalar", "V", sv)
                            if cag[sl]:
                                p.wait("scalar", "V", cag[sl])
                            if t2g[sl]:
                                p.wait("scalar", "T", t2g[sl])
                            gm = p.step("scalar", lambda E, sl=sl: E.activation(
                                aetB[:, sl, :], aetA[:, sl, :],
                                AF.Gelu_apprx_tanh))
                            geg[sl] = gm
                            pend_ca.append((j8, eq, tq, sl, cslot, gm, tqs, rrA, R))
                            state["qidx"] += 1
                            while len(pend_ca) > 1:
                                drain_quad(pend_ca.pop(0))
                    while pend_ca:
                        drain_quad(pend_ca.pop(0))
                    t2m = p.mark("T")
                    p.wait("vector", "T", t2m)
                    if R == 0 and tq == 0 and sh_prev:
                        for tok in sh_prev:
                            p.wait_tok("vector", tok)
                        sh_prev.clear()
                    for e in range(8 if "t2" in L1F_PARTS else 0):
                        tac = t2accA if e < 4 else t2accB
                        ei = e % 4
                        p.wait("vector", "V", p.cnt["V"])
                        if R == 0:
                            p.step("vector", lp(lambda E, e=e, ei=ei, tac=tac,
                                   tqs=tqs:
                                   E.tensor_copy(tac[:, ei, tqs], t2reg(e))))
                        else:
                            p.step("vector", lp(lambda E, e=e, ei=ei, tac=tac,
                                   tqs=tqs:
                                   E.scalar_tensor_tensor(
                                       tac[:, ei, tqs], t2reg(e), 1.0,
                                       tac[:, ei, tqs], ALU.mult, ALU.add)))
                    t2ev = p.mark("V")
                    p.wait("tensor", "V", t2ev)
                p.wait("tensor", "G", p.cnt["G"])
                for dc in range(DC if "w2" in L1F_PARTS else 0):
                    psw = PS0 if dc % 2 == 0 else PS1
                    par = wring_r.next()
                    ldw2 = p.dma("sync", lambda E, dc=dc, fc0=fc0, par=par:
                                 E.dma_start(
                                     out=wring[:, par * 16:par * 16 + 8, :],
                                     in_=w2c[dc, fc0:fc0 + 8, :, :]
                                     .transpose([1, 0, 2])))
                    p.wait_tok("tensor", ldw2)
                    if xg[dc % 2]:
                        p.wait("tensor", "V", xg[dc % 2])
                    if l1g[dc % 2]:
                        p.wait("tensor", "V", l1g[dc % 2])
                        l1g[dc % 2] = 0
                    for j in range(8):
                        p.pend("tensor", lambda E, psw=psw, j=j, par=par:
                               E.matmul(psw[:, :], wring[:, par * 16 + j, :],
                                        mixR[:, j, :], start=(j == 0),
                                        stop=(j == 7)))
                    wm = p.close("tensor")
                    wring_r.consumed(par, wm)
                    p.wait("vector", "T", wm)
                    xg[dc % 2] = p.step("vector",
                                        lambda E, dc=dc, psw=psw, tsl5=tsl5:
                                        E.tensor_add(x_res[:, dc, tsl5],
                                                     x_res[:, dc, tsl5],
                                                     psw[:, :]))
            l2e_on = "l2" in L1F_PARTS or "l2e" in L1F_PARTS
            l2d_on = "l2" in L1F_PARTS or "l2dma" in L1F_PARTS
            l2m_on = "l2" in L1F_PARTS or "l2mm" in L1F_PARTS
            for e in range(8 if l2e_on else 0):
                rr = e % 2
                p.wait("gpsimd", "V", p.cnt["V"])
                p.wait("gpsimd", "T", p.cnt["T"])
                dcr = p.dma("gpsimd", lambda E, e=e, rr=rr, th=th: E.dma_start(
                    out=crt[0:1, rr, 0:512],
                    in_=crows_d[e:e + 1, th * 512:(th + 1) * 512]))
                p.wait_tok("tensor", dcr)
                p.wait("tensor", "V", p.cnt["V"])
                outr = PS2[32:48, :] if e < 4 else PS2[64:80, :]
                tac = t2accA if e < 4 else t2accB
                ei = e % 4
                cbm = p.step("tensor", lambda E, rr=rr, outr=outr:
                             E.matmul(outr, ones_rowb[0:1, 0:16],
                                      crt[0:1, rr, 0:512], start=True, stop=True))
                p.wait("vector", "T", cbm)
                p.step("vector", lp(lambda E, ei=ei, tac=tac, outr=outr:
                       E.scalar_tensor_tensor(
                           tac[:, ei, :], tac[:, ei, :], 2.0, outr,
                           ALU.mult, ALU.mult)))
            tcv = p.mark("V")
            p.wait("tensor", "V", tcv)
            shtoks = []
            if l2d_on:
                p.wait("gpsimd", "T", p.cnt["T"])
                p.wait("gpsimd", "V", tcv)
                for e in range(8):
                    tac = t2accA if e < 4 else t2accB
                    ei = e % 4
                    shtoks.append(p.dma("gpsimd", lambda E, e=e, ei=ei, tac=tac:
                                        E.dma_start(
                                            out=t2all[e * 16:(e + 1) * 16, :],
                                            in_=tac[:, ei, :])))
                sh_prev[:] = shtoks
            lg2 = xg
            for dc in range(DC if (l2d_on or l2m_on) else 0):
                psw = PS0 if dc % 2 == 0 else PS1
                rrb2 = dc % 2
                if l2d_on:
                    if dc >= 2:
                        p.wait("gpsimd", "T", p.cnt["T"])
                    db2r = p.dma("gpsimd", lambda E, dc=dc, rrb2=rrb2:
                                 E.dma_start(
                                     out=b2stk[:, rrb2, :],
                                     in_=b2tc[dc, :, :, :]
                                     .rearrange("e r d -> (e r) d")))
                    p.wait_tok("tensor", db2r)
                if not l2m_on:
                    continue
                if dc == 0:
                    for tok in shtoks:
                        p.wait_tok("tensor", tok)
                if lg2[dc % 2]:
                    p.wait("tensor", "V", lg2[dc % 2])
                if xg[dc % 2]:
                    p.wait("tensor", "V", xg[dc % 2])
                    xg[dc % 2] = 0
                p.pend("tensor", lambda E, dc=dc, psw=psw, th=th: E.matmul(
                    psw[:, :], b2row[0:1, dc * 128:(dc + 1) * 128],
                    csb[0:1, th * 512:(th + 1) * 512], start=True, stop=False))
                p.pend("tensor", lambda E, rrb2=rrb2, psw=psw: E.matmul(
                    psw[:, :], b2stk[:, rrb2, :], t2all[:, :],
                    start=False, stop=True))
                wm = p.close("tensor")
                p.wait("vector", "T", wm)
                lg2[dc % 2] = p.step("vector", lambda E, dc=dc, psw=psw, tsl5=tsl5:
                                     E.tensor_add(x_res[:, dc, tsl5],
                                                  x_res[:, dc, tsl5], psw[:, :]))
        p.barrier()
        ABF.reset()
        AFA.reset()

    def phase_out():
        p.barrier()
        AFA.reset()
        stg = AFA.alloc(128, 1024)
        stg2 = AFA.alloc(128, 1024)
        guards = {0: 0, 1: 0}
        sguard = {0: 0, 1: 0}
        for tt in range(8):
            for dh in range(2):
                par = dh
                st = stg if par == 0 else stg2
                if sguard[par]:
                    p.wait_tok("vector", sguard[par])
                for j in range(8):
                    dc = dh * 8 + j
                    pst = [PS0, PS1][j % 2]
                    if guards[j % 2]:
                        p.wait("tensor", "V", guards[j % 2])
                    td = p.step("tensor", lambda E, pst=pst, dc=dc, tt=tt:
                                E.transpose(pst[:, 0:128],
                                            x_res[:, dc, tt * 128:(tt + 1) * 128],
                                            ident[:, :]))
                    p.wait("vector", "T", td)
                    ev = p.step("vector", lambda E, pst=pst, st=st, j=j:
                                E.tensor_copy(st[:, j * 128:(j + 1) * 128], pst[:, 0:128]))
                    guards[j % 2] = ev
                p.wait("sync", "V")
                sguard[par] = p.dma("sync", lambda E, tt=tt, dh=dh, st=st:
                                    E.dma_start(out=y_out[tt * 128:(tt + 1) * 128,
                                                          dh * 1024:(dh + 1) * 1024],
                                                in_=st))
        p.wait_dma_all("sync", "sync")

    def tap_xln_chunk(chunk):
        p.barrier()
        AFA.reset()
        stg = AFA.alloc(128, 1024)
        p.step("vector", lambda E: E.tensor_copy(stg, xln[:, chunk, :]))
        p.wait("sync", "V")
        p.dma("sync", lambda E: E.dma_start(out=dbg_out[0:128, 0:1024], in_=stg))
        p.wait_dma_all("sync", "sync")

    phase0()
    if tap == "ln1":
        layernorm()
        tap_xln_chunk(0)
    elif tap == "l0a":
        l0_attn()
    elif tap == "l0f":
        l0_attn()
        l0_ffn()
    elif tap == "l1a":
        l0_attn()
        l0_ffn()
        l1_attn()
    else:
        l0_attn()
        l0_ffn()
        l1_attn()
        l1_ffn()
    phase_out()

    with nc.Block() as block:
        @block.sync
        def _(eng):
            for fn in p.q["sync"]:
                fn(eng)

        @block.tensor
        def _(eng):
            for fn in p.q["tensor"]:
                fn(eng)

        @block.vector
        def _(eng):
            for fn in p.q["vector"]:
                fn(eng)

        @block.scalar
        def _(eng):
            for fn in p.q["scalar"]:
                fn(eng)

        @block.gpsimd
        def _(eng):
            for fn in p.q["gpsimd"]:
                fn(eng)

    return nc, meta


def make_inputs_for_core(core, inputs):
    b = core // 2
    half = core % 2
    d = {}
    x = np.asarray(inputs["L0_attn_hidden_states"])
    d["x"] = np.ascontiguousarray(x[b, half * S:(half + 1) * S, :])
    for k, v in inputs.items():
        if k == "L0_attn_hidden_states":
            continue
        d[k] = np.ascontiguousarray(np.asarray(v))
    d["ident128"] = np.eye(128, dtype=np.float32)
    import ml_dtypes
    pm = np.zeros((128, SK), dtype=ml_dtypes.bfloat16)
    pm[:, half * S:(half + 1) * S] = 1.0
    d["pmask"] = pm
    sel = np.zeros((32, 32, 64), dtype=np.float32)
    for h in range(32):
        sel[h, h, :] = 1.0
    d["selbc"] = sel.reshape(32, 2048).astype(ml_dtypes.bfloat16)
    return d


def kernel(**inputs):
    nc, _ = build()
    in_maps = [make_inputs_for_core(c, inputs) for c in range(NCORES)]
    res = run_bass_kernel_spmd(nc, in_maps, core_ids=list(range(NCORES)))
    out = np.empty((B, SK, D), dtype=np.float32)
    for c in range(NCORES):
        b = c // 2
        half = c % 2
        out[b, half * S:(half + 1) * S, :] = res.results[c]["y"]
    return out


# version 05
# speedup vs baseline: 1.7443x; 1.0046x over previous
import numpy as np

import concourse.bass as bass
import concourse.mybir as mybir
from concourse.bass_utils import run_bass_kernel_spmd
from concourse.bass_types import AP

F32 = mybir.dt.float32
BF16 = mybir.dt.bfloat16
AF = mybir.ActivationFunctionType
ALU = mybir.AluOpType
AX = mybir.AxisListType

D = 2048
DC = 16
S = 1024
SK = 2048
FF = 8192
FC = 64
B = 4
NCORES = 8

L1F_PARTS = frozenset({"t1", "base", "quads", "gp", "t2", "w2", "l2"})

ENGS = ("sync", "tensor", "vector", "scalar", "gpsimd")
PROG_SEM = {"sync": "SY", "tensor": "T", "vector": "V", "scalar": "SC", "gpsimd": "G"}
NDMASUB = 4
DMA_SEM = {q: [f"{q[:2].upper()}{i}" for i in range(NDMASUB)]
           for q in ("sync", "scalar", "gpsimd")}


class Prog:
    def __init__(self, nc):
        self.nc = nc
        self.q = {e: [] for e in ENGS}
        self.sem_handles = {}
        self.cnt = {}
        self._sem_ctxs = []
        self.dma_ctr = {q: 0 for q in DMA_SEM}
        self._pend = {e: None for e in ENGS}
        for s in list(PROG_SEM.values()) + ["CC"]:
            self.new_sem(s)
        for q, subs in DMA_SEM.items():
            for s in subs:
                self.new_sem(s)

    def new_sem(self, name):
        ctx = self.nc.semaphore(name)
        handle = ctx.__enter__()
        self._sem_ctxs.append(ctx)
        self.sem_handles[name] = handle
        self.cnt[name] = 0

    def emit(self, eng, fn):
        self._flush(eng)
        self.q[eng].append(fn)

    def pend(self, eng, fn):
        self._flush(eng)
        self._pend[eng] = fn

    def close(self, eng):
        fn = self._pend[eng]
        assert fn is not None, f"close({eng}) with no pending instruction"
        self._pend[eng] = None
        self.cnt[PROG_SEM[eng]] += 1
        tgt = self.cnt[PROG_SEM[eng]]
        h = self.sem_handles[PROG_SEM[eng]]
        self.q[eng].append(lambda E, f=fn, h=h: f(E).then_inc(h, 1))
        return tgt

    def _flush(self, eng):
        fn = self._pend.get(eng)
        if fn is not None:
            self._pend[eng] = None
            self.q[eng].append(fn)

    def inc(self, eng, sem, fn, n=1):
        self._flush(eng)
        self.cnt[sem] += n
        tgt = self.cnt[sem]
        h = self.sem_handles[sem]
        self.q[eng].append(lambda E, f=fn, h=h, n=n: f(E).then_inc(h, n))
        return tgt

    def step(self, eng, fn):
        return self.inc(eng, PROG_SEM[eng], fn)

    def dma(self, eng, fn):
        sub = DMA_SEM[eng][self.dma_ctr[eng] % NDMASUB]
        self.dma_ctr[eng] += 1
        if self.cnt[sub] > 0:
            self.wait(eng, sub, self.cnt[sub])
        v = self.inc(eng, sub, fn, 16)
        return (sub, v)

    def wait_tok(self, eng, tok):
        if tok:
            self.wait(eng, tok[0], tok[1])

    def wait_dma_all(self, eng, queue):
        for sub in DMA_SEM[queue]:
            self.wait(eng, sub, self.cnt[sub])

    def wait(self, eng, sem, val=None):
        v = self.cnt[sem] if val is None else val
        assert v <= self.cnt[sem], f"wait target {v} > emitted {self.cnt[sem]} on {sem}"
        if v <= 0:
            return
        self._flush(eng)
        h = self.sem_handles[sem]
        self.q[eng].append(lambda E, h=h, v=v: E.wait_ge(h, v))

    def mark(self, sem):
        return self.cnt[sem]

    def barrier(self, engs=ENGS):
        marks = {}
        for eng in engs:
            self.wait(eng, PROG_SEM[eng], self.cnt[PROG_SEM[eng]])
            marks[eng] = self.inc(eng, PROG_SEM[eng], lambda E: E.nop())
        for eng in engs:
            for oeng in engs:
                if oeng != eng:
                    self.wait(eng, PROG_SEM[oeng], marks[oeng])
            for q in DMA_SEM:
                self.wait_dma_all(eng, q)


class Arena:

    def __init__(self, tensor, total):
        self.t = tensor
        self.total = total
        self.off = 0

    def alloc(self, parts, n):
        assert self.off + n <= self.total, f"arena overflow: {self.off}+{n} > {self.total}"
        ap = self.t[0:parts, self.off:self.off + n]
        self.off += n
        return ap

    def alloc3(self, parts, a, b_):
        ap = self.alloc(parts, a * b_)
        return ap.rearrange("p (a b) -> p a b", a=a)

    def reset(self):
        self.off = 0


class Ring:

    def __init__(self, p, eng="gpsimd"):
        self.p = p
        self.eng = eng
        self.ctr = 0
        self.guard = [0, 0]

    def next(self):
        par = self.ctr % 2
        self.ctr += 1
        if self.guard[par]:
            self.p.wait(self.eng, "T", self.guard[par])
        return par

    def consumed(self, par, mark):
        self.guard[par] = mark


def build(tap=None):
    nc = bass.Bass("TRN2", target_bir_lowering=False)

    def din(name, shape, dtype=F32):
        return nc.dram_tensor(name, list(shape), dtype, kind="ExternalInput").ap()

    x_in = din("x", [S, D])
    Wq = din("L0_attn_Wq", [D, D])
    Wk = din("L0_attn_Wk", [D, D])
    Wv = din("L0_attn_Wv", [D, D])
    Wo = din("L0_attn_Wo", [D, D])
    bo_in = din("L0_attn_bo", [D])
    up_w = din("L0_ffn_up_weight", [FF, D])
    dn_w = din("L0_ffn_down_weight", [D, FF])
    up_s = din("L0_ffn_up_scale", [FF])
    dn_s = din("L0_ffn_down_scale", [D])
    out_s = din("L0_ffn_output_scale", [1])
    basis_in, phase_in, amp_in = {}, {}, {}
    for nm in ("q", "k", "v", "o"):
        basis_in[nm] = din(f"L1_attn_basis_{nm}", [64, D])
        phase_in[nm] = din(f"L1_attn_phase_{nm}", [D, 64])
        amp_in[nm] = din(f"L1_attn_amp_{nm}", [D, 64])
    Wr = din("L1_ffn_Wr", [D, 8])
    br_in = din("L1_ffn_br", [8])
    W1 = din("L1_ffn_W1", [D, FF])
    b1_in = din("L1_ffn_b1", [FF])
    W2 = din("L1_ffn_W2", [FF, D])
    b2_in = din("L1_ffn_b2", [D])
    A1 = din("L1_ffn_A1", [8, 16, D])
    B1 = din("L1_ffn_B1", [8, FF, 16])
    A2 = din("L1_ffn_A2", [8, 16, FF])
    B2 = din("L1_ffn_B2", [8, D, 16])
    ident_in = din("ident128", [128, 128])
    pmask_in = din("pmask", [128, SK], BF16)
    selbc_in = din("selbc", [32, 2048], BF16)

    y_out = nc.dram_tensor("y", [S, D], F32, kind="ExternalOutput").ap()
    dbg_out = nc.dram_tensor("dbg", [128, 4096], F32, kind="ExternalOutput").ap()

    cc_kv_in = nc.dram_tensor("cc_kv_in", [128, 2064], F32).ap()
    cc_kv_out = nc.dram_tensor("cc_kv_out", [128, 2064], F32).ap()
    cc_res_in = nc.dram_tensor("cc_res_in", [128, SK], BF16).ap()
    cc_res_out = nc.dram_tensor("cc_res_out", [128, SK], BF16).ap()
    upTc = nc.dram_tensor("upTc", [FC, DC, 128, 128], BF16).ap()
    dnTc = nc.dram_tensor("dnTc", [DC, FC, 128, 128], BF16).ap()
    w1c = nc.dram_tensor("w1c", [FC, DC, 128, 128], BF16).ap()
    w2c = nc.dram_tensor("w2c", [DC, FC, 128, 128], BF16).ap()
    b1tc = nc.dram_tensor("b1tc", [8, 16, 8192], BF16).ap()
    a2tc = nc.dram_tensor("a2tc", [FC, 128, 128], BF16).ap()
    b2tc = nc.dram_tensor("b2tc", [DC, 8, 16, 128], BF16).ap()
    crows_d = nc.dram_tensor("crows_d", [9, S], F32).ap()

    x_res = nc.alloc_sbuf_tensor("x_res", [128, DC, S], F32)
    xln = nc.alloc_sbuf_tensor("xln", [128, DC, S], BF16)
    bufA = nc.alloc_sbuf_tensor("bufA", [128, 16384], BF16)
    bufB = nc.alloc_sbuf_tensor("bufB", [128, 16384], BF16)
    wring = nc.alloc_sbuf_tensor("wring", [128, 32, 128], BF16)
    ident = nc.alloc_sbuf_tensor("ident", [128, 128], F32)
    ident_bf = nc.alloc_sbuf_tensor("ident_bf", [128, 128], BF16)
    ones_col = nc.alloc_sbuf_tensor("ones_col", [128, 1], F32)
    ones_colb = nc.alloc_sbuf_tensor("ones_colb", [128, 1], BF16)
    ones_row = nc.alloc_sbuf_tensor("ones_row", [1, 128], F32)
    pio2 = nc.alloc_sbuf_tensor("pio2", [128, 1], F32)
    ones_rowb = nc.alloc_sbuf_tensor("ones_rowb", [1, 512], BF16)
    arena_bf_t = nc.alloc_sbuf_tensor("arena_bf", [128, 13056], BF16)
    arena_f32_t = nc.alloc_sbuf_tensor("arena_f32", [128, 2624], F32)
    sbuf_left = nc.sbuf_bytes_remaining

    ABF = Arena(arena_bf_t, 13056)
    AFA = Arena(arena_f32_t, 2624)

    PSB = nc.alloc_psum_tensor("PSB", [128, 2048], F32)
    PS0 = nc.alloc_psum_tensor("PS0", [128, 512], F32)
    PS1 = nc.alloc_psum_tensor("PS1", [128, 512], F32)
    PS2 = nc.alloc_psum_tensor("PS2", [128, 512], F32)
    PT = nc.alloc_psum_tensor("PT", [128, 1024], BF16)

    p = Prog(nc)
    meta = dict(sbuf_left=sbuf_left)

    def lp(fn):
        def wrapped(E, fn=fn):
            with nc.allow_low_precision(reason="bf16 operand for subsequent matmul"):
                return fn(E)
        return wrapped

    def load_colvec(vec_dram, nch, dst, tmp):
        p.wait("sync", "T", p.cnt["T"])
        ld = p.dma("sync", lambda E: E.dma_start(
            out=tmp[0:nch, 0:128], in_=vec_dram.rearrange("(a b) -> a b", b=128)))
        p.wait_tok("tensor", ld)
        p.wait("tensor", "V", p.cnt["V"])
        tm = p.step("tensor", lambda E: E.transpose(PS2[:, 0:nch], tmp[0:nch, 0:128],
                                                    ident[0:nch, 0:nch]))
        p.wait("vector", "T", tm)
        return p.step("vector", lambda E: E.tensor_copy(dst, PS2[:, 0:nch]))

    def phase0():
        AFA.reset()
        stg = AFA.alloc(128, 1024)
        stg2 = AFA.alloc(128, 1024)
        for t_ in (ones_col, ones_colb, ones_row, ones_rowb):
            p.emit("vector", lambda E, t_=t_: E.memset(t_[:, :], 1.0))
        p.emit("vector", lambda E: E.memset(pio2[:, :], float(np.pi / 2)))
        p.step("vector", lambda E: E.memset(stg2, 0.0))
        p.dma("sync", lambda E: E.dma_start(out=ident[:, :], in_=ident_in[:, :]))
        p.dma("gpsimd", lambda E: E.dma_start(out=ident_bf[:, :], in_=ident_in[:, :]))
        p.wait("sync", "V")
        for c0 in range(0, 2064, 1024):
            w = min(1024, 2064 - c0)
            p.dma("sync", lambda E, c0=c0, w=w:
                  E.dma_start(out=cc_kv_in[:, c0:c0 + w], in_=stg2[:, 0:w]))
        p.wait_dma_all("sync", "sync")
        guards = {0: 0, 1: 0}
        sguard = {0: 0, 1: 0}
        for tt in range(8):
            for dh in range(2):
                par = (tt * 2 + dh) % 2
                st = stg if par == 0 else stg2
                if sguard[par]:
                    p.wait("sync", "T", sguard[par])
                ld = p.dma("sync", lambda E, tt=tt, dh=dh, st=st:
                           E.dma_start(out=st,
                                       in_=x_in[tt * 128:(tt + 1) * 128,
                                                dh * 1024:(dh + 1) * 1024]))
                p.wait_tok("tensor", ld)
                for j in range(8):
                    dc = dh * 8 + j
                    pst = [PS0, PS1][j % 2]
                    if guards[j % 2]:
                        p.wait("tensor", "V", guards[j % 2])
                    td = p.step("tensor", lambda E, pst=pst, st=st, j=j:
                                E.transpose(pst[:, 0:128], st[:, j * 128:(j + 1) * 128],
                                            ident[:, :]))
                    p.wait("vector", "T", td)
                    ev = p.step("vector", lambda E, pst=pst, dc=dc, tt=tt:
                                E.tensor_copy(x_res[:, dc, tt * 128:(tt + 1) * 128],
                                              pst[:, 0:128]))
                    guards[j % 2] = ev
                sguard[par] = p.mark("T")

    def layernorm():
        p.barrier()
        AFA.reset()
        strow = AFA.alloc3(1, 2, 1024)
        stg = AFA.alloc(128, 512)
        prev_mm = 0
        for dc in range(DC):
            for th in range(2):
                sl = slice(th * 512, (th + 1) * 512)
                if prev_mm:
                    p.wait("scalar", "T", prev_mm)
                sq = p.step("scalar", lambda E, dc=dc, sl=sl:
                            E.activation(stg, x_res[:, dc, :][:, sl], AF.Square))
                p.emit("tensor", lambda E, dc=dc, sl=sl:
                       E.matmul(PSB[0:1, 0:1024][:, sl], ones_col[:, :],
                                x_res[:, dc, :][:, sl], start=(dc == 0), stop=(dc == DC - 1)))
                p.wait("tensor", "SC", sq)
                prev_mm = p.step("tensor", lambda E, dc=dc, th=th:
                                 E.matmul(PSB[32:33, 1024 + th * 512:1536 + th * 512],
                                          ones_col[:, :], stg,
                                          start=(dc == 0), stop=(dc == DC - 1)))
        p.wait("vector", "T")
        p.step("vector", lambda E: E.tensor_scalar_mul(strow[:, 0, :], PSB[0:1, 0:1024], 1.0 / D))
        p.wait("vector", "V")
        p.step("vector", lambda E: E.tensor_mul(strow[:, 1, :], strow[:, 0, :], strow[:, 0, :]))
        p.wait("vector", "V")
        p.step("vector", lambda E:
               E.scalar_tensor_tensor(strow[:, 1, :], PSB[32:33, 1024:2048], 1.0 / D,
                                      strow[:, 1, :], ALU.mult, ALU.subtract))
        p.wait("vector", "V")
        vs = p.step("vector", lambda E: E.tensor_scalar_add(strow[:, 1, :], strow[:, 1, :], 1e-5))
        p.wait("scalar", "V", vs)
        sq = p.step("scalar", lambda E: E.activation(strow[:, 1, :], strow[:, 1, :], AF.Sqrt))
        p.wait("vector", "SC", sq)
        vb = p.step("vector", lambda E: E.reciprocal(strow[:, 1, :], strow[:, 1, :]))
        p.wait("tensor", "V", vb)
        for th in range(2):
            sl = slice(th * 512, (th + 1) * 512)
            rps = PS0 if th == 0 else PS1
            p.emit("tensor", lambda E, rps=rps, sl=sl:
                   E.matmul(rps[:, :], ones_row[0:1, :], strow[:, 1, :][:, sl],
                            start=True, stop=True))
            p.step("tensor", lambda E, th=th, sl=sl:
                   E.matmul(PSB[:, th * 512:(th + 1) * 512], ones_row[0:1, :],
                            strow[:, 0, :][:, sl], start=True, stop=True))
        p.wait("vector", "T")
        for dc in range(DC):
            for th in range(2):
                sl = slice(th * 512, (th + 1) * 512)
                mps = PSB[:, th * 512:(th + 1) * 512]
                fn = lambda E, dc=dc, sl=sl, mps=mps: \
                    E.tensor_sub(xln[:, dc, :][:, sl], x_res[:, dc, :][:, sl], mps)
                if dc == DC - 1 and th == 1:
                    p.step("vector", fn)
                else:
                    p.emit("vector", fn)
        p.wait("vector", "V")
        for dc in range(DC):
            for th in range(2):
                sl = slice(th * 512, (th + 1) * 512)
                rps = PS0 if th == 0 else PS1
                p.pend("vector", lambda E, dc=dc, sl=sl, rps=rps:
                       E.tensor_mul(xln[:, dc, :][:, sl], xln[:, dc, :][:, sl], rps[:, :]))
        p.close("vector")
        p.barrier()
        AFA.reset()

    def load_col_chunks(ring, Wap, ic):
        par = ring.next()
        mk = p.dma("gpsimd", lambda E, Wap=Wap, ic=ic, par=par:
                   E.dma_start(out=wring[:, par * 16:(par + 1) * 16, :],
                               in_=Wap[:, ic * 128:(ic + 1) * 128]
                               .rearrange("(a p) c -> p a c", p=128)))
        return par, mk

    def l0_attn():
        layernorm()
        ABF.reset()
        AFA.reset()
        kvp = AFA.alloc(64, 2048)
        kssb = AFA.alloc(128, 16)
        kvbf = ABF.alloc(128, 2048)
        ksD = ABF.alloc3(128, 16, 32)
        rnbf = ABF.alloc(32, 1024)
        selbc = ABF.alloc3(32, 32, 64)
        bocol = AFA.alloc(128, 16)
        kring = ABF.alloc3(128, 2, 1024)
        vring = ABF.alloc3(128, 2, 1024)
        ktm = ABF.alloc(128, 512)
        vtm = ABF.alloc(128, 512)
        rnb = ABF.alloc(64, 512)
        p.dma("sync", lambda E: E.dma_start(
            out=selbc.rearrange("p a b -> p (a b)"), in_=selbc_in[:, :]))
        tmpcv = AFA.alloc(64, 128)
        load_colvec(bo_in, 16, bocol, tmpcv)
        p.step("vector", lambda E: E.memset(ksD.rearrange("p a b -> p (a b)"), 0.0))
        ring = Ring(p)

        ks_first = {0: True, 1: True}
        n_ks = 0
        ev_guard = {"k": {}, "v": {}}
        kv_evac_g = 0
        tp_guard = [0]
        slotg = {0: 0, 1: 0}
        for ic in range(16):
            for which, Wap, dst, relu in (("k", Wk, kring, True), ("v", Wv, vring, False)):
                par, ld = load_col_chunks(ring, Wap, ic)
                psl = PSB[:, 0:1024] if which == "k" else PSB[:, 1024:2048]
                p.wait_tok("tensor", ld)
                if ic >= 1:
                    eng, mk = ev_guard[which][ic - 1]
                    p.wait("tensor", eng, mk)
                for dc in range(DC):
                    for th in range(2):
                        p.pend("tensor", lambda E, psl=psl, dc=dc, th=th, par=par:
                               E.matmul(psl[:, th * 512:(th + 1) * 512],
                                        wring[:, par * 16 + dc, :],
                                        xln[:, dc, th * 512:(th + 1) * 512],
                                        start=(dc == 0), stop=(dc == DC - 1)))
                mm = p.close("tensor")
                ring.consumed(par, mm)
                slot = ic % 2
                if relu:
                    p.wait("scalar", "T", mm)
                    p.wait("scalar", "T", slotg[slot])
                    ev = p.step("scalar", lambda E, psl=psl, dst=dst, slot=slot:
                                E.activation(dst[:, slot, :], psl[:, :], AF.Relu))
                    ev_guard[which][ic] = ("SC", ev)
                else:
                    p.wait("vector", "T", mm)
                    p.wait("vector", "T", slotg[slot])
                    ev = p.step("vector", lambda E, psl=psl, dst=dst, slot=slot:
                                E.tensor_copy(dst[:, slot, :], psl[:, :]))
                    ev_guard[which][ic] = ("V", ev)
            slot = ic % 2
            p.wait("tensor", "SC", ev_guard["k"][ic][1])
            p.wait("tensor", "V", ev_guard["v"][ic][1])
            for hp in range(2):
                h = 2 * ic + hp
                if tp_guard[0]:
                    p.wait("tensor", "V", tp_guard[0])
                idsl = ident_bf[64 * hp:64 * hp + 64, 64 * hp:64 * hp + 64]
                for tt in range(8):
                    p.pend("tensor", lambda E, hp=hp, tt=tt, slot=slot, idsl=idsl:
                           E.transpose(PT[:, tt * 64:(tt + 1) * 64],
                                       kring[64 * hp:64 * hp + 64, slot,
                                             tt * 128:(tt + 1) * 128],
                                       idsl))
                    p.pend("tensor", lambda E, hp=hp, tt=tt, slot=slot, idsl=idsl:
                           E.transpose(PT[:, 512 + tt * 64:512 + (tt + 1) * 64],
                                       vring[64 * hp:64 * hp + 64, slot,
                                             tt * 128:(tt + 1) * 128],
                                       idsl))
                td = p.close("tensor")
                p.wait("vector", "T", td)
                if kv_evac_g:
                    p.wait("vector", "T", kv_evac_g)
                evt = 0
                for half, dsttm in ((0, ktm), (1, vtm)):
                    evt = p.step("vector", lambda E, dsttm=dsttm, half=half:
                                 E.tensor_copy(dsttm, PT[:, half * 512:(half + 1) * 512]))
                tp_guard[0] = evt
                p.wait("tensor", "V", evt)
                for tt in range(8):
                    st = (tt == 0 and hp == 0)
                    sp = (tt == 7 and hp == 1)
                    p.pend("tensor", lambda E, tt=tt, hp=hp, st=st, sp=sp:
                           E.matmul(PS2[0:64, hp * 64:(hp + 1) * 64],
                                    ktm[:, tt * 64:(tt + 1) * 64],
                                    vtm[:, tt * 64:(tt + 1) * 64],
                                    start=st, stop=sp))
                for tt in range(8):
                    n_ks += 1
                    stk = ks_first[hp]
                    ks_first[hp] = False
                    spk = (ic == 15 and tt == 7)
                    ksdst = PS0[0:64, ic:ic + 1] if hp == 0 else PS1[64:128, ic:ic + 1]
                    p.pend("tensor", lambda E, tt=tt, ksdst=ksdst, st=stk, sp=spk:
                           E.matmul(ksdst,
                                    ktm[:, tt * 64:(tt + 1) * 64],
                                    ones_colb[0:128, :],
                                    start=st, stop=sp))
                kv_evac_g = p.close("tensor")
            slotg[slot] = kv_evac_g
            p.wait("vector", "T", kv_evac_g)
            p.step("vector", lambda E, ic=ic:
                   E.tensor_copy(kvp[:, ic * 128:(ic + 1) * 128], PS2[0:64, 0:128]))
        p.step("vector", lambda E: E.tensor_copy(kssb[0:64, :], PS0[0:64, 0:16]))
        evks = p.step("vector", lambda E: E.tensor_copy(kssb[64:128, :], PS1[64:128, 0:16]))
        p.wait("sync", "V", evks)
        p.dma("sync", lambda E: E.dma_start(out=cc_kv_in[0:64, 0:2048], in_=kvp))
        d2 = p.dma("sync", lambda E: E.dma_start(out=cc_kv_in[:, 2048:2064], in_=kssb))
        p.wait_dma_all("gpsimd", "sync")
        cc = p.inc("gpsimd", "CC", lambda E:
                   E.collective_compute("AllReduce", ALU.add,
                                        replica_groups=[[0, 1], [2, 3], [4, 5], [6, 7]],
                                        ins=[cc_kv_in[:, :]], outs=[cc_kv_out[:, :]]))
        qfm = bufA[:, :].rearrange("p (a b) -> p a b", a=16)
        qev = {}
        for ic in range(16):
            par, ld = load_col_chunks(ring, Wq, ic)
            psl = PSB[:, 0:1024] if ic % 2 == 0 else PSB[:, 1024:2048]
            p.wait_tok("tensor", ld)
            if ic >= 2:
                p.wait("tensor", "SC", qev[ic - 2])
            for dc in range(DC):
                for th in range(2):
                    p.pend("tensor", lambda E, psl=psl, dc=dc, th=th, par=par:
                           E.matmul(psl[:, th * 512:(th + 1) * 512],
                                    wring[:, par * 16 + dc, :],
                                    xln[:, dc, th * 512:(th + 1) * 512],
                                    start=(dc == 0), stop=(dc == DC - 1)))
            mq = p.close("tensor")
            ring.consumed(par, mq)
            p.wait("scalar", "T", mq)
            qev[ic] = p.step("scalar", lambda E, psl=psl, ic=ic:
                             E.activation(qfm[:, ic, :], psl[:, :], AF.Relu))
        p.wait("gpsimd", "CC", cc)
        dkv = p.dma("gpsimd", lambda E: E.dma_start(out=kvbf[0:64, :],
                                                    in_=cc_kv_out[0:64, 0:2048]))
        dkv2 = p.dma("gpsimd", lambda E: E.dma_start(out=kvbf[64:128, :],
                                                     in_=cc_kv_out[0:64, 0:2048]))
        p.wait("sync", "CC", cc)
        dks = p.dma("sync", lambda E: E.dma_start(out=kssb, in_=cc_kv_out[:, 2048:2064]))
        p.wait_tok("vector", dks)
        p.wait_tok("vector", dkv)
        p.wait_tok("vector", dkv2)
        for dc in range(DC):
            p.pend("vector", lambda E, dc=dc:
                   E.tensor_copy(ksD[0:64, dc, 2 * dc:2 * dc + 1], kssb[0:64, dc:dc + 1]))
            p.pend("vector", lambda E, dc=dc:
                   E.tensor_copy(ksD[64:128, dc, 2 * dc + 1:2 * dc + 2],
                                 kssb[64:128, dc:dc + 1]))
        evD = p.close("vector")
        p.wait("tensor", "V", evD)
        p.wait("tensor", "SC")
        for th in range(2):
            for dc in range(DC):
                p.pend("tensor", lambda E, th=th, dc=dc:
                       E.matmul(PSB[0:32, th * 512:(th + 1) * 512], ksD[:, dc, :],
                                qfm[:, dc, th * 512:(th + 1) * 512],
                                start=(dc == 0), stop=(dc == DC - 1)))
        nm = p.close("tensor")
        p.wait("vector", "T", nm)
        p.step("vector", lambda E: E.tensor_scalar_max(PSB[0:32, 0:1024],
                                                       PSB[0:32, 0:1024], 1e-6))
        p.wait("vector", "V")
        evn = p.step("vector", lp(lambda E: E.reciprocal(rnbf, PSB[0:32, 0:1024])))
        p.wait("tensor", "V", evn)
        ao = bufB[:, :].rearrange("p (a b) -> p a b", a=16)
        qkv_guard = {0: 0, 1: 0}
        rnb_guard = 0
        for h in range(32):
            hp = h % 2
            ic = h // 2
            ps = PS0 if h % 2 == 0 else PS1
            for th in range(2):
                if rnb_guard:
                    p.wait("tensor", "V", rnb_guard)
                mb = p.step("tensor", lambda E, h=h, th=th:
                            E.matmul(PS2[0:64, :], selbc[:, h, :],
                                     rnbf[:, th * 512:(th + 1) * 512],
                                     start=True, stop=True))
                p.wait("vector", "T", mb)
                ev = p.step("vector", lambda E, th=th:
                            E.tensor_copy(rnb, PS2[0:64, :]))
                p.wait("tensor", "V", ev)
                p.wait("vector", "V", ev)
                if qkv_guard[h % 2]:
                    p.wait("tensor", "V", qkv_guard[h % 2])
                mq2 = p.step("tensor", lambda E, h=h, hp=hp, ic=ic, th=th, ps=ps:
                             E.matmul(ps[0:64, :] if th == 0 else ps[64:128, :],
                                      kvbf[64 * hp:64 * hp + 64, h * 64:(h + 1) * 64],
                                      qfm[64 * hp:64 * hp + 64, ic,
                                          th * 512:(th + 1) * 512],
                                      start=True, stop=True))
                p.wait("vector", "T", mq2)
                src_ = ps[0:64, :] if th == 0 else ps[64:128, :]
                ev2 = p.step("vector", lambda E, src_=src_, hp=hp, ic=ic, th=th:
                             E.tensor_mul(ao[64 * hp:64 * hp + 64, ic,
                                             th * 512:(th + 1) * 512],
                                          src_, rnb))
                qkv_guard[h % 2] = ev2
                rnb_guard = ev2
        aog = p.mark("V")
        p.wait("tensor", "V", aog)
        og = {}
        for dc in range(DC):
            par, ldo = load_col_chunks(ring, Wo, dc)
            psl = PSB[:, 0:1024] if dc % 2 == 0 else PSB[:, 1024:2048]
            p.wait_tok("tensor", ldo)
            if dc >= 2:
                p.wait("tensor", "V", og[dc - 2])
            for ic in range(16):
                for th in range(2):
                    p.pend("tensor", lambda E, psl=psl, ic=ic, th=th, par=par:
                           E.matmul(psl[:, th * 512:(th + 1) * 512],
                                    wring[:, par * 16 + ic, :],
                                    ao[:, ic, th * 512:(th + 1) * 512],
                                    start=(ic == 0), stop=(ic == DC - 1)))
            mo = p.close("tensor")
            ring.consumed(par, mo)
            p.wait("vector", "T", mo)
            for th in range(2):
                og[dc] = p.step("vector", lambda E, psl=psl, dc=dc, th=th:
                                E.scalar_tensor_tensor(
                                    x_res[:, dc, th * 512:(th + 1) * 512],
                                    psl[:, th * 512:(th + 1) * 512],
                                    bocol[:, dc:dc + 1],
                                    x_res[:, dc, th * 512:(th + 1) * 512],
                                    ALU.add, ALU.add))
        p.barrier()
        ABF.reset()
        AFA.reset()

    def l0_ffn():
        layernorm()
        ABF.reset()
        AFA.reset()
        uscol = AFA.alloc(128, 64)
        dscol = AFA.alloc(128, 16)
        os_sb = AFA.alloc(1, 1)
        tmpcv = AFA.alloc(64, 128)
        load_colvec(up_s, 64, uscol, tmpcv)
        load_colvec(dn_s, 16, dscol, tmpcv)
        dos = p.dma("sync", lambda E: E.dma_start(out=os_sb, in_=out_s[0:1]))
        p.wait_tok("tensor", dos)
        p.wait("tensor", "V", p.cnt["V"])
        mos = p.step("tensor", lambda E:
                     E.matmul(PS2[:, 0:1], ones_row[0:1, :], os_sb,
                              start=True, stop=True))
        p.wait("vector", "T", mos)
        p.wait_dma_all("vector", "sync")
        p.step("vector", lambda E:
               E.tensor_scalar_mul(dscol, dscol, PS2[:, 0:1]))

        hA = bufA[:, :].rearrange("p (a b) -> p a b", b=512)
        hB = bufB[:, :].rearrange("p (a b) -> p a b", b=512)

        def h_slot(fc):
            return (hA if fc < 32 else hB)[:, fc % 32, :]

        natr = ABF.alloc3(128, 2, 2048)

        def prep_chunks(par, src_dram, n128, cache_out):
            nslot = natr[:, par, 0:n128 * 128]
            ldn = p.dma("gpsimd", lambda E, nslot=nslot, src_dram=src_dram:
                        E.dma_start(out=nslot, in_=src_dram))
            p.wait_tok("vector", ldn)
            p.step("vector", lambda E, nslot=nslot:
                   E.tensor_scalar(nslot, nslot, 0.0, 2.0, ALU.is_gt, ALU.mult))
            p.wait("vector", "V")
            sg = p.step("vector", lambda E, nslot=nslot:
                        E.tensor_scalar_add(nslot, nslot, -1.0))
            p.wait("tensor", "V", sg)
            for half in range((n128 + 7) // 8):
                jn = min(8, n128 - half * 8)
                p.wait("tensor", "V", ptg[0])
                for j in range(jn):
                    p.pend("tensor", lambda E, nslot=nslot, half=half, j=j:
                           E.transpose(PT[:, j * 128:(j + 1) * 128],
                                       nslot[:, (8 * half + j) * 128:(8 * half + j + 1) * 128],
                                       ident_bf[:, :]))
                tdu = p.close("tensor")
                p.wait("vector", "T", tdu)
                ptg[0] = p.step(
                    "vector", lambda E, par=par, half=half, jn=jn:
                    E.tensor_copy(wring[:, par * 16 + half * 8:par * 16 + half * 8 + jn, :]
                                  .rearrange("p a b -> p (a b)"), PT[:, 0:jn * 128]))
            wvm = p.mark("V")
            p.wait("sync", "V", wvm)
            cd = p.dma("sync", lambda E, cache_out=cache_out, par=par:
                       E.dma_start(out=cache_out,
                                   in_=wring[:, par * 16:par * 16 + 16, :]))
            return p.mark("T"), cd, wvm

        ring = Ring(p)
        ptg = [0]
        cache_rd = {}
        evh = {}
        dvg = 0
        for th in range(2):
            build = (th == 0)
            natg = [0, 0]
            for fc in range(FC):
                if build:
                    parn = fc % 2
                    if natg[parn]:
                        p.wait("gpsimd", "T", natg[parn])
                    par = ring.next()
                    if cache_rd.get(par):
                        p.wait_tok("gpsimd", cache_rd[par])
                    tm, cd, wvm = prep_chunks(
                        par,
                        up_w[fc * 128:(fc + 1) * 128, :],
                        16,
                        upTc[fc, :, :, :].transpose([1, 0, 2]))
                    natg[parn] = p.mark("T")
                    cache_rd[par] = cd
                    p.wait("tensor", "V", wvm)
                else:
                    par = ring.next()
                    if cache_rd.get(par):
                        p.wait_tok("gpsimd", cache_rd[par])
                        cache_rd[par] = 0
                    ld = p.dma("gpsimd", lambda E, fc=fc, par=par:
                               E.dma_start(out=wring[:, par * 16:(par + 1) * 16, :],
                                           in_=upTc[fc, :, :, :].transpose([1, 0, 2])))
                    p.wait_tok("tensor", ld)
                psl = [PS0, PS1][fc % 2]
                if fc >= 2:
                    p.wait("tensor", "SC", evh[fc - 2])
                for dc in range(DC):
                    p.pend("tensor", lambda E, psl=psl, dc=dc, th=th, par=par:
                           E.matmul(psl[:, :], wring[:, par * 16 + dc, :],
                                    xln[:, dc, th * 512:(th + 1) * 512],
                                    start=(dc == 0), stop=(dc == DC - 1)))
                mh = p.close("tensor")
                ring.consumed(par, mh)
                p.wait("scalar", "T", mh)
                evh[fc] = p.step("scalar", lambda E, psl=psl, fc=fc:
                                 E.activation(h_slot(fc), psl[:, :], AF.Relu,
                                              scale=uscol[:, fc:fc + 1]))
            p.wait("tensor", "SC")
            for dc in range(DC):
                for g in range(4):
                    if build:
                        parn = (dc * 4 + g) % 2
                        if natg[parn]:
                            p.wait("gpsimd", "T", natg[parn])
                        par = ring.next()
                        if cache_rd.get(par):
                            p.wait_tok("gpsimd", cache_rd[par])
                        tm, cd, wvm = prep_chunks(
                            par,
                            dn_w[dc * 128:(dc + 1) * 128, g * 2048:(g + 1) * 2048],
                            16,
                            dnTc[dc, g * 16:(g + 1) * 16, :, :].transpose([1, 0, 2]))
                        natg[parn] = p.mark("T")
                        cache_rd[par] = cd
                        p.wait("tensor", "V", wvm)
                    else:
                        par = ring.next()
                        if cache_rd.get(par):
                            p.wait_tok("gpsimd", cache_rd[par])
                            cache_rd[par] = 0
                        ld = p.dma("gpsimd", lambda E, dc=dc, g=g, par=par:
                                   E.dma_start(out=wring[:, par * 16:(par + 1) * 16, :],
                                               in_=dnTc[dc, g * 16:(g + 1) * 16, :, :]
                                               .transpose([1, 0, 2])))
                        p.wait_tok("tensor", ld)
                    if g == 0 and dvg:
                        p.wait("tensor", "V", dvg)
                    for j in range(16):
                        fc = g * 16 + j
                        p.pend("tensor", lambda E, fc=fc, j=j, par=par:
                               E.matmul(PS2[:, :], wring[:, par * 16 + j, :], h_slot(fc),
                                        start=(fc == 0), stop=(fc == FC - 1)))
                    mdg = p.close("tensor")
                    ring.consumed(par, mdg)
                md = p.mark("T")
                p.wait("vector", "T", md)
                dvg = p.step("vector", lambda E, dc=dc, th=th:
                             E.scalar_tensor_tensor(x_res[:, dc, th * 512:(th + 1) * 512],
                                                    PS2[:, :], dscol[:, dc:dc + 1],
                                                    x_res[:, dc, th * 512:(th + 1) * 512],
                                                    ALU.mult, ALU.add))
        p.barrier()
        ABF.reset()
        AFA.reset()

    def l1_attn():
        layernorm()
        ABF.reset()
        AFA.reset()
        res_kT = ABF.alloc(64, 2048)
        res_vT = ABF.alloc(64, 2048)
        res_q = ABF.alloc(64, 1024)
        basTo = ABF.alloc3(128, 16, 64)
        sums = AFA.alloc(128, 8)
        recip = AFA.alloc(128, 8)
        negmax = AFA.alloc(128, 1)
        tmpa = AFA.alloc(128, 512)
        tmpp = AFA.alloc(128, 512)
        mark0 = ABF.off
        wnat = ABF.alloc3(128, 8, 64)
        basn = ABF.alloc3(64, 16, 128)
        basTt = ABF.alloc3(128, 16, 64)
        pmh = ABF.alloc(128, 1024)
        exr = ABF.alloc(128, 2048)
        wT = bufB[0:64, 8192:16384].rearrange("p (j a c) -> p j a c", j=4, a=16)
        AT = bufB[:, 0:8192].rearrange("p (a c) -> p a c", a=16)
        qfm = bufA[:, :].rearrange("p (a b) -> p a b", a=16)
        ao = qfm
        for j, nm in enumerate("qkvo"):
            scl = (1.0 / float(np.sqrt(128.0))) if nm == "q" else 1.0
            for half in range(2):
                rows = slice(half * 1024, (half + 1) * 1024)
                p.wait("sync", "V", p.cnt["V"])
                lda = p.dma("sync", lambda E, nm=nm, rows=rows: E.dma_start(
                    out=tmpa.rearrange("p (a c) -> p a c", a=8),
                    in_=amp_in[nm][rows, :].rearrange("(a p) c -> p a c", p=128)))
                ldp = p.dma("sync", lambda E, nm=nm, rows=rows: E.dma_start(
                    out=tmpp.rearrange("p (a c) -> p a c", a=8),
                    in_=phase_in[nm][rows, :].rearrange("(a p) c -> p a c", p=128)))
                p.wait_tok("scalar", ldp)
                p.wait("scalar", "V", p.cnt["V"])
                sn = p.step("scalar", lambda E: E.activation(
                    tmpp, tmpp, AF.Sin, bias=pio2[:, :]))
                p.wait_tok("vector", lda)
                p.wait("scalar", "SC", sn)
                p.wait("vector", "SC", sn)
                p.wait("vector", "T", p.cnt["T"])
                wv = p.step("vector", lp(lambda E, scl=scl: E.scalar_tensor_tensor(
                    wnat.rearrange("p a c -> p (a c)"), tmpp, scl, tmpa,
                    ALU.mult, ALU.mult)))
                p.wait("tensor", "V", wv)
                for d8 in range(8):
                    p.pend("tensor", lambda E, d8=d8:
                           E.transpose(PT[0:64, d8 * 128:(d8 + 1) * 128],
                                       wnat[:, d8, :], ident_bf[:, :]))
                tw = p.close("tensor")
                p.wait("vector", "T", tw)
                p.step("vector", lambda E, j=j, half=half:
                       E.tensor_copy(wT[0:64, j, half * 8:(half + 1) * 8, :]
                                     .rearrange("p a c -> p (a c)"), PT[0:64, :]))
        kv_res_mark = {}
        for j, nm in enumerate("qkvo"):
            p.wait("gpsimd", "T", p.cnt["T"])
            ldb = p.dma("gpsimd", lambda E, nm=nm: E.dma_start(
                out=basn.rearrange("p a c -> p (a c)"), in_=basis_in[nm][:, :]))
            p.wait_tok("tensor", ldb)
            p.wait("tensor", "V", p.cnt["V"])
            for dc in range(DC):
                p.pend("tensor", lambda E, dc=dc:
                       E.transpose(PT[:, dc * 64:(dc + 1) * 64],
                                   basn[:, dc, :], ident_bf[0:64, 0:64]))
            tb = p.close("tensor")
            p.wait("vector", "T", tb)
            dstT = basTo if nm == "o" else basTt
            bv = p.step("vector", lambda E, dstT=dstT:
                        E.tensor_copy(dstT.rearrange("p a c -> p (a c)"), PT[:, :]))
            if nm == "o":
                continue
            p.wait("tensor", "V", bv)
            for th in range(2):
                psl = PSB[:, th * 512:(th + 1) * 512]
                for dc in range(DC):
                    if nm == "q":
                        out = psl[0:64, :]
                    elif nm == "k":
                        out = psl[0:64, :]
                    else:
                        out = psl[64:128, :]
                    p.pend("tensor", lambda E, out=out, dc=dc, th=th:
                           E.matmul(out, basTt[:, dc, :],
                                    xln[:, dc, th * 512:(th + 1) * 512],
                                    start=(dc == 0), stop=(dc == DC - 1)))
                kv_res_mark[(nm, th)] = p.close("tensor")
            if nm == "q":
                m = kv_res_mark[("q", 1)]
                p.wait("vector", "T", m)
                p.step("vector", lambda E:
                       E.tensor_copy(res_q[:, 0:512], PSB[0:64, 0:512]))
                qv = p.step("vector", lambda E:
                            E.tensor_copy(res_q[:, 512:1024], PSB[0:64, 512:1024]))
                p.wait("tensor", "V", qv)
        ldm0 = p.dma("sync", lambda E: E.dma_start(out=pmh, in_=pmask_in[:, 0:1024]))
        m = kv_res_mark[("v", 1)]
        p.wait("vector", "T", m)
        p.step("vector", lambda E: E.tensor_copy(exr[:, 0:1024], PSB[:, 0:1024]))
        cv = p.step("vector", lambda E: E.tensor_copy(exr[:, 1024:2048], PSB[:, 0:1024]))
        p.wait_tok("vector", ldm0)
        p.wait("vector", "V", cv)
        m0 = p.step("vector", lambda E: E.tensor_mul(exr[:, 0:1024], exr[:, 0:1024], pmh))
        p.wait("sync", "V", m0)
        ldm1 = p.dma("sync", lambda E: E.dma_start(out=pmh, in_=pmask_in[:, 1024:2048]))
        p.wait_tok("vector", ldm1)
        p.wait("vector", "V", p.cnt["V"])
        mv = p.step("vector", lambda E:
                    E.tensor_mul(exr[:, 1024:2048], exr[:, 1024:2048], pmh))
        p.wait("sync", "V", mv)
        dex = p.dma("sync", lambda E: E.dma_start(out=cc_res_in[:, :], in_=exr))
        p.wait_dma_all("gpsimd", "sync")
        ccr = p.inc("gpsimd", "CC", lambda E:
                    E.collective_compute("AllReduce", ALU.add,
                                         replica_groups=[[0, 1], [2, 3], [4, 5], [6, 7]],
                                         ins=[cc_res_in[:, :]], outs=[cc_res_out[:, :]]))
        qg = {0: 0, 1: 0}
        for ic in range(16):
            ps = PS0 if ic % 2 == 0 else PS1
            for th in range(2):
                if qg[ic % 2]:
                    p.wait("tensor", "V", qg[ic % 2])
                mqe = p.step("tensor", lambda E, ic=ic, th=th, ps=ps:
                             E.matmul(ps[:, :],
                                      wT[0:64, 0, ic, :],
                                      res_q[:, th * 512:(th + 1) * 512],
                                      start=True, stop=True))
                p.wait("vector", "T", mqe)
                qg[ic % 2] = p.step("vector", lambda E, ic=ic, th=th, ps=ps:
                                    E.tensor_copy(qfm[:, ic, th * 512:(th + 1) * 512],
                                                  ps[:, :]))
        p.wait("sync", "CC", ccr)
        dk = p.dma("sync", lambda E: E.dma_start(out=res_kT, in_=cc_res_out[0:64, :]))
        dv = p.dma("sync", lambda E: E.dma_start(out=res_vT, in_=cc_res_out[64:128, :]))
        p.wait_tok("tensor", dk)
        p.wait_tok("tensor", dv)
        p.wait("tensor", "V", p.cnt["V"])
        ABF.off = mark0
        Kh = ABF.alloc(128, 2048)
        Vtok = ABF.alloc3(128, 16, 128)
        AT = bufB[:, 0:4096].rearrange("p (a c) -> p a c", a=16)
        AT2 = bufB[:, 8192:12288].rearrange("p (a c) -> p a c", a=16)
        Aexp2 = bufB[:, 4096:8192].rearrange("p (r c) -> p r c", r=2)
        negmax2 = negmax
        negmaxB = AFA.alloc(128, 1)
        pmx = AFA.alloc(128, 4)
        lastcmb = [0]
        pump_state = {"i": 0}
        pump_guard = {}

        def pump_step():
            i = pump_state["i"]
            if i >= 128:
                return
            pump_state["i"] += 1
            par = i % 2
            if pump_guard.get(par):
                p.wait_tok("gpsimd", pump_guard[par])
            if i < 64:
                fc = i
                tk = p.dma("gpsimd", lambda E, fc=fc, par=par: E.dma_start(
                    out=wring[:, par * 16:(par + 1) * 16, :],
                    in_=W1[:, fc * 128:(fc + 1) * 128]
                    .rearrange("(a p) c -> p a c", p=128)))
                p.wait_tok("sync", tk)
                pump_guard[par] = p.dma("sync", lambda E, fc=fc, par=par: E.dma_start(
                    out=w1c[fc, :, :, :].transpose([1, 0, 2]),
                    in_=wring[:, par * 16:(par + 1) * 16, :]))
            else:
                k = i - 64
                dc, q = k // 4, k % 4
                tk = p.dma("gpsimd", lambda E, dc=dc, q=q, par=par: E.dma_start(
                    out=wring[:, par * 16:(par + 1) * 16, :],
                    in_=W2[q * 2048:(q + 1) * 2048, dc * 128:(dc + 1) * 128]
                    .rearrange("(a p) c -> p a c", p=128)))
                p.wait_tok("sync", tk)
                pump_guard[par] = p.dma("sync", lambda E, dc=dc, q=q, par=par:
                                        E.dma_start(
                                            out=w2c[dc, q * 16:(q + 1) * 16, :, :]
                                            .transpose([1, 0, 2]),
                                            in_=wring[:, par * 16:(par + 1) * 16, :]))

        khg = [0]
        scg = {0: 0, 1: 0}
        mxg = {0: 0, 1: 0}
        nrm = {0: 0, 1: 0}
        avg = {0: 0, 1: 0}
        av_dn = {0: 0, 1: 0}
        tp_toks = {}
        ptev = [0]
        trq = {0: 0, 1: 0}
        an_s = {0: 0, 1: 0}
        for h in range(16):
            for k4 in range(4):
                ps = PS0 if k4 % 2 == 0 else PS1
                if avg[k4 % 2]:
                    p.wait("tensor", "V", avg[k4 % 2])
                mk = p.step("tensor", lambda E, h=h, k4=k4, ps=ps:
                            E.matmul(ps[:, :], wT[0:64, 1, h, :],
                                     res_kT[:, k4 * 512:(k4 + 1) * 512],
                                     start=True, stop=True))
                p.wait("vector", "T", mk)
                avg[k4 % 2] = p.step("vector", lambda E, h=h, k4=k4, ps=ps:
                                     E.tensor_copy(Kh[:, k4 * 512:(k4 + 1) * 512],
                                                   ps[:, :]))
            khg[0] = avg[1]
            for grp in range(4):
                if grp > 0 or h > 0:
                    p.wait("tensor", "V", p.cnt["V"])
                for jj in range(4):
                    kc = grp * 4 + jj
                    p.pend("tensor", lambda E, h=h, kc=kc, jj=jj:
                           E.matmul(PS2[:, jj * 128:(jj + 1) * 128],
                                    res_vT[:, kc * 128:(kc + 1) * 128],
                                    wT[0:64, 2, h, :],
                                    start=True, stop=True))
                mv2 = p.close("tensor")
                p.wait("vector", "T", mv2)
                p.step("vector", lambda E, grp=grp:
                       E.tensor_copy(Vtok[:, grp * 4:(grp + 1) * 4, :]
                                     .rearrange("p a c -> p (a c)"), PS2[:, :]))
            vtg = p.mark("V")
            p.wait("tensor", "V", vtg)

            def emit_sc(qt, h=h):
                ar = qt % 2
                nmb = negmax if ar == 0 else negmaxB
                for sl_ in (0, 1):
                    if scg[sl_]:
                        p.wait("tensor", "SC", scg[sl_])
                    if mxg[sl_]:
                        p.wait("tensor", "V", mxg[sl_])
                lastmx = 0
                for k4 in range(4):
                    smk = p.step("tensor", lambda E, h=h, qt=qt, k4=k4:
                                 E.matmul(PSB[:, k4 * 512:(k4 + 1) * 512],
                                          qfm[:, h, qt * 128:(qt + 1) * 128],
                                          Kh[:, k4 * 512:(k4 + 1) * 512],
                                          start=True, stop=True))
                    p.wait("vector", "T", smk)
                    if k4 == 0 and lastcmb[0]:
                        p.wait("vector", "V", lastcmb[0])
                    lastmx = p.step("vector", lambda E, k4=k4: E.reduce_max(
                        pmx[:, k4:k4 + 1], PSB[:, k4 * 512:(k4 + 1) * 512],
                        axis=AX.X))
                p.wait("vector", "V", lastmx)
                mxg[ar] = p.step("vector", lambda E, nmb=nmb: E.reduce_max(
                    nmb, pmx[:, :], axis=AX.X, negate=True))
                lastcmb[0] = mxg[ar]
                p.wait("scalar", "V", mxg[ar])
                if trq[ar]:
                    p.wait("scalar", "T", trq[ar])
                ex = p.step("scalar", lambda E, qt=qt, ar=ar, nmb=nmb:
                            E.activation(Aexp2[:, ar, :], PSB[:, :], AF.Exp,
                                         bias=nmb,
                                         accum_out=sums[:, qt:qt + 1]))
                scg[ar] = ex
                p.wait("vector", "SC", ex)
                p.step("vector", lambda E, qt=qt:
                       E.reciprocal(recip[:, qt:qt + 1], sums[:, qt:qt + 1]))
                p.wait("vector", "V", p.cnt["V"])
                an_s[ar] = p.step("vector", lp(lambda E, qt=qt, ar=ar:
                                  E.tensor_scalar_mul(Aexp2[:, ar, :],
                                                      Aexp2[:, ar, :],
                                                      recip[:, qt:qt + 1])))

            def emit_tp(jq):
                ar = jq % 2
                qx = jq % 2
                qq = jq // 2
                at_b = AT if qq % 2 == 0 else AT2
                evs = []
                tr = 0
                for r in range(2):
                    p.wait("tensor", "V", an_s[ar])
                    if ptev[0]:
                        p.wait("tensor", "V", ptev[0])
                    for j in range(8):
                        kc = r * 8 + j
                        p.pend("tensor", lambda E, kc=kc, j=j, ar=ar:
                               E.transpose(PT[:, j * 128:(j + 1) * 128],
                                           Aexp2[:, ar,
                                                 kc * 128:(kc + 1) * 128],
                                           ident_bf[:, :]))
                    tr = p.close("tensor")
                    p.wait("vector", "T", tr)
                    if av_dn[qq % 2]:
                        p.wait("vector", "T", av_dn[qq % 2])
                    ev = p.step("vector", lambda E, r=r, qx=qx, at_b=at_b:
                                E.tensor_copy(
                                    at_b[:, r * 8:(r + 1) * 8,
                                         qx * 128:(qx + 1) * 128],
                                    PT[:, :].rearrange(
                                        "p (a c) -> p a c", a=8)))
                    ptev[0] = ev
                    evs.append(ev)
                trq[ar] = tr
                tp_toks[jq] = evs

            def emit_av(qq, h=h):
                at_b = AT if qq % 2 == 0 else AT2
                for ev in tp_toks.get(qq * 2, ()):
                    p.wait("tensor", "V", ev)
                for ev in tp_toks.get(qq * 2 + 1, ()):
                    p.wait("tensor", "V", ev)
                ps = PS0 if qq % 2 == 0 else PS1
                if avg[qq % 2]:
                    p.wait("tensor", "V", avg[qq % 2])
                for kc in range(16):
                    p.pend("tensor", lambda E, kc=kc, ps=ps, at_b=at_b:
                           E.matmul(ps[:, 0:256], Vtok[:, kc, :],
                                    at_b[:, kc, :],
                                    start=(kc == 0), stop=(kc == 15)))
                av = p.close("tensor")
                av_dn[qq % 2] = av
                p.wait("vector", "T", av)
                avg[qq % 2] = p.step("vector", lambda E, h=h, qq=qq, ps=ps:
                                     E.tensor_copy(
                                         ao[:, h, qq * 256:(qq + 1) * 256],
                                         ps[:, 0:256]))

            for qt in range(8):
                pump_step()
                emit_sc(qt)
                if qt >= 1:
                    emit_tp(qt - 1)
                if qt >= 2 and qt % 2 == 0:
                    emit_av((qt - 2) // 2)
            emit_tp(7)
            emit_av(3)
        reso = res_q
        p.wait("tensor", "V", p.cnt["V"])
        for th in range(2):
            for ic in range(16):
                p.pend("tensor", lambda E, ic=ic, th=th:
                       E.matmul(PSB[0:64, th * 512:(th + 1) * 512], basTo[:, ic, :],
                                ao[:, ic, th * 512:(th + 1) * 512],
                                start=(ic == 0), stop=(ic == DC - 1)))
            ro = p.close("tensor")
            p.wait("vector", "T", ro)
            rv = p.step("vector", lambda E, th=th:
                        E.tensor_copy(reso[:, th * 512:(th + 1) * 512],
                                      PSB[0:64, th * 512:(th + 1) * 512]))
            p.wait("tensor", "V", rv)
        fg = {0: 0, 1: 0}
        for dc in range(DC):
            ps = PS0 if dc % 2 == 0 else PS1
            for th in range(2):
                if fg[dc % 2]:
                    p.wait("tensor", "V", fg[dc % 2])
                fm = p.step("tensor", lambda E, dc=dc, th=th, ps=ps:
                            E.matmul(ps[:, :], wT[0:64, 3, dc, :],
                                     reso[:, th * 512:(th + 1) * 512],
                                     start=True, stop=True))
                p.wait("vector", "T", fm)
                fg[dc % 2] = p.step("vector", lambda E, dc=dc, th=th, ps=ps:
                                    E.tensor_add(x_res[:, dc, th * 512:(th + 1) * 512],
                                                 x_res[:, dc, th * 512:(th + 1) * 512],
                                                 ps[:, :]))
        p.barrier()
        ABF.reset()
        AFA.reset()

    def l1_ffn():
        layernorm()
        ABF.reset()
        AFA.reset()
        b1col = AFA.alloc(128, 64)
        lgrow = AFA.alloc3(8, 2, 512)
        lgT = AFA.alloc3(128, 8, 8)
        top8 = AFA.alloc(128, 8)
        nm8 = AFA.alloc(128, 1)
        prb = AFA.alloc(128, 8)
        crtmp = AFA.alloc(16, 128)
        tmpcv2 = AFA.alloc(64, 128)
        brcol = AFA.alloc(8, 1)
        a2r = ABF.alloc3(128, 2, 1024)
        wrch = ABF.alloc3(128, 16, 8)
        crt = ABF.alloc3(1, 2, 1024)
        csb = ABF.alloc(1, 1024)
        b2row = ABF.alloc(1, 2048)
        gtmp2 = ABF.alloc3(128, 2, 1024)
        base_r = ABF.alloc3(128, 4, 512)
        t1sb = bufB[0:16, 0:4096].rearrange("p (a c) -> p a c", a=8)
        t2accA = bufB[32:48, 0:2048].rearrange("p (a c) -> p a c", a=4)
        t2accB = bufB[64:80, 0:2048].rearrange("p (a c) -> p a c", a=4)
        b1r = bufB[0:16, 4096:12288].rearrange("p (a b c) -> p a b c", a=8, b=8)
        cbe = bufB[:, 12288:16384].rearrange("p (t a c) -> p t a c", t=2, a=8)
        t2all = bufA[:, 1024:1536]
        b2stk = bufA[:, 1536:1792].rearrange("p (r c) -> p r c", r=2)
        aetA = bufA[:, 2048:4096].rearrange("p (r c) -> p r c", r=2)
        aetB = bufA[:, 4096:6144].rearrange("p (r c) -> p r c", r=2)
        caeq = bufA[:, 6144:10240].rearrange("p (r a c) -> p r a c", r=2, a=8)
        a1T = bufA[:, 10240:12288].rearrange("p (a c) -> p a c", a=16)
        mixR = bufA[:, 12288:16384].rearrange("p (a c) -> p a c", a=8)
        load_colvec(b1_in, 64, b1col, tmpcv2)
        dbr = p.dma("sync", lambda E: E.dma_start(
            out=brcol, in_=br_in.rearrange("(a b) -> a b", b=1)))
        db2 = p.dma("gpsimd", lambda E: E.dma_start(
            out=b2row, in_=b2_in.rearrange("(a b) -> a b", a=1)))
        dwr = p.dma("gpsimd", lambda E: E.dma_start(
            out=wrch[:, :, :], in_=Wr.rearrange("(a p) c -> p a c", p=128)))
        a1nat = bufA[:, 2048:4096]
        da1 = p.dma("gpsimd", lambda E: E.dma_start(
            out=a1nat, in_=A1.rearrange("e r d -> (e r) d")))
        p.wait_tok("tensor", da1)
        for rnd in range(2):
            p.wait("tensor", "V", p.cnt["V"])
            for jj in range(8):
                p.pend("tensor", lambda E, rnd=rnd, jj=jj:
                       E.transpose(PT[:, jj * 128:(jj + 1) * 128],
                                   a1nat.rearrange("p (a c) -> p a c", a=16)
                                   [:, rnd * 8 + jj, :], ident_bf[:, :]))
            ta = p.close("tensor")
            p.wait("vector", "T", ta)
            p.step("vector", lambda E, rnd=rnd:
                   E.tensor_copy(a1T[:, rnd * 8:(rnd + 1) * 8, :]
                                 .rearrange("p a c -> p (a c)"), PT[:, :]))
        bnat = bufA[:, 2048:3072].rearrange("p (a c) -> p a c", a=64)
        stg16 = bufA[0:16, 3072:4096]
        for e in range(8):
            p.wait("gpsimd", "T", p.cnt["T"])
            p.wait("gpsimd", "V", p.cnt["V"])
            p.wait_dma_all("gpsimd", "sync")
            dbn = p.dma("gpsimd", lambda E, e=e: E.dma_start(
                out=bnat.rearrange("p a c -> p (a c)"),
                in_=B1[e, :, :].rearrange("(a p) c -> p a c", p=128)))
            p.wait_tok("tensor", dbn)
            for blk in range(8):
                p.wait("tensor", "V", p.cnt["V"])
                for jj in range(8):
                    p.pend("tensor", lambda E, blk=blk, jj=jj:
                           E.transpose(PT[0:16, jj * 128:(jj + 1) * 128],
                                       bnat[:, blk * 8 + jj, :], ident_bf[:, :]))
                tb1 = p.close("tensor")
                p.wait("vector", "T", tb1)
                p.wait_dma_all("vector", "sync")
                sv = p.step("vector", lambda E: E.tensor_copy(stg16, PT[0:16, :]))
                p.wait("sync", "V", sv)
                p.dma("sync", lambda E, e=e, blk=blk: E.dma_start(
                    out=b1tc[blk, :, e * 1024:(e + 1) * 1024], in_=stg16))
        a2stg = bufA[:, 2048:4096]
        aev1 = bufA[:, 4096:4608]
        aev2 = bufA[:, 4608:5120]
        for half in range(4):
            p.wait("gpsimd", "T", p.cnt["T"])
            p.wait("gpsimd", "V", p.cnt["V"])
            p.wait_dma_all("gpsimd", "sync")
            dan = p.dma("gpsimd", lambda E, half=half: E.dma_start(
                out=a2stg, in_=A2.rearrange("e r f -> (e r) f")
                [:, half * 2048:(half + 1) * 2048]))
            p.wait_tok("tensor", dan)
            for blk in range(2):
                p.wait("tensor", "V", p.cnt["V"])
                for jj in range(8):
                    p.pend("tensor", lambda E, blk=blk, jj=jj:
                           E.transpose(PT[:, jj * 128:(jj + 1) * 128],
                                       a2stg.rearrange("p (a c) -> p a c", a=16)
                                       [:, blk * 8 + jj, :], ident_bf[:, :]))
                ta2 = p.close("tensor")
                p.wait("vector", "T", ta2)
                p.wait_dma_all("vector", "sync")
                sv = p.step("vector", lambda E: E.tensor_copy(aev1, PT[:, 0:512]))
                sv2 = p.step("vector", lambda E: E.tensor_copy(aev2, PT[:, 512:1024]))
                p.wait("sync", "V", sv2)
                for q4 in range(8):
                    fcx = half * 16 + blk * 8 + q4
                    srcb = aev1 if q4 < 4 else aev2
                    p.dma("sync", lambda E, fcx=fcx, srcb=srcb, q4=q4: E.dma_start(
                        out=a2tc[fcx, :, :],
                        in_=srcb.rearrange("p (a c) -> p a c", a=4)[:, q4 % 4, :]))
        b2nat = bufA[:, 2048:2304].rearrange("p (a c) -> p a c", a=16)
        for e in range(8):
            p.wait("gpsimd", "T", p.cnt["T"])
            p.wait("gpsimd", "V", p.cnt["V"])
            p.wait_dma_all("gpsimd", "sync")
            dbn = p.dma("gpsimd", lambda E, e=e: E.dma_start(
                out=b2nat.rearrange("p a c -> p (a c)"),
                in_=B2[e, :, :].rearrange("(a p) c -> p a c", p=128)))
            p.wait_tok("tensor", dbn)
            for blk in range(2):
                p.wait("tensor", "V", p.cnt["V"])
                for jj in range(8):
                    p.pend("tensor", lambda E, blk=blk, jj=jj:
                           E.transpose(PT[0:16, jj * 128:(jj + 1) * 128],
                                       b2nat[:, blk * 8 + jj, :], ident_bf[:, :]))
                tb2 = p.close("tensor")
                p.wait("vector", "T", tb2)
                p.wait_dma_all("vector", "sync")
                sv = p.step("vector", lambda E: E.tensor_copy(stg16, PT[0:16, :]))
                p.wait("sync", "V", sv)
                p.dma("sync", lambda E, e=e, blk=blk: E.dma_start(
                    out=b2tc[blk * 8:(blk + 1) * 8, e, :, :].transpose([1, 0, 2]),
                    in_=stg16.rearrange("p (a c) -> p a c", a=8)))
        p.wait_dma_all("tensor", "sync")
        p.wait_tok("tensor", dwr)
        p.wait_tok("scalar", dbr)
        for th in range(2):
            for dc in range(DC):
                p.pend("tensor", lambda E, th=th, dc=dc:
                       E.matmul(PSB[0:8, th * 512:(th + 1) * 512], wrch[:, dc, :],
                                xln[:, dc, th * 512:(th + 1) * 512],
                                start=(dc == 0), stop=(dc == DC - 1)))
            rm = p.close("tensor")
            p.wait("scalar", "T", rm)
            p.step("scalar", lambda E, th=th: E.activation(
                lgrow[:, th, :], PSB[0:8, th * 512:(th + 1) * 512], AF.Identity,
                bias=brcol))
        p.wait("tensor", "SC")
        p.wait("tensor", "V", p.cnt["V"])
        for tt in range(8):
            p.pend("tensor", lambda E, tt=tt:
                   E.transpose(PS2[:, tt * 8:(tt + 1) * 8],
                               lgrow.rearrange("p a c -> p (a c)")
                               [0:8, tt * 128:(tt + 1) * 128], ident[0:8, 0:8]))
        tl = p.close("tensor")
        p.wait("vector", "T", tl)
        p.step("vector", lambda E: E.tensor_copy(
            lgT.rearrange("p a c -> p (a c)"), PS2[:, 0:64]))
        for tt in range(8):
            p.wait("vector", "V", p.cnt["V"])
            p.step("vector", lambda E, tt=tt: E.reduce_max(
                nm8, lgT[:, tt, :], axis=AX.X, negate=True))
            p.wait("scalar", "V", p.cnt["V"])
            exm = p.step("scalar", lambda E, tt=tt: E.activation(
                prb, lgT[:, tt, :], AF.Exp, bias=nm8, accum_out=nm8))
            p.wait("vector", "SC", exm)
            p.step("vector", lambda E: E.reciprocal(nm8, nm8))
            p.wait("vector", "V", p.cnt["V"])
            p.step("vector", lambda E: E.tensor_scalar_mul(prb, prb, nm8))
            p.wait("vector", "V", p.cnt["V"])
            p.step("vector", lambda E: E.max(top8, prb))
            p.wait("vector", "V", p.cnt["V"])
            p.step("vector", lambda E, tt=tt: E.tensor_scalar(
                lgT[:, tt, :], prb, top8[:, 1:2], None, ALU.is_ge, ALU.bypass))
            p.wait("vector", "V", p.cnt["V"])
            p.step("vector", lambda E, tt=tt: E.tensor_mul(lgT[:, tt, :],
                                                           lgT[:, tt, :], prb))
            p.wait("tensor", "V", p.cnt["V"])
            t1m = p.step("tensor", lambda E, tt=tt:
                         E.transpose(PS2[0:8, 0:128], lgT[:, tt, :], ident[:, :]))
            p.wait("vector", "T", t1m)
            p.wait_dma_all("vector", "sync")
            sv = p.step("vector", lambda E: E.tensor_copy(crtmp[0:8, :],
                                                          PS2[0:8, 0:128]))
            p.wait("sync", "V", sv)
            p.dma("sync", lambda E, tt=tt: E.dma_start(
                out=crows_d[0:8, tt * 128:(tt + 1) * 128], in_=crtmp[0:8, :]))
        p.wait_dma_all("gpsimd", "sync")
        first = [True]
        for e in range(8):
            rr = e % 2
            p.wait("gpsimd", "V", p.cnt["V"])
            dcr = p.dma("gpsimd", lambda E, e=e, rr=rr: E.dma_start(
                out=crt[:, rr, :], in_=crows_d[e:e + 1, :]))
            p.wait_tok("vector", dcr)
            p.wait("vector", "V", p.cnt["V"])
            if first[0]:
                p.step("vector", lambda E, rr=rr: E.tensor_copy(csb, crt[0:1, rr, :]))
                first[0] = False
            else:
                p.step("vector", lp(lambda E, rr=rr: E.tensor_add(
                    csb, csb, crt[0:1, rr, :])))
        wring_r = Ring(p, "sync")
        state = dict(basem=0, qidx=0)
        base_sl = [0, 0, 0, 0]
        l1g = {0: 0, 1: 0}
        geg = {0: 0, 1: 0}
        cag = {0: 0, 1: 0}
        t2g = {0: 0, 1: 0}
        trg = {0: 0, 1: 0}
        xg = {0: 0, 1: 0}
        pend_ca = []
        sh_prev = []
        PB2 = PSB[:, 1024:1536]
        PB3 = PSB[:, 1536:2048]
        QSET = [(PS0, PS1), (PB2, PB3)]

        def t2reg(e):
            if e < 4:
                return PSB[32:48, e * 256:e * 256 + 256]
            return PSB[64:80, (e - 4) * 256:(e - 4) * 256 + 256]

        def drain_quad(item):
            (j8, eq, tq, sl, cslot, gm, tqs_i, rrA_i, R_i) = item
            if "gp" in L1F_PARTS:
                p.wait("vector", "SC", gm)
                if trg[cslot]:
                    p.wait("vector", "G", trg[cslot])
                cm = p.step("vector", lp(
                    lambda E, eq=eq, sl=sl, cslot=cslot, tq=tq:
                    E.tensor_mul(caeq[:, cslot, eq * 4:(eq + 1) * 4, :]
                                 .rearrange("p a c -> p (a c)"),
                                 aetB[:, sl, :],
                                 cbe[:, tq, eq * 4:(eq + 1) * 4, :]
                                 .rearrange("p a c -> p (a c)"))))
                cag[sl] = cm
            if "t2" in L1F_PARTS:
                p.wait("tensor", "SC", gm)
                for ee in range(4):
                    e = eq * 4 + ee
                    st = (j8 == 0 and e in (0, 2, 4, 6))
                    sp = (j8 == 7 and e in (1, 3, 5, 7))
                    p.pend("tensor", lambda E, e=e, ee=ee, j8=j8, rrA_i=rrA_i,
                           sl=sl, st=st, sp=sp:
                           E.matmul(t2reg(e),
                                    a2r[:, rrA_i, :]
                                    .rearrange("p (a c) -> p a c", a=8)
                                    [:, j8, e * 16:(e + 1) * 16],
                                    aetB[:, sl, ee * 256:(ee + 1) * 256],
                                    start=st, stop=sp))
                t2g[sl] = p.close("tensor")
            if eq == 1 and "gp" in L1F_PARTS:
                p.wait("gpsimd", "V", cm)
                if trg[cslot]:
                    p.wait("gpsimd", "G", trg[cslot])
                s1 = p.step("gpsimd", lp(lambda E, cslot=cslot: E.tensor_add(
                    gtmp2[:, cslot, 0:1024],
                    caeq[:, cslot, 0:4, :].rearrange("p a c -> p (a c)"),
                    caeq[:, cslot, 4:8, :].rearrange("p a c -> p (a c)"))))
                p.wait("gpsimd", "G", s1)
                s2 = p.step("gpsimd", lp(lambda E, cslot=cslot: E.tensor_add(
                    gtmp2[:, cslot, 0:512], gtmp2[:, cslot, 0:512],
                    gtmp2[:, cslot, 512:1024])))
                p.wait("gpsimd", "G", s2)
                s3 = p.step("gpsimd", lp(lambda E, j8=j8, tqs_i=tqs_i, cslot=cslot:
                            E.tensor_add(mixR[:, j8, tqs_i],
                                         gtmp2[:, cslot, 0:256],
                                         gtmp2[:, cslot, 256:512])))
                trg[cslot] = s3

        for th in range(2):
            tsl5 = slice(th * 512, (th + 1) * 512)
            for e in range(8 if "t1" in L1F_PARTS else 0):
                if e == 0:
                    p.wait("tensor", "V", p.cnt["V"])
                for dc in range(DC):
                    p.pend("tensor", lambda E, e=e, dc=dc, tsl5=tsl5:
                           E.matmul(PS2[0:16, :], a1T[:, dc, e * 16:(e + 1) * 16],
                                    xln[:, dc, tsl5],
                                    start=(dc == 0), stop=(dc == DC - 1)))
                t1m = p.close("tensor")
                p.wait("vector", "T", t1m)
                tv = p.step("vector", lambda E, e=e:
                            E.tensor_copy(t1sb[:, e, :], PS2[0:16, :]))
                p.wait("tensor", "V", tv)
            for R in range(8):
                fc0 = R * 8
                rrA = R % 2
                p.wait("gpsimd", "T", p.cnt["T"])
                da2 = p.dma("gpsimd", lambda E, fc0=fc0, rrA=rrA: E.dma_start(
                    out=a2r[:, rrA, :].rearrange("p (a c) -> p a c", a=8),
                    in_=a2tc[fc0:fc0 + 8, :, :].transpose([1, 0, 2])))
                p.wait("gpsimd", "T", p.cnt["T"])
                db1 = p.dma("gpsimd", lambda E, R=R: E.dma_start(
                    out=b1r.rearrange("p a b c -> p (a b c)"), in_=b1tc[R, :, :]))
                for j8 in range(8 if "base" in L1F_PARTS else 0):
                    fc = fc0 + j8
                    par = wring_r.next()
                    ldw = p.dma("sync", lambda E, fc=fc, par=par: E.dma_start(
                        out=wring[:, par * 16:(par + 1) * 16, :],
                        in_=w1c[fc, :, :, :].transpose([1, 0, 2])))
                    p.wait_tok("tensor", ldw)
                    bsl = j8 % 4
                    if base_sl[bsl]:
                        p.wait("tensor", "V", base_sl[bsl])
                    if state["basem"]:
                        p.wait("tensor", "SC", state["basem"])
                    for dc in range(DC):
                        p.pend("tensor", lambda E, dc=dc, par=par, tsl5=tsl5:
                               E.matmul(PS2[:, :], wring[:, par * 16 + dc, :],
                                        xln[:, dc, tsl5],
                                        start=(dc == 0), stop=(dc == DC - 1)))
                    bm = p.close("tensor")
                    wring_r.consumed(par, bm)
                    p.wait("scalar", "T", bm)
                    state["basem"] = p.step("scalar", lambda E, fc=fc, bsl=bsl:
                                            E.activation(base_r[:, bsl, :],
                                                         PS2[:, :], AF.Identity,
                                                         bias=b1col[:, fc:fc + 1]))
                    base_sl[bsl] = 0
                p.wait("tensor", "SC")
                p.wait_tok("tensor", da2)
                p.wait_tok("tensor", db1)
                for pg in (0, 1):
                    if xg[pg]:
                        p.wait("tensor", "V", xg[pg])
                        xg[pg] = 0
                for tq in range(2):
                    tq_abs = th * 2 + tq
                    tqs = slice(tq * 256, (tq + 1) * 256)
                    if R == 0 and "quads" in L1F_PARTS:
                        for e in range(8):
                            rr = e % 2
                            p.wait("gpsimd", "V", p.cnt["V"])
                            p.wait("gpsimd", "T", p.cnt["T"])
                            dcr = p.dma("gpsimd",
                                        lambda E, e=e, rr=rr, tq_abs=tq_abs:
                                        E.dma_start(
                                            out=crt[0:1, rr, 0:256],
                                            in_=crows_d[e:e + 1,
                                                        tq_abs * 256:
                                                        (tq_abs + 1) * 256]))
                            p.wait_tok("tensor", dcr)
                            p.wait("tensor", "V", p.cnt["V"])
                            cbm = p.step("tensor", lambda E, rr=rr:
                                         E.matmul(PS2[:, 0:256],
                                                  ones_rowb[0:1, 0:128],
                                                  crt[0:1, rr, 0:256],
                                                  start=True, stop=True))
                            p.wait("vector", "T", cbm)
                            cb = p.step("vector", lp(lambda E, e=e, tq=tq:
                                        E.tensor_copy(cbe[:, tq, e, :],
                                                      PS2[:, 0:256])))
                        p.wait("tensor", "V", p.cnt["V"])
                    for j8 in range(8 if "quads" in L1F_PARTS else 0):
                        fc = fc0 + j8
                        bsl = j8 % 4
                        for eq in range(2):
                            sl = state["qidx"] % 2
                            cslot = j8 % 2
                            if l1g[sl]:
                                p.wait("tensor", "V", l1g[sl])
                            qa, qb = QSET[sl]
                            for ee in range(4):
                                e = eq * 4 + ee
                                psx = qa if ee < 2 else qb
                                p.pend("tensor",
                                       lambda E, e=e, ee=ee, j8=j8, psx=psx,
                                       tqs=tqs:
                                       E.matmul(psx[:, (ee % 2) * 256:
                                                    (ee % 2) * 256 + 256],
                                                b1r[:, e, j8, :],
                                                t1sb[0:16, e, tqs],
                                                start=True, stop=True))
                            lm = p.close("tensor")
                            p.wait("vector", "T", lm)
                            if geg[sl]:
                                p.wait("vector", "SC", geg[sl])
                            bs = base_r[:, bsl, tqs]
                            brep = AP(bs.tensor, bs.offset,
                                      [list(bs.ap[0])] + [[0, 2]] +
                                      [list(d) for d in bs.ap[1:]])
                            p.step("vector", lp(lambda E, sl=sl, brep=brep, qa=qa:
                                   E.scalar_tensor_tensor(
                                       aetA[:, sl, 0:512], qa[:, :], 2.0,
                                       brep, ALU.mult, ALU.add)))
                            sv = p.step("vector", lp(lambda E, sl=sl, brep=brep, qb=qb:
                                        E.scalar_tensor_tensor(
                                            aetA[:, sl, 512:1024], qb[:, :], 2.0,
                                            brep, ALU.mult, ALU.add)))
                            l1g[sl] = sv
                            base_sl[bsl] = sv
                            p.wait("scalar", "V", sv)
                            if cag[sl]:
                                p.wait("scalar", "V", cag[sl])
                            if t2g[sl]:
                                p.wait("scalar", "T", t2g[sl])
                            gm = p.step("scalar", lambda E, sl=sl: E.activation(
                                aetB[:, sl, :], aetA[:, sl, :],
                                AF.Gelu_apprx_tanh))
                            geg[sl] = gm
                            pend_ca.append((j8, eq, tq, sl, cslot, gm, tqs, rrA, R))
                            state["qidx"] += 1
                            while len(pend_ca) > 1:
                                drain_quad(pend_ca.pop(0))
                    while pend_ca:
                        drain_quad(pend_ca.pop(0))
                    t2m = p.mark("T")
                    p.wait("vector", "T", t2m)
                    if R == 0 and tq == 0 and sh_prev:
                        for tok in sh_prev:
                            p.wait_tok("vector", tok)
                        sh_prev.clear()
                    for e in range(8 if "t2" in L1F_PARTS else 0):
                        tac = t2accA if e < 4 else t2accB
                        ei = e % 4
                        p.wait("vector", "V", p.cnt["V"])
                        if R == 0:
                            p.step("vector", lp(lambda E, e=e, ei=ei, tac=tac,
                                   tqs=tqs:
                                   E.tensor_copy(tac[:, ei, tqs], t2reg(e))))
                        else:
                            p.step("vector", lp(lambda E, e=e, ei=ei, tac=tac,
                                   tqs=tqs:
                                   E.scalar_tensor_tensor(
                                       tac[:, ei, tqs], t2reg(e), 1.0,
                                       tac[:, ei, tqs], ALU.mult, ALU.add)))
                    t2ev = p.mark("V")
                    p.wait("tensor", "V", t2ev)
                p.wait("tensor", "G", p.cnt["G"])
                for dc in range(DC if "w2" in L1F_PARTS else 0):
                    psw = PS0 if dc % 2 == 0 else PS1
                    par = wring_r.next()
                    ldw2 = p.dma("sync", lambda E, dc=dc, fc0=fc0, par=par:
                                 E.dma_start(
                                     out=wring[:, par * 16:par * 16 + 8, :],
                                     in_=w2c[dc, fc0:fc0 + 8, :, :]
                                     .transpose([1, 0, 2])))
                    p.wait_tok("tensor", ldw2)
                    if xg[dc % 2]:
                        p.wait("tensor", "V", xg[dc % 2])
                    if l1g[dc % 2]:
                        p.wait("tensor", "V", l1g[dc % 2])
                        l1g[dc % 2] = 0
                    for j in range(8):
                        p.pend("tensor", lambda E, psw=psw, j=j, par=par:
                               E.matmul(psw[:, :], wring[:, par * 16 + j, :],
                                        mixR[:, j, :], start=(j == 0),
                                        stop=(j == 7)))
                    wm = p.close("tensor")
                    wring_r.consumed(par, wm)
                    p.wait("vector", "T", wm)
                    xg[dc % 2] = p.step("vector",
                                        lambda E, dc=dc, psw=psw, tsl5=tsl5:
                                        E.tensor_add(x_res[:, dc, tsl5],
                                                     x_res[:, dc, tsl5],
                                                     psw[:, :]))
            l2e_on = "l2" in L1F_PARTS or "l2e" in L1F_PARTS
            l2d_on = "l2" in L1F_PARTS or "l2dma" in L1F_PARTS
            l2m_on = "l2" in L1F_PARTS or "l2mm" in L1F_PARTS
            for e in range(8 if l2e_on else 0):
                rr = e % 2
                p.wait("gpsimd", "V", p.cnt["V"])
                p.wait("gpsimd", "T", p.cnt["T"])
                dcr = p.dma("gpsimd", lambda E, e=e, rr=rr, th=th: E.dma_start(
                    out=crt[0:1, rr, 0:512],
                    in_=crows_d[e:e + 1, th * 512:(th + 1) * 512]))
                p.wait_tok("tensor", dcr)
                p.wait("tensor", "V", p.cnt["V"])
                outr = PS2[32:48, :] if e < 4 else PS2[64:80, :]
                tac = t2accA if e < 4 else t2accB
                ei = e % 4
                cbm = p.step("tensor", lambda E, rr=rr, outr=outr:
                             E.matmul(outr, ones_rowb[0:1, 0:16],
                                      crt[0:1, rr, 0:512], start=True, stop=True))
                p.wait("vector", "T", cbm)
                p.step("vector", lp(lambda E, ei=ei, tac=tac, outr=outr:
                       E.scalar_tensor_tensor(
                           tac[:, ei, :], tac[:, ei, :], 2.0, outr,
                           ALU.mult, ALU.mult)))
            tcv = p.mark("V")
            p.wait("tensor", "V", tcv)
            shtoks = []
            if l2d_on:
                p.wait("gpsimd", "T", p.cnt["T"])
                p.wait("gpsimd", "V", tcv)
                for e in range(8):
                    tac = t2accA if e < 4 else t2accB
                    ei = e % 4
                    shtoks.append(p.dma("gpsimd", lambda E, e=e, ei=ei, tac=tac:
                                        E.dma_start(
                                            out=t2all[e * 16:(e + 1) * 16, :],
                                            in_=tac[:, ei, :])))
                sh_prev[:] = shtoks
            lg2 = xg
            for dc in range(DC if (l2d_on or l2m_on) else 0):
                psw = PS0 if dc % 2 == 0 else PS1
                rrb2 = dc % 2
                if l2d_on:
                    if dc >= 2:
                        p.wait("gpsimd", "T", p.cnt["T"])
                    db2r = p.dma("gpsimd", lambda E, dc=dc, rrb2=rrb2:
                                 E.dma_start(
                                     out=b2stk[:, rrb2, :],
                                     in_=b2tc[dc, :, :, :]
                                     .rearrange("e r d -> (e r) d")))
                    p.wait_tok("tensor", db2r)
                if not l2m_on:
                    continue
                if dc == 0:
                    for tok in shtoks:
                        p.wait_tok("tensor", tok)
                if lg2[dc % 2]:
                    p.wait("tensor", "V", lg2[dc % 2])
                if xg[dc % 2]:
                    p.wait("tensor", "V", xg[dc % 2])
                    xg[dc % 2] = 0
                p.pend("tensor", lambda E, dc=dc, psw=psw, th=th: E.matmul(
                    psw[:, :], b2row[0:1, dc * 128:(dc + 1) * 128],
                    csb[0:1, th * 512:(th + 1) * 512], start=True, stop=False))
                p.pend("tensor", lambda E, rrb2=rrb2, psw=psw: E.matmul(
                    psw[:, :], b2stk[:, rrb2, :], t2all[:, :],
                    start=False, stop=True))
                wm = p.close("tensor")
                p.wait("vector", "T", wm)
                lg2[dc % 2] = p.step("vector", lambda E, dc=dc, psw=psw, tsl5=tsl5:
                                     E.tensor_add(x_res[:, dc, tsl5],
                                                  x_res[:, dc, tsl5], psw[:, :]))
        p.barrier()
        ABF.reset()
        AFA.reset()

    def phase_out():
        p.barrier()
        AFA.reset()
        stg = AFA.alloc(128, 1024)
        stg2 = AFA.alloc(128, 1024)
        guards = {0: 0, 1: 0}
        sguard = {0: 0, 1: 0}
        for tt in range(8):
            for dh in range(2):
                par = dh
                st = stg if par == 0 else stg2
                if sguard[par]:
                    p.wait_tok("vector", sguard[par])
                for j in range(8):
                    dc = dh * 8 + j
                    pst = [PS0, PS1][j % 2]
                    if guards[j % 2]:
                        p.wait("tensor", "V", guards[j % 2])
                    td = p.step("tensor", lambda E, pst=pst, dc=dc, tt=tt:
                                E.transpose(pst[:, 0:128],
                                            x_res[:, dc, tt * 128:(tt + 1) * 128],
                                            ident[:, :]))
                    p.wait("vector", "T", td)
                    ev = p.step("vector", lambda E, pst=pst, st=st, j=j:
                                E.tensor_copy(st[:, j * 128:(j + 1) * 128], pst[:, 0:128]))
                    guards[j % 2] = ev
                p.wait("sync", "V")
                sguard[par] = p.dma("sync", lambda E, tt=tt, dh=dh, st=st:
                                    E.dma_start(out=y_out[tt * 128:(tt + 1) * 128,
                                                          dh * 1024:(dh + 1) * 1024],
                                                in_=st))
        p.wait_dma_all("sync", "sync")

    def tap_xln_chunk(chunk):
        p.barrier()
        AFA.reset()
        stg = AFA.alloc(128, 1024)
        p.step("vector", lambda E: E.tensor_copy(stg, xln[:, chunk, :]))
        p.wait("sync", "V")
        p.dma("sync", lambda E: E.dma_start(out=dbg_out[0:128, 0:1024], in_=stg))
        p.wait_dma_all("sync", "sync")

    phase0()
    if tap == "ln1":
        layernorm()
        tap_xln_chunk(0)
    elif tap == "l0a":
        l0_attn()
    elif tap == "l0f":
        l0_attn()
        l0_ffn()
    elif tap == "l1a":
        l0_attn()
        l0_ffn()
        l1_attn()
    else:
        l0_attn()
        l0_ffn()
        l1_attn()
        l1_ffn()
    phase_out()

    with nc.Block() as block:
        @block.sync
        def _(eng):
            for fn in p.q["sync"]:
                fn(eng)

        @block.tensor
        def _(eng):
            for fn in p.q["tensor"]:
                fn(eng)

        @block.vector
        def _(eng):
            for fn in p.q["vector"]:
                fn(eng)

        @block.scalar
        def _(eng):
            for fn in p.q["scalar"]:
                fn(eng)

        @block.gpsimd
        def _(eng):
            for fn in p.q["gpsimd"]:
                fn(eng)

    return nc, meta


def make_inputs_for_core(core, inputs):
    b = core // 2
    half = core % 2
    d = {}
    x = np.asarray(inputs["L0_attn_hidden_states"])
    d["x"] = np.ascontiguousarray(x[b, half * S:(half + 1) * S, :])
    for k, v in inputs.items():
        if k == "L0_attn_hidden_states":
            continue
        d[k] = np.ascontiguousarray(np.asarray(v))
    d["ident128"] = np.eye(128, dtype=np.float32)
    import ml_dtypes
    pm = np.zeros((128, SK), dtype=ml_dtypes.bfloat16)
    pm[:, half * S:(half + 1) * S] = 1.0
    d["pmask"] = pm
    sel = np.zeros((32, 32, 64), dtype=np.float32)
    for h in range(32):
        sel[h, h, :] = 1.0
    d["selbc"] = sel.reshape(32, 2048).astype(ml_dtypes.bfloat16)
    return d


def kernel(**inputs):
    nc, _ = build()
    in_maps = [make_inputs_for_core(c, inputs) for c in range(NCORES)]
    res = run_bass_kernel_spmd(nc, in_maps, core_ids=list(range(NCORES)))
    out = np.empty((B, SK, D), dtype=np.float32)
    for c in range(NCORES):
        b = c // 2
        half = c % 2
        out[b, half * S:(half + 1) * S, :] = res.results[c]["y"]
    return out
